# version 09
# baseline (speedup 1.0000x reference)
import sys as _sys
for _p in ("/opt/trn_rl_repo",):
    if _p not in _sys.path:
        try:
            import concourse.bass
            break
        except Exception:
            _sys.path.insert(0, _p)

import numpy as np
import ml_dtypes

import concourse.bass as bass
import concourse.mybir as mybir

F32 = mybir.dt.float32
BF16 = mybir.dt.bfloat16
F16 = mybir.dt.float16
I32 = mybir.dt.int32
AF = mybir.ActivationFunctionType
ALU = mybir.AluOpType
AX = mybir.AxisListType

BF16NP = ml_dtypes.bfloat16

B = 4
S = 2048
D = 2048
NHEAD = 16
HD = 128
DFF = 8192
NT = 16
NOWN = 8
TOK = NOWN * 128

OWN_TILES = [
    [0, 1, 2, 3, 12, 13, 14, 15],
    [4, 5, 6, 7, 8, 9, 10, 11],
]
EXT = [16, 15, 14, 13, 8, 7, 6, 5]
MASKW = 5

EPS_SUBLN = 1e-5
EPS_RMS = 1.1920929e-07
EPS_LN = 1e-5
NEG = -1e30

RSQRT_HD = 1.0 / float(np.sqrt(128.0))
MAGIC = 12582912.0
FFN_PARTS = 4
FFN_MM1 = 4


COMPUTE_ENGINES = ("tensor", "vector", "scalar", "gpsimd")
ALL_ENGINES = COMPUTE_ENGINES + ("sync",)
QSTREAM = {"sync": "qsync", "scalar": "qscal", "gpsimd": "qgp"}
DRAIN_ENGINES = ("vector", "scalar", "gpsimd")
SEM_CAP = 24000


class Tsr:
    __slots__ = ("h", "name", "space")

    def __init__(self, h, name, space):
        self.h = h
        self.name = name
        self.space = space

    def __getitem__(self, key):
        return APR(self, key)

    @property
    def full(self):
        return APR(self, None)


class APR:
    __slots__ = ("t", "ap", "p", "c")

    def __init__(self, t, key, ap_override=None):
        self.t = t
        shp = list(t.h.shape)
        if key is None:
            key = (slice(None),) * len(shp)
        if not isinstance(key, tuple):
            key = (key,)

        def rg(sl, n):
            if isinstance(sl, slice):
                return ((sl.start or 0), n if sl.stop is None else sl.stop)
            return (sl, sl + 1)
        self.p = rg(key[0], shp[0])
        self.c = rg(key[1], shp[1]) if len(key) > 1 and len(shp) > 1 else (0, shp[1] if len(shp) > 1 else 1)
        self.ap = ap_override if ap_override is not None else t.h[key]


def raw_apr(t, ap, prange, crange):
    a = APR.__new__(APR)
    a.t = t
    a.ap = ap
    a.p = prange
    a.c = crange
    return a


def _overlap(a, b):
    if a.t is not b.t:
        return False
    if a.t.space == "psum":
        return True
    return a.p[0] < b.p[1] and b.p[0] < a.p[1] \
        and a.c[0] < b.c[1] and b.c[0] < a.c[1]


class Op:
    __slots__ = ("stream", "issue_eng", "idx", "fn", "wait_ops", "sig", "sig_k",
                 "drain_before", "kind", "accum_group", "tag", "vc", "emitted",
                 "async_")

    def __init__(self, stream, issue_eng, idx, fn, kind, tag=""):
        self.stream = stream
        self.issue_eng = issue_eng
        self.idx = idx
        self.fn = fn
        self.wait_ops = []
        self.sig = False
        self.sig_k = None
        self.drain_before = False
        self.kind = kind
        self.accum_group = None
        self.tag = tag
        self.vc = {}
        self.emitted = False
        self.async_ = False


class Access:
    __slots__ = ("op", "apr", "is_write", "accum_group")

    def __init__(self, op, apr, is_write, accum_group=None):
        self.op = op
        self.apr = apr
        self.is_write = is_write
        self.accum_group = accum_group


class Builder:
    def __init__(self, nc):
        self.nc = nc
        self.stream_len = {}
        self.sig_count = {}
        self.pending = {e: [] for e in ALL_ENGINES}
        self.records = {}
        self.vc = {e: {} for e in ALL_ENGINES}
        self.drained = {e: -1 for e in DRAIN_ENGINES}
        self.op_count = 0
        self.n_drains = 0
        self.n_autowaits = 0
        self.n_dma = 0
        self.sems = None
        self._seen_waits = {e: {} for e in ALL_ENGINES}


    def op(self, eng, fn, reads=(), writes=(), accum_group=None, tag="",
           dma=False, after=()):
        if dma:
            q = QSTREAM[eng]
            qidx = self.stream_len.get(q, 0)
            self.stream_len[q] = qidx + 1
            stream = (q, qidx % self.NSEMQ)
            idx = qidx // self.NSEMQ
            self.stream_len[stream] = idx + 1
        else:
            stream = eng
            idx = self.stream_len.get(stream, 0)
            self.stream_len[stream] = idx + 1
        o = Op(stream, eng, idx, fn, "dma" if dma else "compute", tag)
        o.accum_group = accum_group
        self.pending[eng].append(o)
        self.op_count += 1
        if dma:
            self.n_dma += 1

        myvc = self.vc[eng]
        need = []
        need_drain = False

        def check(apr, is_w):
            nonlocal need_drain
            recs = self.records.get(id(apr.t))
            if not recs:
                return
            psum = apr.t.space == "psum"
            for acc in recs:
                if not (acc.is_write or is_w or psum):
                    continue
                if accum_group is not None and acc.accum_group == accum_group:
                    continue
                if not _overlap(apr, acc.apr):
                    continue
                p = acc.op
                if p is o:
                    continue
                if p.stream == stream and not p.async_:
                    if p.kind == "dma":
                        continue
                    if eng == "tensor":
                        continue
                    if p.idx > self.drained[eng]:
                        need_drain = True
                    continue
                if p.kind == "compute" and p.issue_eng == eng and not p.async_:
                    if eng == "tensor":
                        continue
                    if eng in DRAIN_ENGINES and p.idx > self.drained[eng]:
                        need_drain = True
                    continue
                if myvc.get(p.stream, -1) >= p.idx:
                    continue
                if p not in need:
                    need.append(p)

        for a in reads:
            check(a, False)
        for a in writes:
            check(a, True)
        for p in after:
            if p.stream == stream:
                continue
            if myvc.get(p.stream, -1) < p.idx and p not in need:
                need.append(p)

        if need_drain and eng in DRAIN_ENGINES:
            o.drain_before = True
            self.drained[eng] = (idx - 1) if not dma \
                else (self.stream_len.get(eng, 0) - 1)
            self.n_drains += 1

        for p in need:
            if p.emitted and not p.sig:
                raise RuntimeError(
                    f"dependency on already-emitted unsignaled op {p.tag!r} "
                    f"({p.stream}#{p.idx}) from {tag!r}")
            p.sig = True
            o.wait_ops.append(p)
            self.n_autowaits += 1
            cvc = dict(p.vc)
            cvc[p.stream] = p.idx
            for k_, v_ in cvc.items():
                if myvc.get(k_, -1) < v_:
                    myvc[k_] = v_
        o.vc = dict(myvc)

        for a in reads:
            self.records.setdefault(id(a.t), []).append(Access(o, a, False))
        for a in writes:
            self.records.setdefault(id(a.t), []).append(
                Access(o, a, True, accum_group))
        return o

    def barrier(self):
        evs = []
        last_by_stream = {}
        for eng in ALL_ENGINES:
            for o in self.pending[eng]:
                if o.kind in ("compute", "dma") and o.fn is not None:
                    last_by_stream[o.stream if not isinstance(o.stream, tuple)
                                   else o.stream] = o
        for o in last_by_stream.values():
            o.sig = True
            evs.append(o)
        for eng in ALL_ENGINES:
            myvc = self.vc[eng]
            nops = [p for p in evs
                    if p.stream != eng and myvc.get(p.stream, -1) < p.idx]
            if not nops and not self.pending[eng]:
                continue
            if nops:
                idx = self.stream_len.get(eng, 0)
                self.stream_len[eng] = idx + 1
                o = Op(eng, eng, idx, None, "wait", "barrier")
                o.wait_ops = nops
                self.pending[eng].append(o)
                for p in nops:
                    cvc = dict(p.vc)
                    cvc[p.stream] = p.idx
                    for k_, v_ in cvc.items():
                        if myvc.get(k_, -1) < v_:
                            myvc[k_] = v_
                o.vc = dict(myvc)
        self.records.clear()
        return evs


    NSEMQ = 12

    def make_sems(self, stack):
        self.sems = {}
        for s in ALL_ENGINES:
            for rot in range(3):
                self.sems[(s, rot)] = stack.enter_context(
                    self.nc.semaphore(f"as_{s}_{rot}"))
        for s in QSTREAM.values():
            for k in range(self.NSEMQ):
                self.sems[((s, k),)] = None
                self.sems[(s, k)] = stack.enter_context(
                    self.nc.semaphore(f"aq_{s}_{k}"))

    def _sig_value(self, op):
        if isinstance(op.stream, tuple):
            return op.stream, (op.idx + 1) * 16, 16
        cap_ops = SEM_CAP
        rot = op.sig_k // cap_ops
        val = op.sig_k - rot * cap_ops + 1
        if rot > 2:
            raise RuntimeError(f"sem rotation overflow on {op.stream}")
        return (op.stream, rot), val, 1

    def flush(self):
        nc = self.nc
        per_stream = {}
        for eng in ALL_ENGINES:
            for o in self.pending[eng]:
                if isinstance(o.stream, tuple):
                    o.sig_k = o.idx
                    continue
                per_stream.setdefault(o.stream, []).append(o)
        for sname, lst in per_stream.items():
            lst.sort(key=lambda o: o.idx)
            k = self.sig_count.get(sname, 0)
            for o in lst:
                if o.sig:
                    o.sig_k = k
                    k += 1
            self.sig_count[sname] = k

        with nc.Block() as block:
            eng_map = {"tensor": block.tensor, "vector": block.vector,
                       "scalar": block.scalar, "gpsimd": block.gpsimd,
                       "sync": block.sync}
            for ename, deco in eng_map.items():
                ops = self.pending[ename]
                if not ops:
                    continue

                def make_body(ops=ops, ename=ename):
                    def body(eng):
                        seen = self._seen_waits[ename]
                        for o in ops:
                            for p in o.wait_ops:
                                if p.sig_k is None:
                                    raise RuntimeError(
                                        f"wait on unsignaled {p.tag}")
                                key, val, _ = self._sig_value(p)
                                if seen.get(key, 0) >= val:
                                    continue
                                seen[key] = val
                                eng.wait_ge(self.sems[key], val)
                            if o.drain_before:
                                eng.drain()
                            if o.kind == "dma":
                                keyd, vald, _ = self._sig_value(o)
                                pred = vald - 16
                                if pred > 0 and seen.get(keyd, 0) < pred:
                                    seen[keyd] = pred
                                    eng.wait_ge(self.sems[keyd], pred)
                            if o.fn is None:
                                if o.sig:
                                    key, val, amt = self._sig_value(o)
                                    eng.nop().then_inc(self.sems[key], amt)
                                    o.emitted = True
                                continue
                            inst = o.fn(eng)
                            if o.sig:
                                key, val, amt = self._sig_value(o)
                                inst.then_inc(self.sems[key], amt)
                            o.emitted = True
                    return body
                deco(make_body())
        for eng in ALL_ENGINES:
            for o in self.pending[eng]:
                o.emitted = True
            self.pending[eng] = []


    def dma(self, eng, out, in_, tag=""):
        oap, owr = (out.ap, [out]) if isinstance(out, APR) else out
        iap, ird = (in_.ap, [in_]) if isinstance(in_, APR) else in_

        def fn(e):
            return e.dma_start(out=oap, in_=iap)
        o = self.op(eng, fn, reads=ird, writes=owr, dma=True, tag=tag)
        o.sig = True
        return o

    def mm(self, out, lhsT, rhs, start, stop, group, tag=""):
        def fn(e):
            return e.matmul(out.ap, lhsT.ap, rhs.ap, start=start, stop=stop)
        return self.op("tensor", fn, reads=[lhsT, rhs], writes=[out],
                       accum_group=group, tag=tag)

    def transpose(self, out, in_, ident, tag=""):
        def fn(e):
            return e.transpose(out.ap, in_.ap, ident.ap)
        return self.op("tensor", fn, reads=[in_, ident], writes=[out], tag=tag)

    def act(self, out, in_, func, bias=0.0, scale=1.0, accum_out=None, tag=""):
        reads, writes = [in_], [out]
        b_, s_ = bias, scale
        if isinstance(bias, APR):
            reads.append(bias)
            b_ = bias.ap
        if isinstance(scale, APR):
            reads.append(scale)
            s_ = scale.ap
        ao = None
        if accum_out is not None:
            writes.append(accum_out)
            ao = accum_out.ap

        def fn(e):
            return e.activation(out=out.ap, in_=in_.ap, func=func, bias=b_,
                                scale=s_, accum_out=ao)
        return self.op("scalar", fn, reads=reads, writes=writes, tag=tag)

    def copy(self, out, in_, eng="vector", tag=""):
        if eng == "scalar":
            def fn(e):
                return e.copy(out=out.ap, in_=in_.ap)
        else:
            def fn(e):
                return e.tensor_copy(out=out.ap, in_=in_.ap)
        return self.op(eng, fn, reads=[in_], writes=[out], tag=tag)

    def memset(self, apr, val, eng="vector", tag=""):
        def fn(e):
            return e.memset(apr.ap, val)
        return self.op(eng, fn, reads=[], writes=[apr], tag=tag)

    def ts(self, out, in0, s1, s2=None, op0=ALU.mult, op1=ALU.bypass,
           eng="vector", tag="", accum_out=None):
        reads, writes = [in0], [out]
        a1, a2 = s1, s2
        if isinstance(s1, APR):
            reads.append(s1)
            a1 = s1.ap
        if isinstance(s2, APR):
            reads.append(s2)
            a2 = s2.ap
        ao = None
        if accum_out is not None:
            writes.append(accum_out)
            ao = accum_out.ap
        single = s2 is None and ao is None

        def fn(e):
            if single:
                return e.tensor_single_scalar(out.ap, in0.ap, a1, op0)
            return e.tensor_scalar(out.ap, in0.ap, a1, a2, op0, op1,
                                   accum_out=ao)
        return self.op(eng, fn, reads=reads, writes=writes, tag=tag)

    def tt(self, out, in0, in1, alu, eng="vector", tag=""):
        def fn(e):
            return e.tensor_tensor(out.ap, in0.ap, in1.ap, alu)
        return self.op(eng, fn, reads=[in0, in1], writes=[out], tag=tag)

    def red(self, out, in_, alu=ALU.max, absval=None, negate=None, tag=""):
        def fn(e):
            return e.tensor_reduce(out.ap, in_.ap, AX.X, alu,
                                   apply_absolute_value=absval, negate=negate)
        return self.op("vector", fn, reads=[in_], writes=[out], tag=tag)

    def scan(self, out, d0, d1, initial, op0, op1, tag=""):
        reads = [d0, d1]
        ini = initial
        if isinstance(initial, APR):
            reads.append(initial)
            ini = initial.ap

        def fn(e):
            return e.tensor_tensor_scan(out.ap, d0.ap, d1.ap, ini, op0, op1)
        return self.op("vector", fn, reads=reads, writes=[out], tag=tag)

    def recip(self, out, in_, tag=""):
        def fn(e):
            return e.reciprocal(out.ap, in_.ap)
        return self.op("vector", fn, reads=[in_], writes=[out], tag=tag)

    def stt(self, out, in0, scalar, in1, op0, op1, eng="vector", tag=""):
        reads = [in0, in1]
        sc = scalar
        if isinstance(scalar, APR):
            reads.append(scalar)
            sc = scalar.ap

        def fn(e):
            return e.scalar_tensor_tensor(out.ap, in0.ap, sc, in1.ap, op0, op1)
        return self.op(eng, fn, reads=reads, writes=[out], tag=tag)

    def collective(self, kind, alu, replica_groups, ins, outs, tag="cc"):
        iaps = [x.ap for x in ins]
        oaps = [x.ap for x in outs]

        def fn(e):
            return e.collective_compute(kind, alu, replica_groups=replica_groups,
                                        ins=iaps, outs=oaps)
        o = self.op("gpsimd", fn, reads=list(ins), writes=list(outs), tag=tag)
        o.async_ = True
        return o



def _hadamard(n):
    h = np.array([[1.0]], dtype=np.float32)
    while h.shape[0] < n:
        h = np.block([[h, h], [h, -h]])
    return h


def stage_consts():
    ident = np.eye(128, dtype=np.float32).astype(BF16NP)
    h128 = _hadamard(128).astype(BF16NP)
    g64 = np.kron(np.eye(2, dtype=np.float32), _hadamard(64)).astype(BF16NP)
    return ident, h128, g64


def stage_mask(role):
    m = np.zeros((NOWN, 128, MASKW * 128), dtype=np.float32)
    qrow = np.arange(128)[:, None]
    col = np.arange(MASKW * 128)[None, :]
    for j in range(NOWN):
        st = 7 - j
        gt = OWN_TILES[role][st]
        q = gt * 128 + qrow
        k = (EXT[j] - MASKW) * 128 + col
        m[j] = np.where(k <= q, 0.0, NEG)
    return m


def stage_weights(inp):
    f32 = np.float32
    w = {}

    def tile_cols(a, width):
        return np.ascontiguousarray(
            a.reshape(a.shape[0], a.shape[1] // width, width).transpose(1, 0, 2))
    w["wqT"] = tile_cols(inp["L0_attn_wq"].T, 256)
    w["wkT"] = tile_cols(inp["L0_attn_wk"].T, 256)
    w["wvT"] = tile_cols(inp["L0_attn_wv"].T, 256)
    w["wpT"] = tile_cols(inp["L0_attn_wp"].T, 512)
    w["cfcT"] = np.ascontiguousarray(
        inp["L0_ffn_c_fc"].reshape(D, 32, 256).transpose(1, 0, 2))
    w["cprojT"] = np.ascontiguousarray(inp["L0_ffn_c_proj"].T)
    w["wqkvT"] = np.ascontiguousarray(
        inp["L1_attn_w_qkv"].T.reshape(D, 12, 512).transpose(1, 0, 2))
    w["wswT"] = np.ascontiguousarray(
        inp["L1_attn_w_swiglu"].T.reshape(D, 32, 512).transpose(1, 0, 2))
    w["woutT"] = np.ascontiguousarray(inp["L1_attn_w_out"].T)
    w["W1T"] = np.ascontiguousarray(
        inp["L1_ffn_W1"].T.reshape(D, 16, 512).transpose(1, 0, 2))
    w["W3T"] = np.ascontiguousarray(
        inp["L1_ffn_W3"].T.reshape(D, 16, 512).transpose(1, 0, 2))
    j = np.arange(128)
    c = np.arange(64)
    Hcj = ((j[None, :] % 64) * 128 + 2 * c[:, None] + (j[None, :] // 64))
    w2t = inp["L1_ffn_W2"].T
    w2r = w2t[Hcj.reshape(-1)]
    w["W2relay"] = np.ascontiguousarray(
        w2r.reshape(64, 128, 4, 512).transpose(2, 0, 1, 3))
    return {k: np.ascontiguousarray(v.astype(f32)) for k, v in w.items()}


def stage_inputs(inputs):
    ident, h128, g64 = stage_consts()
    wts = stage_weights(inputs)
    x = np.asarray(inputs["L0_attn_x"], dtype=np.float32)
    in_maps = []
    for core in range(8):
        b, role = core // 2, core % 2
        x_all = np.ascontiguousarray(x[b])
        own = OWN_TILES[role]
        x_own = np.ascontiguousarray(
            x[b].reshape(NT, 128, D)[own].reshape(TOK, D))
        cr = {0: [0.0, 0.0, 0.0, 1.0], 1: [1.0, 0.0, 1.0, 0.0]}[role]
        CR = np.tile(np.array(cr, dtype=np.float32), (128, 1))
        sel0 = np.full((128, 1), 1.0 if role == 1 else 0.0, np.float32)
        sel1 = np.full((128, 1), 1.0 if role == 0 else 0.0, np.float32)
        m = {"x_own": x_own, "x_all": x_all,
             "mask": stage_mask(role), "CR": CR, "sel0": sel0, "sel1": sel1,
             "ident": ident, "H128": h128, "G64": g64}
        m.update(wts)
        in_maps.append(m)
    return in_maps


def unstage_output(results):
    out = np.zeros((B, S, D), dtype=np.float32)
    for core in range(8):
        b, role = core // 2, core % 2
        y = results[core]["y_own"].reshape(NOWN, 128, D)
        for st, gt in enumerate(OWN_TILES[role]):
            out[b, gt * 128:(gt + 1) * 128] = y[st]
    return out



class Env:

    def __init__(self, nc):
        self.nc = nc
        self.dram = {}
        self.sb = {}

    def din(self, name, shape, dtype):
        t = Tsr(self.nc.dram_tensor(name, list(shape), dtype,
                                    kind="ExternalInput").ap(), name, "dram")
        self.dram[name] = t
        return t

    def dout(self, name, shape, dtype):
        t = Tsr(self.nc.dram_tensor(name, list(shape), dtype,
                                    kind="ExternalOutput").ap(), name, "dram")
        self.dram[name] = t
        return t

    def dint(self, name, shape, dtype):
        t = Tsr(self.nc.dram_tensor(name, list(shape), dtype,
                                    kind="Internal").ap(), name, "dram")
        self.dram[name] = t
        return t


_ALLOC_N = [0]


def alloc_sb(env, stack, name, shape, dtype):
    _ALLOC_N[0] += 1
    name = f"{name}_{_ALLOC_N[0]}"
    h = stack.enter_context(env.nc.sbuf_tensor(name, list(shape), dtype))
    return Tsr(h, name, "sbuf")


def alloc_ps(env, stack, name, shape, dtype):
    _ALLOC_N[0] += 1
    name = f"{name}_{_ALLOC_N[0]}"
    h = stack.enter_context(env.nc.psum_tensor(name, list(shape), dtype))
    return Tsr(h, name, "psum")


SUM, SS, MEAN, T1, M2, VE, RSTD, MX, MXC, RECM, INVS, SC1, SC2, SC3 = range(14)


def emit_ln_stats(b, stt, x_apr, sq_scratch, eps, n_inv):
    c = lambda i: stt[:, i:i + 1]
    b.red(c(SUM), x_apr, alu=ALU.add, tag="ln_sum")
    b.act(sq_scratch, x_apr, AF.Square, accum_out=c(SS), tag="ln_sq")
    b.ts(c(MEAN), c(SUM), n_inv, tag="ln_mean")
    b.ts(c(T1), c(SS), n_inv, tag="ln_ss")
    b.tt(c(M2), c(MEAN), c(MEAN), ALU.mult, tag="ln_m2")
    b.stt(c(VE), c(T1), eps, c(M2), ALU.add, ALU.subtract, tag="ln_var")
    b.act(c(RSTD), c(VE), AF.Ln, tag="ln_ln")
    b.act(c(RSTD), c(RSTD), AF.Exp, scale=-0.5, tag="ln_rstd")


def emit_rsqrt(b, out_apr, in_apr):
    b.act(out_apr, in_apr, AF.Ln, tag="rsq_ln")
    b.act(out_apr, out_apr, AF.Exp, scale=-0.5, tag="rsq_ex")


def emit_quant_x(b, stt, xn_apr, nq_i32, xq_out):
    c = lambda i: stt[:, i:i + 1]
    b.red(c(MX), xn_apr, alu=ALU.max, absval=True, tag="qx_mx")
    b.ts(c(MXC), c(MX), 1e-5, op0=ALU.max, tag="qx_clip")
    b.recip(c(RECM), c(MXC), tag="qx_rec")
    b.ts(c(INVS), c(MXC), 1.0 / 127.0, tag="qx_invs")
    b.ts(nq_i32, xn_apr, c(RECM), 127.0, op0=ALU.mult, op1=ALU.mult,
         tag="qx_scale1")
    b.ts(nq_i32, nq_i32, MAGIC, MAGIC, op0=ALU.add, op1=ALU.subtract,
         tag="qx_round")
    b.ts(xq_out, nq_i32, c(INVS), tag="qx_scale")



def phase_l0_stats(b, env, stack, weights, wcols=4096):
    nc = env.nc
    AMS, RAMS = env.sb["AMS"], env.sb["RAMS"]
    ones_c, ones_r = env.sb["ones_c"], env.sb["ones_r"]
    wbuf = alloc_sb(env, stack, "st_wbuf", [128, 2 * wcols], F32)
    acc = alloc_sb(env, stack, "st_acc", [128, 192], F32)
    red1 = alloc_sb(env, stack, "st_red", [128, 8], F32)
    am1 = alloc_sb(env, stack, "st_am", [1, 16], F32)
    psum = alloc_ps(env, stack, "st_ps", [128, 512], F32)

    for key, dname, col, numel, kind, eps in weights:
        dr = env.dram[dname]
        shp = dr.h.shape
        views = []
        if len(shp) == 3:
            nsl, rows, cols = shp
            per = rows * cols // 128
            for i in range(nsl):
                full = dr.h[i].rearrange("(p a) c -> p (a c)", p=128)
                for k in range(0, per, wcols):
                    w_ = min(wcols, per - k)
                    views.append(raw_apr(dr, full[:, k:k + w_], (0, 128),
                                         (i * per + k, i * per + k + w_)))
        else:
            rows, cols = shp
            for i in range(rows // 128):
                for k in range(0, cols, wcols):
                    w_ = min(wcols, cols - k)
                    views.append(raw_apr(
                        dr, dr.h[i * 128:(i + 1) * 128, k:k + w_], (0, 128),
                        (k, k + w_)))
        nacc = 0
        half = 0
        for i, v in enumerate(views):
            ncols = int(np.prod(v.ap.shape[1:]))
            assert ncols <= wcols
            dst = raw_apr(wbuf, wbuf.h[:, half * wcols: half * wcols + ncols],
                          (0, 128), (half * wcols, half * wcols + ncols))
            b.dma("sync", dst, v, tag=f"st_ld_{key}_{i}")
            b.red(acc[:, nacc:nacc + 1], dst, alu=ALU.add, absval=True,
                  tag=f"st_red_{key}_{i}")
            nacc += 1
            half ^= 1
        b.red(red1[:, 0:1], acc[:, 0:nacc], alu=ALU.add, tag=f"st_r2_{key}")
        b.mm(psum[0:1, 0:1], red1[:, 0:1], ones_c[:, 0:1],
             start=True, stop=True, group=None, tag=f"st_mm_{key}")
        b.copy(am1[0:1, col:col + 1], psum[0:1, 0:1], tag=f"st_cp_{key}")
        b.ts(am1[0:1, col:col + 1], am1[0:1, col:col + 1], 1.0 / numel,
             tag=f"st_am_{key}")
        if kind == "wq":
            b.ts(am1[0:1, col + 8:col + 9], am1[0:1, col:col + 1], eps,
                 op0=ALU.add, tag=f"st_ame_{key}")
        else:
            b.ts(am1[0:1, col:col + 1], am1[0:1, col:col + 1], eps,
                 op0=ALU.max, tag=f"st_mx_{key}")
            b.copy(am1[0:1, col + 8:col + 9], am1[0:1, col:col + 1],
                   tag=f"st_cp2_{key}")
        b.recip(am1[0:1, col + 8:col + 9], am1[0:1, col + 8:col + 9],
                tag=f"st_rec_{key}")
        b.mm(psum[0:128, col:col + 1], ones_r[0:1, :],
             am1[0:1, col:col + 1], start=True, stop=True, group=None,
             tag=f"st_bc_{key}")
        b.copy(AMS[:, col:col + 1], psum[0:128, col:col + 1],
               eng="scalar", tag=f"st_bc2_{key}")
        b.mm(psum[0:128, col + 8:col + 9], ones_r[0:1, :],
             am1[0:1, col + 8:col + 9], start=True, stop=True, group=None,
             tag=f"st_bc3_{key}")
        b.copy(RAMS[:, col:col + 1], psum[0:128, col + 8:col + 9],
               eng="scalar", tag=f"st_bc4_{key}")


W_WQ, W_WK, W_WV, W_WP, W_CFC, W_CPROJ, W_QSC = range(7)

L0_STATS = [
    ("wq", "wqT", W_WQ, D * D, "wq", EPS_SUBLN),
    ("wk", "wkT", W_WK, D * D, "wq", EPS_SUBLN),
    ("wv", "wvT", W_WV, D * D, "wq", EPS_SUBLN),
    ("wp", "wpT", W_WP, D * D, "wq", EPS_SUBLN),
]
L0FFN_STATS = [
    ("cfc", "cfcT", W_CFC, D * DFF, "ternary", EPS_SUBLN),
    ("cproj", "cprojT", W_CPROJ, D * DFF, "ternary", EPS_SUBLN),
]


def _xqT_rhs(xqT, tg, c, ntile=16):
    v = xqT.h.rearrange("p (t c k) -> p t c k", t=ntile, c=16, k=128)
    ap = v[:, tg * 4:(tg + 1) * 4, c, :]
    return raw_apr(xqT, ap, (0, 128),
                   (tg * 4 * 2048, (tg * 4 + 4) * 2048))


def phase_l0_pre(b, env, stack):
    nc = env.nc
    ident = env.sb["ident"]
    xq_allT = env.sb["xq_allT"]
    x_all, x_own = env.dram["x_all"], env.dram["x_own"]
    xq_ownT = env.dram["xq_ownT"]

    xin = [alloc_sb(env, stack, f"pre_x{i}", [128, D], F32) for i in range(2)]
    xn = [alloc_sb(env, stack, f"pre_xn{i}", [128, D], F32) for i in range(2)]
    nq = [alloc_sb(env, stack, f"pre_nq{i}", [128, D], F32) for i in range(2)]
    xq = [alloc_sb(env, stack, f"pre_xq{i}", [128, D], BF16) for i in range(2)]
    stg = [alloc_sb(env, stack, f"pre_st{i}", [128, D], BF16) for i in range(2)]
    sq = alloc_sb(env, stack, "pre_sq", [128, D], F32)
    stt = [alloc_sb(env, stack, f"pre_tt{i}", [128, 16], F32) for i in range(2)]
    pst = [alloc_ps(env, stack, f"pre_ps{i}", [128, 1024], BF16)
           for i in range(2)]

    work = [("all", g) for g in range(NT)] + [("own", t) for t in range(NOWN)]
    for i, (kind, g) in enumerate(work):
        r = i % 2
        src = x_all if kind == "all" else x_own
        b.dma("sync", xin[r].full,
              raw_apr(src, src.h[g * 128:(g + 1) * 128, :], (0, 128), (0, D)),
              tag=f"pre_ld{i}")
        emit_ln_stats(b, stt[r], xin[r].full, sq.full, EPS_SUBLN, 1.0 / D)
        c = lambda k: stt[r][:, k:k + 1]
        b.ts(xn[r].full, xin[r].full, c(MEAN), c(RSTD),
             op0=ALU.subtract, op1=ALU.mult, tag=f"pre_xn{i}")
        emit_quant_x(b, stt[r], xn[r].full, nq[r].full, xq[r].full)
        for half in range(2):
            for cc in range(8):
                ch = half * 8 + cc
                b.transpose(pst[half][:, cc * 128:(cc + 1) * 128],
                            xq[r][:, ch * 128:(ch + 1) * 128], ident.full,
                            tag=f"pre_tp{i}_{ch}")
            if kind == "all":
                dst = xq_allT[:, g * 2048 + half * 1024:
                              g * 2048 + (half + 1) * 1024]
                b.copy(dst, pst[half].full, eng="scalar", tag=f"pre_ev{i}_{half}")
            else:
                b.copy(stg[r][:, half * 1024:(half + 1) * 1024],
                       pst[half].full, eng="scalar", tag=f"pre_ev{i}_{half}")
        if kind == "own":
            b.dma("gpsimd",
                  raw_apr(xq_ownT, xq_ownT.h[g], (0, 128), (g * 2048, (g + 1) * 2048)),
                  stg[r].full, tag=f"pre_st{i}")


def phase_l0_attn_hg(b, env, stack, hg):
    nc = env.nc
    ident = env.sb["ident"]
    AMS, RAMS = env.sb["AMS"], env.sb["RAMS"]
    xq_allT = env.sb["xq_allT"]
    xq_ownT = env.dram["xq_ownT"]

    K_hg = alloc_sb(env, stack, "a_K", [128, 4 * 2048], BF16)
    V_hg = alloc_sb(env, stack, "a_V", [128, 16 * 512], BF16)
    Q_hg = alloc_sb(env, stack, "a_Q", [128, 4 * 1024], BF16)
    proj_stack = ExitStack()
    stack.enter_context(proj_stack)
    wsl_f = alloc_sb(env, proj_stack, "a_wf", [128, 16 * 256], F32)
    wsl_i = alloc_sb(env, proj_stack, "a_wi", [128, 8 * 256], F32)
    wsl_b = [alloc_sb(env, proj_stack, f"a_wb{i}", [128, 16 * 256], BF16)
             for i in range(2)]
    qin = [alloc_sb(env, proj_stack, f"a_qin{i}", [128, 4 * 2048], BF16)
           for i in range(2)]
    psA = [alloc_ps(env, proj_stack, f"a_psA{i}", [128, 512], F32)
           for i in range(2)]
    psV = [alloc_ps(env, proj_stack, f"a_psV{i}", [128, 512], F32)
           for i in range(2)]

    def load_quant_slab(wname, s, r, wcol, tag):
        dr = env.dram[wname]
        for h in range(2):
            v = raw_apr(dr,
                        dr.h[s].rearrange("(g p) w -> p g w", p=128)[
                            :, h * 8:(h + 1) * 8, :],
                        (0, 128), (h * 2048, (h + 1) * 2048))
            b.dma("sync",
                  wsl_f[:, h * 2048:(h + 1) * 2048], v, tag=f"{tag}_ld{h}")
            b.ts(wsl_i.full, wsl_f[:, h * 2048:(h + 1) * 2048],
                 RAMS[:, wcol:wcol + 1], op0=ALU.mult, tag=f"{tag}_scl{h}")
            b.ts(wsl_i.full, wsl_i.full, MAGIC, MAGIC, op0=ALU.add,
                 op1=ALU.subtract, tag=f"{tag}_rnd{h}")
            b.ts(wsl_b[r][:, h * 2048:(h + 1) * 2048], wsl_i.full, -1.0, 1.0,
                 op0=ALU.max, op1=ALU.min, tag=f"{tag}_clip{h}")

    gid = [0]

    def proj_form2(wname, wcol, scol, dst, dst_head_stride, rhs_tiles, rhs_src,
                   ntile, tagp):
        for si, s in enumerate((hg * 2, hg * 2 + 1)):
            r = si % 2
            load_quant_slab(wname, s, r, wcol, f"{tagp}{s}")
            for ob in range(2):
                hl = si * 2 + ob
                for tg in range(ntile // 4):
                    gid[0] += 1
                    g = (tagp, gid[0])
                    for c in range(16):
                        b.mm(psA[tg % 2][:, 0:512],
                             wsl_b[r][:, c * 256 + ob * 128:c * 256 + (ob + 1) * 128],
                             rhs_src(tg, c),
                             start=(c == 0), stop=(c == 15), group=g,
                             tag=f"{tagp}_mm{hl}_{tg}_{c}")
                    b.act(dst[:, hl * dst_head_stride + tg * 512:
                              hl * dst_head_stride + (tg + 1) * 512],
                          psA[tg % 2][:, 0:512], AF.Copy,
                          scale=AMS[:, scol:scol + 1], tag=f"{tagp}_ev{hl}_{tg}")

    proj_form2("wkT", W_WK, W_WK, K_hg, 2048,
               None, lambda tg, c: _xqT_rhs(xq_allT, tg, c), NT, "K")

    for si, s in enumerate((hg * 2, hg * 2 + 1)):
        r = si % 2
        load_quant_slab("wvT", s, r, W_WV, f"V{s}")
        for kt in range(NT):
            g = ("V", s, kt)
            for c in range(16):
                b.mm(psV[kt % 2][:, 0:256],
                     xq_allT[:, kt * 2048 + c * 128:kt * 2048 + (c + 1) * 128],
                     wsl_b[r][:, c * 256:(c + 1) * 256],
                     start=(c == 0), stop=(c == 15), group=g,
                     tag=f"V_mm{s}_{kt}_{c}")
            b.act(V_hg[:, kt * 512 + si * 256:kt * 512 + (si + 1) * 256],
                  psV[kt % 2][:, 0:256], AF.Copy,
                  scale=AMS[:, W_WV:W_WV + 1], tag=f"V_ev{s}_{kt}")

    for tg in range(2):
        v = xq_ownT.h.rearrange("t p c -> p t c", t=8)[:, tg * 4:(tg + 1) * 4, :]
        b.dma("gpsimd", qin[tg].full,
              raw_apr(xq_ownT, v, (0, 128), (tg * 4 * 2048, (tg + 1) * 4 * 2048)),
              tag=f"Q_ld{tg}")

    def q_rhs(tg, c):
        v = qin[tg].h.rearrange("p (t c k) -> p t c k", t=4, c=16, k=128)
        return raw_apr(qin[tg], v[:, :, c, :], (0, 128), (0, 4 * 2048))

    proj_form2("wqT", W_WQ, W_QSC, Q_hg, 1024,
               None, q_rhs, NOWN, "Q")

    dump = env.sb.get("_dumpKVQ")
    if dump:
        dbuf = alloc_sb(env, proj_stack, "dkv", [128, 2048], F32)
        for nm, src_t, n in (("d_K", K_hg, 4 * 2048), ("d_V", V_hg, 16 * 512),
                             ("d_Q", Q_hg, 4 * 1024)):
            for o in range(0, n, 2048):
                w_ = min(2048, n - o)
                b.copy(raw_apr(dbuf, dbuf.h[:, 0:w_], (0, 128), (0, w_)),
                       raw_apr(src_t, src_t.h[:, o:o + w_], (0, 128), (o, o + w_)))
                b.dma("gpsimd",
                      raw_apr(dump[nm], dump[nm].h[:, o:o + w_], (0, 128),
                              (o, o + w_)),
                      raw_apr(dbuf, dbuf.h[:, 0:w_], (0, 128), (0, w_)),
                      tag=f"dkv_{nm}_{o}")
        env.sb["_dumpKVQ"] = None

    b.barrier()
    proj_stack.close()
    maskrot = [alloc_sb(env, stack, f"b_msk{i}", [128, 640], F32)
               for i in range(2)]
    psS = alloc_ps(env, stack, "b_psS", [128, 2048], F32)
    psPT = [alloc_ps(env, stack, f"b_psPT{i}", [128, 1024], BF16)
            for i in range(2)]
    psAV = [alloc_ps(env, stack, f"b_psAV{i}", [128, 512], F32)
            for i in range(2)]
    P = [alloc_sb(env, stack, f"b_P{i}", [128, 2048], BF16) for i in range(2)]
    PT = [alloc_sb(env, stack, f"b_PT{i}", [128, 2048], BF16) for i in range(2)]
    st2 = [alloc_sb(env, stack, f"b_st{i}", [128, 4], F32) for i in range(2)]

    mask_d = env.dram["mask"]
    aoc = alloc_sb(env, stack, "b_aoc", [128, 512], BF16)
    attn_out_d = env.dram["attn_out_d"]
    for j in range(NOWN):
        stt_ = 7 - j
        E = EXT[j]
        b.dma("sync", maskrot[j % 2].full,
              raw_apr(mask_d, mask_d.h[j], (0, 128), (0, 640)),
              tag=f"msk_ld{j}")
        for hl in range(4):
            r = (j * 4 + hl) % 2
            gS = ("S", j, hl)
            for c0 in range(0, E * 128, 512):
                c1 = min(E * 128, c0 + 512)
                b.mm(psS[:, c0:c1],
                     Q_hg[:, hl * 1024 + stt_ * 128:hl * 1024 + (stt_ + 1) * 128],
                     K_hg[:, hl * 2048 + c0:hl * 2048 + c1],
                     start=True, stop=True, group=gS, tag=f"S_mm{j}_{hl}_{c0}")
            b.tt(psS[:, (E - MASKW) * 128:E * 128],
                 psS[:, (E - MASKW) * 128:E * 128],
                 maskrot[j % 2].full, ALU.add, tag=f"S_mask{j}_{hl}")
            b.act(P[r][:, 0:E * 128], psS[:, 0:E * 128], AF.Exp,
                  bias=env.sb["neg8"][:, 0:1], accum_out=st2[r][:, 1:2],
                  tag=f"S_exp{j}_{hl}")
            b.recip(st2[r][:, 2:3], st2[r][:, 1:2], tag=f"S_rec{j}_{hl}")
            for kb in range(E):
                half = (kb // 8) % 2
                b.transpose(psPT[half][:, (kb % 8) * 128:(kb % 8 + 1) * 128],
                            P[r][:, kb * 128:(kb + 1) * 128], ident.full,
                            tag=f"PT{j}_{hl}_{kb}")
                if kb % 8 == 7 or kb == E - 1:
                    lo = (kb // 8) * 8
                    b.copy(PT[r][:, lo * 128:(kb + 1) * 128],
                           psPT[half][:, 0:(kb + 1 - lo) * 128],
                           eng="vector", tag=f"PT_ev{j}_{hl}_{half}")
            gAV = ("AV", j, hl)
            for kb in range(E):
                b.mm(psAV[r][:, 0:128],
                     PT[r][:, kb * 128:(kb + 1) * 128],
                     V_hg[:, kb * 512 + hl * 128:kb * 512 + (hl + 1) * 128],
                     start=(kb == 0), stop=(kb == E - 1), group=gAV,
                     tag=f"AV{j}_{hl}_{kb}")
            b.act(aoc[:, hl * 128:(hl + 1) * 128],
                  psAV[r][:, 0:128], AF.Copy, scale=st2[r][:, 2:3],
                  tag=f"AV_ev{j}_{hl}")
        b.dma("gpsimd",
              raw_apr(attn_out_d,
                      attn_out_d.h[stt_][:, hg * 512:(hg + 1) * 512],
                      (0, 128), (stt_ * 2048 + hg * 512,
                                 stt_ * 2048 + (hg + 1) * 512)),
              aoc.full, tag=f"AV_st{j}")


def phase_l0_wp(b, env, stack):
    nc = env.nc
    ident = env.sb["ident"]
    AMS, RAMS = env.sb["AMS"], env.sb["RAMS"]
    attn_out_d = env.dram["attn_out_d"]
    xstream = env.sb["xstream"]
    x_own = env.dram["x_own"]

    wsl_f = alloc_sb(env, stack, "c_wf", [128, 2 * 2048], F32)
    wsl_i = alloc_sb(env, stack, "c_wi", [128, 2048], F32)
    wpq = alloc_sb(env, stack, "c_wpq", [128, 16 * 2048], BF16)
    dr = env.dram["wpT"]
    for s in range(4):
        for h in range(4):
            r = h % 2
            v = raw_apr(dr, dr.h[s].rearrange("(c p) w -> p c w", p=128)[
                :, h * 4:(h + 1) * 4, :],
                (0, 128), (s * 8192 + h * 2048, s * 8192 + (h + 1) * 2048))
            b.dma("sync", wsl_f[:, r * 2048:(r + 1) * 2048], v,
                  tag=f"wp_ld{s}_{h}")
            b.ts(wsl_i.full, wsl_f[:, r * 2048:(r + 1) * 2048],
                 RAMS[:, W_WP:W_WP + 1], op0=ALU.mult, tag=f"wp_scl{s}_{h}")
            b.ts(wsl_i.full, wsl_i.full, MAGIC, MAGIC, op0=ALU.add,
                 op1=ALU.subtract, tag=f"wp_rnd{s}_{h}")
            wv = wpq.h.rearrange("p (c o) -> p c o", c=16)[
                :, h * 4:(h + 1) * 4, s * 512:(s + 1) * 512]
            b.ts(raw_apr(wpq, wv, (0, 128), (0, 16 * 2048)),
                 wsl_i.full, -1.0, 1.0, op0=ALU.max, op1=ALU.min,
                 tag=f"wp_clip{s}_{h}")

    dump = env.sb.get("_dumpWP")
    if dump:
        dbuf = alloc_sb(env, stack, "dwp", [128, 2048], F32)
        b.copy(dbuf[:, 0:16], AMS.full)
        b.copy(dbuf[:, 16:32], RAMS.full)
        b.dma("gpsimd", dump["d_ams"].full, dbuf[:, 0:32], tag="dams")
        for o in range(0, 16 * 2048, 2048):
            b.copy(dbuf.full,
                   raw_apr(wpq, wpq.h[:, o:o + 2048], (0, 128), (o, o + 2048)))
            b.dma("gpsimd",
                  raw_apr(dump["d_wpq"], dump["d_wpq"].h[:, o:o + 2048],
                          (0, 128), (o, o + 2048)), dbuf.full, tag=f"dwpq{o}")
        env.sb["_dumpWP"] = None

    xin = [alloc_sb(env, stack, f"c_x{i}", [128, D], F32) for i in range(2)]
    xn = alloc_sb(env, stack, "c_xn", [128, D], F32)
    nq = alloc_sb(env, stack, "c_nq", [128, D], F32)
    aoq = alloc_sb(env, stack, "c_aoq", [128, D], BF16)
    aoqT = [alloc_sb(env, stack, f"c_aoqT{i}", [128, D], BF16) for i in range(2)]
    ow = [alloc_sb(env, stack, f"c_ow{i}", [128, D], F32) for i in range(2)]
    sq = alloc_sb(env, stack, "c_sq", [128, D], F32)
    stt = [alloc_sb(env, stack, f"c_tt{i}", [128, 16], F32) for i in range(2)]
    pst = alloc_ps(env, stack, "c_pst", [128, 1024], BF16)
    psW = [alloc_ps(env, stack, f"c_psW{i}", [128, 512], F32) for i in range(4)]

    aob = [alloc_sb(env, stack, f"c_ao{i}", [128, D], BF16) for i in range(2)]
    for t in range(NOWN):
        r = t % 2
        b.dma("gpsimd", aob[r].full,
              raw_apr(attn_out_d, attn_out_d.h[t], (0, 128), (0, D)),
              tag=f"wp_aold{t}")
        ao = aob[r].full
        emit_ln_stats(b, stt[r], ao, sq.full, EPS_SUBLN, 1.0 / D)
        c = lambda k: stt[r][:, k:k + 1]
        b.ts(xn.full, ao, c(MEAN), c(RSTD), op0=ALU.subtract, op1=ALU.mult,
             tag=f"wp_xn{t}")
        emit_quant_x(b, stt[r], xn.full, nq.full, aoq.full)
        for half in range(2):
            for cc in range(8):
                ch = half * 8 + cc
                b.transpose(pst[:, cc * 128:(cc + 1) * 128],
                            aoq[:, ch * 128:(ch + 1) * 128], ident.full,
                            tag=f"wp_tp{t}_{ch}")
            b.copy(aoqT[r][:, half * 1024:(half + 1) * 1024], pst.full,
                   eng="vector", tag=f"wp_tpe{t}_{half}")
        b.dma("sync", xin[r].full,
              raw_apr(x_own, x_own.h[t * 128:(t + 1) * 128, :], (0, 128),
                      (0, D)), tag=f"wp_x0{t}")
        for ds in range(4):
            g = ("WP", t, ds)
            for ch in range(16):
                b.mm(psW[ds][:, 0:512],
                     aoqT[r][:, ch * 128:(ch + 1) * 128],
                     wpq[:, ch * 2048 + ds * 512:ch * 2048 + (ds + 1) * 512],
                     start=(ch == 0), stop=(ch == 15), group=g,
                     tag=f"wp_mm{t}_{ds}_{ch}")
            b.act(ow[r][:, ds * 512:(ds + 1) * 512], psW[ds][:, 0:512],
                  AF.Copy, scale=AMS[:, W_WP:W_WP + 1], tag=f"wp_ev{t}_{ds}")
        b.tt(ow[r].full, ow[r].full, xin[r].full, ALU.add, tag=f"wp_res{t}")
        if env.sb.get("_dumpOW") is not None:
            dmp = env.sb["_dumpOW"]
            b.dma("gpsimd",
                  raw_apr(dmp, dmp.h[:, t * 2048:(t + 1) * 2048], (0, 128),
                          (t * 2048, (t + 1) * 2048)), ow[r].full,
                  tag=f"dow{t}")
        emit_ln_stats(b, stt[r], ow[r].full, sq.full, EPS_LN, 1.0 / D)
        b.ts(xstream[:, t * 2048:(t + 1) * 2048], ow[r].full, c(MEAN), c(RSTD),
             op0=ALU.subtract, op1=ALU.mult, tag=f"wp_str{t}")



from contextlib import ExitStack

DRAM_INPUTS = [
    ("x_own", [TOK, D], F32),
    ("x_all", [S, D], F32),
    ("mask", [NOWN, 128, MASKW * 128], F32),
    ("CR", [128, 4], F32),
    ("sel0", [128, 1], F32),
    ("sel1", [128, 1], F32),
    ("ident", [128, 128], BF16),
    ("H128", [128, 128], BF16),
    ("G64", [128, 128], BF16),
    ("wqT", [8, D, 256], F32),
    ("wkT", [8, D, 256], F32),
    ("wvT", [8, D, 256], F32),
    ("wpT", [4, D, 512], F32),
    ("cfcT", [32, D, 256], F32),
    ("cprojT", [DFF, D], F32),
    ("wqkvT", [12, D, 512], F32),
    ("wswT", [32, D, 512], F32),
    ("woutT", [DFF, D], F32),
    ("W1T", [16, D, 512], F32),
    ("W3T", [16, D, 512], F32),
    ("W2relay", [4, 64, 128, 512], F32),
]


def build_program(upto="l1ffn", stream_out=False):
    nc = bass.Bass("TRN2", target_bir_lowering=False, debug=False)
    env = Env(nc)
    for name, shape, dt in DRAM_INPUTS:
        env.din(name, shape, dt)
    env.dint("xq_ownT", [NOWN, 128, 2048], BF16)
    env.dint("attn_out_d", [NOWN, 128, 2048], BF16)
    env.dint("x2nT_d", [NOWN, 128, 2048], BF16)
    env.dint("h1_d", [64, 128, 1024], BF16)
    env.dint("kvc_d", [16, 128, 1024], BF16)
    env.dint("wc_d", [16, 128, 1024], BF16)
    env.dint("y_d", [16, 128, 1024], BF16)
    env.dint("cfcq_d", [32, 128, 4096], BF16)
    env.dint("h_d", [NOWN, 128, DFF], BF16)
    env.dint("cpjq_d", [64, 128, 2048], BF16)
    env.dint("cc_src", [128, 64], F32)
    env.dint("cc_dst", [256, 64], F32)
    if stream_out:
        env.dout("xs_out", [128, NOWN * 2048], F32)
    else:
        env.dout("y_own", [TOK, D], F32)

    b = Builder(nc)
    with ExitStack() as top:
        env.sb["ident"] = alloc_sb(env, top, "k_ident", [128, 128], BF16)
        env.sb["ones_c"] = alloc_sb(env, top, "k_onec", [128, 1], F32)
        env.sb["ones_r"] = alloc_sb(env, top, "k_oner", [1, 128], F32)
        env.sb["AMS"] = alloc_sb(env, top, "k_AMS", [128, 16], F32)
        env.sb["RAMS"] = alloc_sb(env, top, "k_RAMS", [128, 16], F32)
        env.sb["neg8"] = alloc_sb(env, top, "k_neg8", [128, 1], F32)
        b.make_sems(top)

        b.dma("sync", env.sb["ident"].full, env.dram["ident"].full,
              tag="ld_ident")
        b.memset(env.sb["ones_c"].full, 1.0)
        b.memset(env.sb["ones_r"].full, 1.0)
        b.memset(env.sb["neg8"].full, -8.0)

        with ExitStack() as attn_scope:
            env.sb["xq_allT"] = alloc_sb(env, attn_scope, "k_xqT",
                                         [128, NT * 2048], BF16)
            with ExitStack() as ph:
                phase_l0_stats(b, env, ph, L0_STATS)
                b.ts(env.sb["AMS"][:, W_QSC:W_QSC + 1],
                     env.sb["AMS"][:, W_WQ:W_WQ + 1], RSQRT_HD, tag="qsc")
                b.barrier()
                b.flush()
            with ExitStack() as ph:
                phase_l0_pre(b, env, ph)
                b.barrier()
                b.flush()
            for hg in range(4):
                with ExitStack() as ph:
                    phase_l0_attn_hg(b, env, ph, hg)
                    b.barrier()
                    b.flush()
        env.sb["xstream"] = alloc_sb(env, top, "k_xs", [128, NOWN * 2048], F16)
        with ExitStack() as ph:
            phase_l0_wp(b, env, ph)
            if upto != "l0attn":
                phase_l0_stats(b, env, ph, L0FFN_STATS, wcols=1024)
            b.barrier()
            b.flush()

        if upto == "l0attn":
            with ExitStack() as ph:
                _emit_stream_dump(b, env, ph, stream_out)
                b.barrier()
                b.flush()
            return nc

        with ExitStack() as ph:
            phase_l0_ffn(b, env, ph)
            b.barrier()
            b.flush()
        if upto == "l0ffn":
            with ExitStack() as ph:
                _emit_stream_dump(b, env, ph, stream_out)
                b.barrier()
                b.flush()
            return nc

        with ExitStack() as ph:
            phase_l1_attn(b, env, ph)
            b.barrier()
            b.flush()
        if upto == "l1attn":
            with ExitStack() as ph:
                _emit_stream_dump(b, env, ph, stream_out)
                b.barrier()
                b.flush()
            return nc

        with ExitStack() as ph:
            phase_l1_ffn(b, env, ph)
            b.barrier()
            b.flush()
    return nc


def _emit_stream_dump(b, env, stack, stream_out):
    xs = env.sb["xstream"]
    buf = alloc_sb(env, stack, "dump", [128, 2048], F32)
    out = env.dram["xs_out"]
    for t in range(NOWN):
        b.copy(buf.full, xs[:, t * 2048:(t + 1) * 2048], tag=f"dump_c{t}")
        b.dma("gpsimd", raw_apr(out, out.h[:, t * 2048:(t + 1) * 2048],
                                (0, 128), (t * 2048, (t + 1) * 2048)),
              buf.full, tag=f"dump_d{t}")


def build_debug_l0(dumps):
    nc = bass.Bass("TRN2", target_bir_lowering=False, debug=False)
    env = Env(nc)
    for name, shape, dt in DRAM_INPUTS:
        env.din(name, shape, dt)
    env.dint("xq_ownT", [NOWN, 128, 2048], BF16)
    env.dint("attn_out_d", [NOWN, 128, 2048], BF16)
    outs = {}
    if "xq" in dumps:
        outs["d_xq"] = env.dout("d_xq", [128, NT * 2048], F32)
        outs["d_xqo"] = env.dout("d_xqo", [128, NOWN * 2048], F32)
    if "K" in dumps:
        outs["d_K"] = env.dout("d_K", [128, 4 * 2048], F32)
        outs["d_V"] = env.dout("d_V", [128, 16 * 512], F32)
        outs["d_Q"] = env.dout("d_Q", [128, 4 * 1024], F32)
    if "ao" in dumps:
        outs["d_ao"] = env.dout("d_ao", [128, NOWN * 2048], F32)
    if "wp" in dumps:
        outs["d_ams"] = env.dout("d_ams", [128, 32], F32)
        outs["d_wpq"] = env.dout("d_wpq", [128, 16 * 2048], F32)
        outs["d_xs"] = env.dout("d_xs", [128, NOWN * 2048], F32)
        outs["d_ow"] = env.dout("d_ow", [128, NOWN * 2048], F32)

    b = Builder(nc)
    with ExitStack() as top:
        env.sb["ident"] = alloc_sb(env, top, "k_ident", [128, 128], BF16)
        env.sb["ones_c"] = alloc_sb(env, top, "k_onec", [128, 1], F32)
        env.sb["ones_r"] = alloc_sb(env, top, "k_oner", [1, 128], F32)
        env.sb["AMS"] = alloc_sb(env, top, "k_AMS", [128, 16], F32)
        env.sb["RAMS"] = alloc_sb(env, top, "k_RAMS", [128, 16], F32)
        env.sb["neg8"] = alloc_sb(env, top, "k_neg8", [128, 1], F32)
        b.make_sems(top)
        b.dma("sync", env.sb["ident"].full, env.dram["ident"].full,
              tag="ld_ident")
        b.memset(env.sb["ones_c"].full, 1.0)
        b.memset(env.sb["ones_r"].full, 1.0)
        b.memset(env.sb["neg8"].full, -8.0)
        attn_scope = ExitStack()
        top.enter_context(attn_scope)
        env.sb["xq_allT"] = alloc_sb(env, attn_scope, "k_xqT",
                                     [128, NT * 2048], BF16)

        with ExitStack() as ph:
            phase_l0_stats(b, env, ph, L0_STATS)
            b.ts(env.sb["AMS"][:, W_QSC:W_QSC + 1],
                 env.sb["AMS"][:, W_WQ:W_WQ + 1], RSQRT_HD, tag="qsc")
            b.barrier()
            b.flush()
        with ExitStack() as ph:
            phase_l0_pre(b, env, ph)
            b.barrier()
            b.flush()
        if "xq" in dumps:
            with ExitStack() as ph:
                buf = alloc_sb(env, ph, "dbuf", [128, 2048], F32)
                for t in range(NT):
                    b.copy(buf.full, env.sb["xq_allT"][:, t * 2048:(t + 1) * 2048])
                    b.dma("gpsimd",
                          raw_apr(outs["d_xq"], outs["d_xq"].h[:, t * 2048:(t + 1) * 2048],
                                  (0, 128), (t * 2048, (t + 1) * 2048)),
                          buf.full, tag=f"dxq{t}")
                bo = alloc_sb(env, ph, "dbuf2", [128, 2048], BF16)
                for t in range(NOWN):
                    b.dma("gpsimd", bo.full,
                          raw_apr(env.dram["xq_ownT"], env.dram["xq_ownT"].h[t],
                                  (0, 128), (0, 2048)), tag=f"dxqo_l{t}")
                    b.copy(buf.full, bo.full)
                    b.dma("gpsimd",
                          raw_apr(outs["d_xqo"], outs["d_xqo"].h[:, t * 2048:(t + 1) * 2048],
                                  (0, 128), (t * 2048, (t + 1) * 2048)),
                          buf.full, tag=f"dxqo{t}")
                b.barrier()
                b.flush()
        if "K" in dumps or "ao" in dumps:
            hgs = range(4) if "ao" in dumps else [0]
            for hg in hgs:
                with ExitStack() as ph:
                    env.sb["_dumpKVQ"] = outs if ("K" in dumps and hg == 0) else None
                    phase_l0_attn_hg(b, env, ph, hg)
                    b.barrier()
                    b.flush()
        if "wp" in dumps:
            b.barrier()
            attn_scope.close()
            env.sb["xstream"] = alloc_sb(env, top, "k_xs", [128, NOWN * 2048], F16)
            with ExitStack() as ph:
                env.sb["_dumpWP"] = outs
                env.sb["_dumpOW"] = outs["d_ow"]
                phase_l0_wp(b, env, ph)
                env.sb["_dumpOW"] = None
                b.barrier()
                b.flush()
            with ExitStack() as ph:
                buf = alloc_sb(env, ph, "dbuf9", [128, 2048], F32)
                for t in range(NOWN):
                    b.copy(buf.full, env.sb["xstream"][:, t * 2048:(t + 1) * 2048])
                    b.dma("gpsimd",
                          raw_apr(outs["d_xs"], outs["d_xs"].h[:, t * 2048:(t + 1) * 2048],
                                  (0, 128), (t * 2048, (t + 1) * 2048)),
                          buf.full, tag=f"dxs{t}")
                b.barrier()
                b.flush()
        if "ao" in dumps:
            with ExitStack() as ph:
                buf = alloc_sb(env, ph, "dbuf3", [128, 2048], F32)
                bo = alloc_sb(env, ph, "dbuf4", [128, 2048], BF16)
                for t in range(NOWN):
                    b.dma("gpsimd", bo.full,
                          raw_apr(env.dram["attn_out_d"], env.dram["attn_out_d"].h[t],
                                  (0, 128), (0, 2048)), tag=f"dao_l{t}")
                    b.copy(buf.full, bo.full)
                    b.dma("gpsimd",
                          raw_apr(outs["d_ao"], outs["d_ao"].h[:, t * 2048:(t + 1) * 2048],
                                  (0, 128), (t * 2048, (t + 1) * 2048)),
                          buf.full, tag=f"dao{t}")
                b.barrier()
                b.flush()
    return nc


def emit_rms_stats(b, stt, x_apr, sq_scratch, eps, n_inv):
    c = lambda i: stt[:, i:i + 1]
    b.act(sq_scratch, x_apr, AF.Square, accum_out=c(SS), tag="rms_sq")
    b.ts(c(T1), c(SS), n_inv, s2=eps, op0=ALU.mult, op1=ALU.add, tag="rms_ms")
    b.act(c(RSTD), c(T1), AF.Ln, tag="rms_ln")
    b.act(c(RSTD), c(RSTD), AF.Exp, scale=-0.5, tag="rms_ex")



def phase_l0_ffn(b, env, stack):
    ident = env.sb["ident"]
    AMS, RAMS = env.sb["AMS"], env.sb["RAMS"]
    xstream = env.sb["xstream"]
    cfc = env.dram["cfcT"]
    cpj = env.dram["cprojT"]
    h_d = env.dram["h_d"]
    cpjq = env.dram["cpjq_d"]

    sq = alloc_sb(env, stack, "f_sq", [128, 2048], F32)
    stt = [alloc_sb(env, stack, f"f_tt{i}", [128, 16], F32) for i in range(2)]
    rmx = alloc_sb(env, stack, "f_rmx", [128, 8], F32)
    esc = alloc_sb(env, stack, "f_esc", [128, 8], F32)
    with ExitStack() as p1:
        xn = alloc_sb(env, p1, "f_xn", [128, D], F32)
        nq = alloc_sb(env, p1, "f_nq", [128, D], F32)
        xq = alloc_sb(env, p1, "f_xq", [128, D], BF16)
        x1qT = alloc_sb(env, p1, "f_x1qT", [128, NOWN * 2048], BF16)
        wsl_f = alloc_sb(env, p1, "f_wf", [128, 2 * 2048], F32)
        wsl_i = alloc_sb(env, p1, "f_wi", [128, 2048], F32)
        wsl_b = [alloc_sb(env, p1, f"f_wb{i}", [128, 16 * 256], BF16)
                 for i in range(3)]
        hst = [alloc_sb(env, p1, f"f_hs{i}", [128, 8 * 256], BF16)
               for i in range(2)]
        pst = alloc_ps(env, p1, "f_pst", [128, 1024], BF16)
        psH = [alloc_ps(env, p1, f"f_psH{i}", [128, 512], F32)
               for i in range(2)]
        for t in range(NOWN):
            r = t % 2
            xs_t = xstream[:, t * 2048:(t + 1) * 2048]
            emit_ln_stats(b, stt[r], xs_t, sq.full, EPS_SUBLN, 1.0 / D)
            c = lambda k: stt[r][:, k:k + 1]
            b.ts(xn.full, xs_t, c(MEAN), c(RSTD), op0=ALU.subtract,
                 op1=ALU.mult, tag=f"ff_xn{t}")
            emit_quant_x(b, stt[r], xn.full, nq.full, xq.full)
            for hf in range(2):
                for cc in range(8):
                    ch = hf * 8 + cc
                    b.transpose(pst[:, cc * 128:(cc + 1) * 128],
                                xq[:, ch * 128:(ch + 1) * 128], ident.full,
                                tag=f"ff_tp{t}_{ch}")
                b.copy(x1qT[:, t * 2048 + hf * 1024:t * 2048 + (hf + 1) * 1024],
                       pst.full, eng="scalar", tag=f"ff_tpe{t}_{hf}")
        b.memset(rmx[:, 0:8], -1e30, tag="ff_rmx")
        for s in range(32):
            r = s % 3
            for hf in range(2):
                v = raw_apr(cfc, cfc.h[s].rearrange("(g p) w -> p g w", p=128)[
                    :, hf * 8:(hf + 1) * 8, :],
                    (0, 128), (s * 4096 + hf * 2048, s * 4096 + (hf + 1) * 2048))
                b.dma("sync", wsl_f[:, hf * 2048:(hf + 1) * 2048], v,
                      tag=f"ff_wld{s}_{hf}")
                b.ts(wsl_i.full, wsl_f[:, hf * 2048:(hf + 1) * 2048],
                     RAMS[:, W_CFC:W_CFC + 1], op0=ALU.mult, tag=f"ff_ws{s}_{hf}")
                b.ts(wsl_i.full, wsl_i.full, MAGIC, MAGIC, op0=ALU.add,
                     op1=ALU.subtract, tag=f"ff_wr{s}_{hf}")
                b.ts(wsl_b[r][:, hf * 2048:(hf + 1) * 2048], wsl_i.full,
                     -1.0, 1.0, op0=ALU.max, op1=ALU.min, tag=f"ff_wc{s}_{hf}")
            for t in range(NOWN):
                g = ("H", s, t)
                for ch in range(16):
                    b.mm(psH[t % 2][:, 0:256],
                         x1qT[:, t * 2048 + ch * 128:t * 2048 + (ch + 1) * 128],
                         wsl_b[r][:, ch * 256:(ch + 1) * 256],
                         start=(ch == 0), stop=(ch == 15), group=g,
                         tag=f"ff_mm{s}_{t}_{ch}")
                b.act(hst[s % 2][:, t * 256:(t + 1) * 256], psH[t % 2][:, 0:256],
                      AF.Copy, scale=AMS[:, W_CFC:W_CFC + 1], tag=f"ff_ev{s}_{t}")
                b.red(stt[t % 2][:, SC3:SC3 + 1], psH[t % 2][:, 0:256],
                      alu=ALU.max, tag=f"ff_rm{s}_{t}")
                b.tt(rmx[:, t:t + 1], rmx[:, t:t + 1],
                     stt[t % 2][:, SC3:SC3 + 1], ALU.max, tag=f"ff_rmu{s}_{t}")
            hv = h_d.h[:, :, s * 256:(s + 1) * 256].rearrange("t p w -> p t w")
            b.dma("gpsimd",
                  raw_apr(h_d, hv, (0, 128), (0, NOWN * DFF)),
                  hst[s % 2].full, tag=f"ff_hsp{s}")
        b.barrier()

    hbuf = alloc_sb(env, stack, "f_h", [128, 2 * 8192], BF16)
    nT = alloc_sb(env, stack, "f_nT", [128, 64 * 256], BF16)
    cpj_b = [alloc_sb(env, stack, f"f_cp{i}", [128, 2048], BF16)
             for i in range(3)]
    wsl_f2 = alloc_sb(env, stack, "f_wf2", [128, 2 * 2048], F32)
    wsl_i2 = alloc_sb(env, stack, "f_wi2", [128, 2048], F32)
    rbuf = alloc_sb(env, stack, "f_rb", [128, 1024], F32)
    tbuf = alloc_sb(env, stack, "f_tb", [128, 1024], F32)
    ibuf = alloc_sb(env, stack, "f_ib", [128, 1024], F32)
    nstg = alloc_sb(env, stack, "f_ns", [128, 1024], BF16)
    ow_ = alloc_sb(env, stack, "f_ow", [128, D], F32)
    ow = [ow_, ow_]

    for quarter in range(4):
        tiles = [quarter * 2, quarter * 2 + 1]
        hs1 = ExitStack()
        pst = alloc_ps(env, hs1, "f_pst2", [128, 1024], BF16)
        for i, t in enumerate(tiles):
            b.dma("sync", hbuf[:, i * 8192:(i + 1) * 8192],
                  raw_apr(h_d, h_d.h[t], (0, 128), (0, NOWN * DFF)),
                  tag=f"ff_hld{quarter}_{i}")
        for i, t in enumerate(tiles):
            r = i % 2
            c = lambda k: stt[r][:, k:k + 1]
            b.ts(c(MX), rmx[:, t:t + 1], AMS[:, W_CFC:W_CFC + 1],
                 op0=ALU.mult, s2=0.0, op1=ALU.max, tag=f"ff_xm{t}")
            b.tt(c(MX), c(MX), c(MX), ALU.mult, tag=f"ff_xm2{t}")
            b.ts(c(MXC), c(MX), 1e-5, op0=ALU.max, tag=f"ff_xmc{t}")
            b.recip(c(RECM), c(MXC), tag=f"ff_rc{t}")
            b.ts(c(SC1), c(RECM), 255.0, tag=f"ff_c2{t}")
            b.act(c(SC2), c(SC1), AF.Ln, tag=f"ff_sql{t}")
            b.act(c(SC2), c(SC2), AF.Exp, scale=0.5, tag=f"ff_sqe{t}")
            b.ts(esc[:, i:i + 1], c(MXC), 1.0 / 255.0, tag=f"ff_is{t}")
            b.tt(esc[:, i:i + 1], esc[:, i:i + 1],
                 AMS[:, W_CPROJ:W_CPROJ + 1], ALU.mult, tag=f"ff_is2{t}")
            for piece in range(8):
                hsl = hbuf[:, i * 8192 + piece * 1024:i * 8192 + (piece + 1) * 1024]
                b.ts(rbuf.full, hsl, 0.0, op0=ALU.max, tag=f"ff_rl{t}_{piece}")
                b.act(tbuf.full, rbuf.full, AF.Square, scale=c(SC2),
                      tag=f"ff_s2{t}_{piece}")
                b.ts(ibuf.full, tbuf.full, MAGIC, MAGIC, op0=ALU.add,
                     op1=ALU.subtract, tag=f"ff_rn{t}_{piece}")
                b.ts(nstg.full, ibuf.full, 1.0, op0=ALU.mult,
                     tag=f"ff_nb{t}_{piece}")
                for cc in range(8):
                    b.transpose(pst[:, cc * 128:(cc + 1) * 128],
                                nstg[:, cc * 128:(cc + 1) * 128], ident.full,
                                tag=f"ff_ntp{t}_{piece}_{cc}")
                cg0 = piece * 8
                dst = nT.h.rearrange("p (cg w) -> p cg w", cg=64)[
                    :, cg0:cg0 + 8, i * 128:(i + 1) * 128]
                b.copy(raw_apr(nT, dst, (0, 128),
                               (cg0 * 256, (cg0 + 8) * 256)),
                       pst.full, eng="scalar", tag=f"ff_ne{t}_{piece}")
        b.barrier()
        hs1.close()
        hs2 = ExitStack()
        psW = [alloc_ps(env, hs2, f"f_psW{i}", [128, 2048], F32)
               for i in range(2)]
        for ch in range(64):
            r = ch % 3
            if quarter == 0:
                v = raw_apr(cpj, cpj.h[ch * 128:(ch + 1) * 128, :],
                            (0, 128), (0, 2048))
                b.dma("sync", wsl_f2[:, (ch % 2) * 2048:(ch % 2 + 1) * 2048], v,
                      tag=f"ff2_wld{quarter}_{ch}")
                b.ts(wsl_i2.full, wsl_f2[:, (ch % 2) * 2048:(ch % 2 + 1) * 2048],
                     RAMS[:, W_CPROJ:W_CPROJ + 1], op0=ALU.mult,
                     tag=f"ff2_ws{quarter}_{ch}")
                b.ts(wsl_i2.full, wsl_i2.full, MAGIC, MAGIC, op0=ALU.add,
                     op1=ALU.subtract, tag=f"ff2_wr{quarter}_{ch}")
                b.ts(cpj_b[r].full, wsl_i2.full, -1.0, 1.0, op0=ALU.max,
                     op1=ALU.min, tag=f"ff2_wc{quarter}_{ch}")
                b.dma("gpsimd",
                      raw_apr(cpjq, cpjq.h[ch], (0, 128),
                              (ch * 2048, (ch + 1) * 2048)),
                      cpj_b[r].full, tag=f"ff2_wst{quarter}_{ch}")
            else:
                b.dma("sync", cpj_b[r].full,
                      raw_apr(cpjq, cpjq.h[ch], (0, 128),
                              (ch * 2048, (ch + 1) * 2048)),
                      tag=f"ff2_wlq{quarter}_{ch}")
            for ti in range(2):
                g = ("W2", quarter, ti)
                for ds in range(4):
                    b.mm(psW[ti][:, ds * 512:(ds + 1) * 512],
                         nT[:, ch * 256 + ti * 128:ch * 256 + (ti + 1) * 128],
                         cpj_b[r][:, ds * 512:(ds + 1) * 512],
                         start=(ch == 0), stop=(ch == 63), group=g,
                         tag=f"ff2_mm{quarter}_{ch}_{ti}_{ds}")
        for ti in range(2):
            t = tiles[ti]
            r = ti % 2
            c = lambda k: stt[r][:, k:k + 1]
            b.act(ow[ti].full, psW[ti].full, AF.Copy,
                  scale=esc[:, ti:ti + 1], tag=f"ff2_ev{quarter}_{ti}")
            b.tt(ow[ti].full, ow[ti].full,
                 xstream[:, t * 2048:(t + 1) * 2048], ALU.add,
                 tag=f"ff2_res{quarter}_{ti}")
            emit_ln_stats(b, stt[r], ow[ti].full, sq.full, EPS_LN, 1.0 / D)
            b.ts(xstream[:, t * 2048:(t + 1) * 2048], ow[ti].full,
                 c(MEAN), c(RSTD), op0=ALU.subtract, op1=ALU.mult,
                 tag=f"ff2_str{quarter}_{ti}")
        b.barrier()
        hs2.close()



def phase_l1_attn(b, env, stack):
    ident = env.sb["ident"]
    xstream = env.sb["xstream"]
    x2nT_d = env.dram["x2nT_d"]
    wqkv = env.dram["wqkvT"]
    CRd, sel0d, sel1d = env.dram["CR"], env.dram["sel0"], env.dram["sel1"]

    sq = alloc_sb(env, stack, "l_sq", [128, 2048], F32)
    stt = [alloc_sb(env, stack, f"l_tt{i}", [128, 16], F32) for i in range(2)]
    rstdk = alloc_sb(env, stack, "l_rk", [128, 8], F32)
    tmp16 = alloc_sb(env, stack, "l_t16", [128, 16], F32)
    send = alloc_sb(env, stack, "l_send", [128, 64], F32)
    addv = alloc_sb(env, stack, "l_addv", [128, 64], F32)
    kvc_d, wc_d = env.dram["kvc_d"], env.dram["wc_d"]
    y_d = env.dram["y_d"]

    qscope = ExitStack()
    stack.enter_context(qscope)
    qkvbufs = {
        "wslb": [alloc_sb(env, qscope, f"l_ws{i}", [128, 16 * 512], BF16)
                 for i in range(2)],
        "xtn": [alloc_sb(env, qscope, f"l_xt{i}", [128, 2048], BF16)
                for i in range(3)],
        "psK": [alloc_ps(env, qscope, f"l_psK{i}", [128, 512], F32)
                for i in range(2)],
    }

    def qkv_proj(slab0, dst_write, tag):
        wslb, xtn, psK = qkvbufs["wslb"], qkvbufs["xtn"], qkvbufs["psK"]
        for si in range(4):
            s = slab0 + si
            r = si % 2
            b.dma("gpsimd", wslb[r].full,
                  raw_apr(wqkv, wqkv.h[s].rearrange("(g p) w -> p g w", p=128),
                          (0, 128), (s * 8192, (s + 1) * 8192)),
                  tag=f"{tag}_w{s}")
            for t in range(NOWN):
                b.dma("sync", xtn[t % 3].full,
                      raw_apr(x2nT_d, x2nT_d.h[t], (0, 128),
                              (t * 2048, (t + 1) * 2048)),
                      tag=f"{tag}_x{s}_{t}")
                g = (tag, s, t)
                for ch in range(16):
                    b.mm(psK[t % 2][:, 0:512],
                         xtn[t % 3][:, ch * 128:(ch + 1) * 128],
                         wslb[r][:, ch * 512:(ch + 1) * 512],
                         start=(ch == 0), stop=(ch == 15), group=g,
                         tag=f"{tag}_mm{s}_{t}_{ch}")
                dst_write(t, si, psK[t % 2], t % 2)

    with ExitStack() as wscope:
        w_tm = alloc_sb(env, wscope, "l_w", [128, NOWN * 2048], BF16)
        wv_tm = alloc_sb(env, wscope, "l_wv", [128, NOWN * 2048], BF16)

        with ExitStack() as ph:
            xqb = alloc_sb(env, ph, "l_xq", [128, 2048], BF16)
            stg = [alloc_sb(env, ph, f"l_st{i}", [128, 2048], BF16)
                   for i in range(2)]
            pst = alloc_ps(env, ph, "l_pst", [128, 1024], BF16)
            for t in range(NOWN):
                r = t % 2
                xs_t = xstream[:, t * 2048:(t + 1) * 2048]
                emit_rms_stats(b, stt[r], xs_t, sq.full, EPS_RMS, 1.0 / D)
                c = lambda k: stt[r][:, k:k + 1]
                b.ts(xqb.full, xs_t, c(RSTD), op0=ALU.mult, tag=f"la_xn{t}")
                for hf in range(2):
                    for cc in range(8):
                        ch = hf * 8 + cc
                        b.transpose(pst[:, cc * 128:(cc + 1) * 128],
                                    xqb[:, ch * 128:(ch + 1) * 128], ident.full,
                                    tag=f"la_tp{t}_{ch}")
                    b.copy(stg[r][:, hf * 1024:(hf + 1) * 1024], pst.full,
                           eng="scalar", tag=f"la_te{t}_{hf}")
                b.dma("gpsimd",
                      raw_apr(x2nT_d, x2nT_d.h[t], (0, 128),
                              (t * 2048, (t + 1) * 2048)),
                      stg[r].full, tag=f"la_sp{t}")
            b.barrier()

        with ExitStack() as kscope:
            k_tm = alloc_sb(env, kscope, "l_k", [128, NOWN * 2048], BF16)

            def k_ev(t, si, ps, r):
                b.copy(k_tm[:, t * 2048 + si * 512:t * 2048 + (si + 1) * 512],
                       ps[:, 0:512], eng="scalar", tag=f"k_ev{t}_{si}")
            qkv_proj(4, k_ev, "lk")
            for t in range(NOWN):
                r = t % 2
                kt = k_tm[:, t * 2048:(t + 1) * 2048]
                emit_rms_stats(b, stt[r], kt, sq.full, EPS_RMS, 1.0 / D)
                b.copy(rstdk[:, t:t + 1], stt[r][:, RSTD:RSTD + 1],
                       tag=f"k_rs{t}")
                b.act(w_tm[:, t * 2048:(t + 1) * 2048], kt, AF.Exp,
                      scale=rstdk[:, t:t + 1], tag=f"k_exp{t}")
            b.barrier()

        def v_ev(t, si, ps, r):
            sl = slice(t * 2048 + si * 512, t * 2048 + (si + 1) * 512)
            b.tt(raw_apr(wv_tm, wv_tm.h[:, sl], (0, 128), (sl.start, sl.stop)),
                 ps[:, 0:512],
                 raw_apr(w_tm, w_tm.h[:, sl], (0, 128), (sl.start, sl.stop)),
                 ALU.mult, tag=f"v_ev{t}_{si}")
        qkv_proj(8, v_ev, "lv")

        with ExitStack() as ph:
            NR = 3
            wvT = [alloc_sb(env, ph, f"l_wvT{i}", [128, 1024], BF16)
                   for i in range(NR)]
            wT = [alloc_sb(env, ph, f"l_wT{i}", [128, 1024], BF16)
                  for i in range(NR)]
            kvcC = [alloc_sb(env, ph, f"l_kvcC{i}", [128, 1024], BF16)
                    for i in range(NR)]
            wcC = [alloc_sb(env, ph, f"l_wcC{i}", [128, 1024], BF16)
                   for i in range(NR)]
            pstA = alloc_ps(env, ph, "l_psA", [128, 1024], BF16)
            pstB = alloc_ps(env, ph, "l_psB", [128, 1024], BF16)
            for c in range(16):
                r = c % NR
                for t in range(NOWN):
                    b.transpose(pstA[:, t * 128:(t + 1) * 128],
                                wv_tm[:, t * 2048 + c * 128:t * 2048 + (c + 1) * 128],
                                ident.full, tag=f"sc_tA{c}_{t}")
                    b.transpose(pstB[:, t * 128:(t + 1) * 128],
                                w_tm[:, t * 2048 + c * 128:t * 2048 + (c + 1) * 128],
                                ident.full, tag=f"sc_tB{c}_{t}")
                b.copy(wvT[r].full, pstA.full, eng="scalar", tag=f"sc_eA{c}")
                b.copy(wT[r].full, pstB.full, eng="scalar", tag=f"sc_eB{c}")
                b.scan(kvcC[r].full, wvT[r].full, wvT[r].full, 0.0,
                       ALU.add, ALU.bypass, tag=f"sc_kv{c}")
                b.scan(wcC[r].full, wT[r].full, wT[r].full, 0.0,
                       ALU.add, ALU.bypass, tag=f"sc_w{c}")
                b.copy(send[:, 0 * 16 + c:0 * 16 + c + 1],
                       kvcC[r][:, 511:512], tag=f"sc_g0{c}")
                b.copy(send[:, 1 * 16 + c:1 * 16 + c + 1],
                       kvcC[r][:, 1023:1024], tag=f"sc_g1{c}")
                b.copy(send[:, 2 * 16 + c:2 * 16 + c + 1],
                       wcC[r][:, 511:512], tag=f"sc_g2{c}")
                b.copy(send[:, 3 * 16 + c:3 * 16 + c + 1],
                       wcC[r][:, 1023:1024], tag=f"sc_g3{c}")
                b.dma("gpsimd",
                      raw_apr(kvc_d, kvc_d.h[c], (0, 128),
                              (c * 1024, (c + 1) * 1024)),
                      kvcC[r].full, tag=f"sc_sp0{c}")
                b.dma("gpsimd",
                      raw_apr(wc_d, wc_d.h[c], (0, 128),
                              (c * 1024, (c + 1) * 1024)),
                      wcC[r].full, tag=f"sc_sp1{c}")
            b.barrier()

    cc_src, cc_dst = env.dram["cc_src"], env.dram["cc_dst"]
    with ExitStack() as ph:
        ag = [alloc_sb(env, ph, f"l_ag{i}", [128, 64], F32) for i in range(2)]
        sib = alloc_sb(env, ph, "l_sib", [128, 64], F32)
        CRs = alloc_sb(env, ph, "l_CR", [128, 4], F32)
        sels = alloc_sb(env, ph, "l_sel", [128, 2], F32)
        b.dma("sync", CRs.full, CRd.full, tag="cc_cr")
        b.dma("sync", sels[:, 0:1], sel0d.full, tag="cc_s0")
        b.dma("sync", sels[:, 1:2], sel1d.full, tag="cc_s1")
        b.dma("gpsimd", cc_src.full, send.full, tag="cc_up")
        b.collective("AllGather", ALU.bypass,
                     [[0, 1], [2, 3], [4, 5], [6, 7]],
                     [cc_src.full], [cc_dst.full], tag="cc_ag")
        b.dma("gpsimd", ag[0].full,
              raw_apr(cc_dst, cc_dst.h[0:128, :], (0, 128), (0, 64)),
              tag="cc_d0")
        b.dma("gpsimd", ag[1].full,
              raw_apr(cc_dst, cc_dst.h[128:256, :], (0, 128), (0, 64)),
              tag="cc_d1")
        b.ts(ag[0].full, ag[0].full, sels[:, 0:1], op0=ALU.mult, tag="cc_m0")
        b.ts(ag[1].full, ag[1].full, sels[:, 1:2], op0=ALU.mult, tag="cc_m1")
        b.tt(sib.full, ag[0].full, ag[1].full, ALU.add, tag="cc_sib")
        for kind in range(2):
            r1c = slice(kind * 32, kind * 32 + 16)
            r2c = slice(kind * 32 + 16, kind * 32 + 32)
            for run in range(2):
                dst = addv[:, kind * 32 + run * 16:kind * 32 + (run + 1) * 16]
                b.ts(dst, sib[:, r1c], CRs[:, run * 2:run * 2 + 1],
                     op0=ALU.mult, tag=f"cc_a{kind}{run}")
                b.ts(tmp16.full, sib[:, r2c], CRs[:, run * 2 + 1:run * 2 + 2],
                     op0=ALU.mult, tag=f"cc_b{kind}{run}")
                b.tt(dst, dst, tmp16.full, ALU.add, tag=f"cc_c{kind}{run}")
        b.barrier()

    with ExitStack() as ph:
        sigq = alloc_sb(env, ph, "l_sigq", [128, NOWN * 2048], BF16)
        with ExitStack() as qs:
            q_tm = alloc_sb(env, qs, "l_q", [128, NOWN * 2048], BF16)

            def q_ev(t, si, ps, r):
                b.copy(q_tm[:, t * 2048 + si * 512:t * 2048 + (si + 1) * 512],
                       ps[:, 0:512], eng="scalar", tag=f"q_ev{t}_{si}")
            qkv_proj(0, q_ev, "lq")
            for t in range(NOWN):
                r = t % 2
                qt = q_tm[:, t * 2048:(t + 1) * 2048]
                emit_rms_stats(b, stt[r], qt, sq.full, EPS_RMS, 1.0 / D)
                b.act(sigq[:, t * 2048:(t + 1) * 2048], qt, AF.Sigmoid,
                      scale=stt[r][:, RSTD:RSTD + 1], tag=f"q_sig{t}")
            b.barrier()
        rc = [alloc_sb(env, ph, f"l_rc{i}", [128, 1024], F32) for i in range(2)]
        kvb = [alloc_sb(env, ph, f"l_kvb{i}", [128, 1024], BF16)
               for i in range(2)]
        wcb = [alloc_sb(env, ph, f"l_wcb{i}", [128, 1024], BF16)
               for i in range(2)]
        sqT = [alloc_sb(env, ph, f"l_sqT{i}", [128, 1024], BF16)
               for i in range(2)]
        pstC = alloc_ps(env, ph, "l_psC", [128, 1024], BF16)
        for c in range(16):
            r = c % 2
            b.dma("gpsimd", kvb[r].full,
                  raw_apr(kvc_d, kvc_d.h[c], (0, 128), (c * 1024, (c + 1) * 1024)),
                  tag=f"y_lk{c}")
            b.dma("gpsimd", wcb[r].full,
                  raw_apr(wc_d, wc_d.h[c], (0, 128), (c * 1024, (c + 1) * 1024)),
                  tag=f"y_lw{c}")
            for run in range(2):
                b.ts(kvb[r][:, run * 512:(run + 1) * 512],
                     kvb[r][:, run * 512:(run + 1) * 512],
                     addv[:, 0 * 32 + run * 16 + c:0 * 32 + run * 16 + c + 1],
                     op0=ALU.add, tag=f"y_ck{c}_{run}")
                b.ts(wcb[r][:, run * 512:(run + 1) * 512],
                     wcb[r][:, run * 512:(run + 1) * 512],
                     addv[:, 1 * 32 + run * 16 + c:1 * 32 + run * 16 + c + 1],
                     op0=ALU.add, tag=f"y_cw{c}_{run}")
            b.ts(rc[r].full, wcb[r].full, 1e-6, op0=ALU.add, tag=f"y_e{c}")
            b.recip(rc[r].full, rc[r].full, tag=f"y_r{c}")
            b.tt(rc[r].full, rc[r].full, kvb[r].full, ALU.mult, tag=f"y_kv{c}")
            for t in range(NOWN):
                b.transpose(pstC[:, t * 128:(t + 1) * 128],
                            sigq[:, t * 2048 + c * 128:t * 2048 + (c + 1) * 128],
                            ident.full, tag=f"y_tp{c}_{t}")
            b.copy(sqT[r].full, pstC.full, eng="scalar", tag=f"y_te{c}")
            b.tt(raw_apr(sqT[r], sqT[r].h[:, :], (0, 128), (0, 1024)),
                 rc[r].full, sqT[r].full, ALU.mult, tag=f"y_y{c}")
            b.dma("gpsimd",
                  raw_apr(y_d, y_d.h[c], (0, 128), (c * 1024, (c + 1) * 1024)),
                  sqT[r].full, tag=f"y_sp{c}")
        b.barrier()

    b.barrier()
    qscope.close()
    wsw = env.dram["wswT"]
    wout = env.dram["woutT"]
    with ExitStack() as ph:
        y = alloc_sb(env, ph, "l_y", [128, 16 * 1024], BF16)
        for c in range(16):
            b.dma("gpsimd", y[:, c * 1024:(c + 1) * 1024],
                  raw_apr(y_d, y_d.h[c], (0, 128), (c * 1024, (c + 1) * 1024)),
                  tag=f"y_ld{c}")
        acc = alloc_sb(env, ph, "l_acc", [128, NOWN * 2048], F16)
        hh = alloc_sb(env, ph, "l_hh", [128, 4 * 1024], BF16)
        woq = alloc_sb(env, ph, "l_woq", [128, 4 * 2048], BF16)
        uslab_ = [alloc_sb(env, ph, f"l_us{i}", [128, 16 * 512], BF16)
                  for i in range(2)]
        gslab_ = [alloc_sb(env, ph, f"l_gs{i}", [128, 16 * 512], BF16)
                  for i in range(2)]
        sgb = alloc_sb(env, ph, "l_sg", [128, 1024], BF16)
        psU = alloc_ps(env, ph, "l_psU", [128, 1024], F32)
        psG = alloc_ps(env, ph, "l_psG", [128, 1024], F32)
        psO = [alloc_ps(env, ph, f"l_psO{i}", [128, 512], F32) for i in range(2)]
        first = [True] * NOWN * 4
        for e8 in range(8):
            for h16 in range(2):
                e16 = e8 * 2 + h16
                uslab = uslab_[e16 % 2]
                gslab = gslab_[e16 % 2]
                b.dma("gpsimd", uslab.full,
                      raw_apr(wsw, wsw.h[e16].rearrange("(g p) w -> p g w", p=128),
                              (0, 128), (e16 * 8192, (e16 + 1) * 8192)),
                      tag=f"sw_u{e16}")
                b.dma("gpsimd", gslab.full,
                      raw_apr(wsw, wsw.h[16 + e16].rearrange("(g p) w -> p g w", p=128),
                              (0, 128), ((16 + e16) * 8192, (17 + e16) * 8192)),
                      tag=f"sw_g{e16}")
                for bi in range(4):
                    du = e16 * 4 + bi
                    gU = ("U", du)
                    gG = ("G", du)
                    for th in range(2):
                        for ch in range(16):
                            lu = uslab.h.rearrange("p (c w) -> p c w", c=16)[
                                :, ch, bi * 128:(bi + 1) * 128]
                            lg = gslab.h.rearrange("p (c w) -> p c w", c=16)[
                                :, ch, bi * 128:(bi + 1) * 128]
                            rhsv = y.h.rearrange("p (c t) -> p c t", c=16)[
                                :, ch, th * 512:(th + 1) * 512]
                            rapr = raw_apr(y, rhsv, (0, 128),
                                           (ch * 1024 + th * 512,
                                            ch * 1024 + (th + 1) * 512))
                            b.mm(psU[:, th * 512:(th + 1) * 512],
                                 raw_apr(uslab, lu, (0, 128),
                                         (ch * 512 + bi * 128,
                                          ch * 512 + (bi + 1) * 128)),
                                 rapr, start=(ch == 0), stop=(ch == 15),
                                 group=gU, tag=f"sw_mu{du}_{th}_{ch}")
                            b.mm(psG[:, th * 512:(th + 1) * 512],
                                 raw_apr(gslab, lg, (0, 128),
                                         (ch * 512 + bi * 128,
                                          ch * 512 + (bi + 1) * 128)),
                                 rapr, start=(ch == 0), stop=(ch == 15),
                                 group=gG, tag=f"sw_mg{du}_{th}_{ch}")
                    b.act(sgb.full, psG.full, AF.Silu, tag=f"sw_si{du}")
                    b.tt(hh[:, (du % 4) * 1024:(du % 4 + 1) * 1024],
                         psU.full, sgb.full, ALU.mult, tag=f"sw_hh{du}")
                    b.dma("gpsimd", woq[:, (du % 4) * 2048:(du % 4 + 1) * 2048],
                          raw_apr(wout, wout.h[du * 128:(du + 1) * 128, :],
                                  (0, 128), (0, 2048)), tag=f"sw_wo{du}")
                for t in range(NOWN):
                    for ds in range(4):
                        rot = (t * 4 + ds) % 2
                        g = ("O", e16, t, ds)
                        for dloc in range(4):
                            b.mm(psO[rot][:, 0:512],
                                 hh[:, dloc * 1024 + t * 128:dloc * 1024 + (t + 1) * 128],
                                 woq[:, dloc * 2048 + ds * 512:dloc * 2048 + (ds + 1) * 512],
                                 start=(dloc == 0), stop=(dloc == 3), group=g,
                                 tag=f"sw_o{e16}_{t}_{ds}_{dloc}")
                        dst = acc[:, t * 2048 + ds * 512:t * 2048 + (ds + 1) * 512]
                        if e16 == 0:
                            b.copy(dst, psO[rot][:, 0:512], eng="scalar",
                                   tag=f"sw_oc{e16}_{t}_{ds}")
                        else:
                            b.tt(dst, psO[rot][:, 0:512], dst, ALU.add,
                                 tag=f"sw_oa{e16}_{t}_{ds}")
        ow = alloc_sb(env, ph, "l_ow", [128, 2048], F32)
        for t in range(NOWN):
            r = t % 2
            b.stt(ow.full, xstream[:, t * 2048:(t + 1) * 2048], 2.0,
                  acc[:, t * 2048:(t + 1) * 2048], ALU.mult, ALU.add,
                  tag=f"l1_res{t}")
            emit_ln_stats(b, stt[r], ow.full, sq.full, EPS_LN, 1.0 / D)
            c = lambda k: stt[r][:, k:k + 1]
            b.ts(xstream[:, t * 2048:(t + 1) * 2048], ow.full, c(MEAN),
                 c(RSTD), op0=ALU.subtract, op1=ALU.mult, tag=f"l1_str{t}")
        b.barrier()



def phase_l1_ffn(b, env, stack):
    ident = env.sb["ident"]
    xstream = env.sb["xstream"]
    h1_d = env.dram["h1_d"]
    W1d, W3d = env.dram["W1T"], env.dram["W3T"]
    W2d = env.dram["W2relay"]
    yout = env.dram["y_own"]

    sq = alloc_sb(env, stack, "n_sq", [128, 2048], F32)
    stt = [alloc_sb(env, stack, f"n_tt{i}", [128, 16], F32) for i in range(2)]
    H128s = alloc_sb(env, stack, "n_H", [128, 128], BF16)
    G64s = alloc_sb(env, stack, "n_G", [128, 128], BF16)
    b.dma("sync", H128s.full, env.dram["H128"].full, tag="n_ldH")
    b.dma("sync", G64s.full, env.dram["G64"].full, tag="n_ldG")

    with ExitStack() as ph:
        x3T = alloc_sb(env, ph, "n_x3T", [128, NOWN * 2048], BF16)
        xb = alloc_sb(env, ph, "n_xb", [128, 2048], BF16)
        pst = alloc_ps(env, ph, "n_pst", [128, 1024], BF16)
        for t in range(NOWN):
            b.copy(xb.full, xstream[:, t * 2048:(t + 1) * 2048], tag=f"n_xc{t}")
            for hf in range(2):
                for cc in range(8):
                    ch = hf * 8 + cc
                    b.transpose(pst[:, cc * 128:(cc + 1) * 128],
                                xb[:, ch * 128:(ch + 1) * 128], ident.full,
                                tag=f"n_tp{t}_{ch}")
                b.copy(x3T[:, t * 2048 + hf * 1024:t * 2048 + (hf + 1) * 1024],
                       pst.full, eng="scalar", tag=f"n_te{t}_{hf}")
        w1s = [alloc_sb(env, ph, f"n_w1{i}", [128, 16 * 512], BF16)
               for i in range(2)]
        w3s = [alloc_sb(env, ph, f"n_w3{i}", [128, 16 * 512], BF16)
               for i in range(2)]
        sgb = alloc_sb(env, ph, "n_sg", [128, 1024], BF16)
        h0 = alloc_sb(env, ph, "n_h0", [128, 1024], BF16)
        h1b = [alloc_sb(env, ph, f"n_h1{i}", [128, 1024], BF16)
               for i in range(2)]
        psU = alloc_ps(env, ph, "n_psU", [128, 1024], F32)
        psL = alloc_ps(env, ph, "n_psL", [128, 1024], F32)
        psF = alloc_ps(env, ph, "n_psF", [128, 1024], F32)
        for hb in range(64):
            sl, off = hb // 4, hb % 4
            r = sl % 2
            if off == 0:
                b.dma("gpsimd", w1s[r].full,
                      raw_apr(W1d, W1d.h[sl].rearrange("(g p) w -> p g w", p=128),
                              (0, 128), (sl * 8192, (sl + 1) * 8192)),
                      tag=f"n_lw1{sl}")
                b.dma("gpsimd", w3s[r].full,
                      raw_apr(W3d, W3d.h[sl].rearrange("(g p) w -> p g w", p=128),
                              (0, 128), (sl * 8192, (sl + 1) * 8192)),
                      tag=f"n_lw3{sl}")
            gU, gL = ("nU", hb), ("nL", hb)
            for th in range(2):
                for ch in range(16):
                    lu = w1s[r].h.rearrange("p (c w) -> p c w", c=16)[
                        :, ch, off * 128:(off + 1) * 128]
                    ll = w3s[r].h.rearrange("p (c w) -> p c w", c=16)[
                        :, ch, off * 128:(off + 1) * 128]
                    rhs = _xqT_rhs(x3T, th, ch, ntile=8)
                    b.mm(psU[:, th * 512:(th + 1) * 512],
                         raw_apr(w1s[r], lu, (0, 128),
                                 (ch * 512 + off * 128, ch * 512 + (off + 1) * 128)),
                         rhs, start=(ch == 0), stop=(ch == 15), group=gU,
                         tag=f"n_mu{hb}_{th}_{ch}")
                    b.mm(psL[:, th * 512:(th + 1) * 512],
                         raw_apr(w3s[r], ll, (0, 128),
                                 (ch * 512 + off * 128, ch * 512 + (off + 1) * 128)),
                         rhs, start=(ch == 0), stop=(ch == 15), group=gL,
                         tag=f"n_ml{hb}_{th}_{ch}")
            b.act(sgb.full, psU.full, AF.Silu, tag=f"n_si{hb}")
            b.tt(h0.full, psL.full, sgb.full, ALU.mult, tag=f"n_h0{hb}")
            gF = ("nF", hb)
            b.mm(psF[:, 0:512], H128s.full, h0[:, 0:512], start=True,
                 stop=True, group=gF, tag=f"n_f1a{hb}")
            b.mm(psF[:, 512:1024], H128s.full, h0[:, 512:1024], start=True,
                 stop=True, group=gF, tag=f"n_f1b{hb}")
            b.act(h1b[hb % 2].full, psF.full, AF.Copy,
                  scale=1.0 / float(np.sqrt(128.0)), tag=f"n_f1e{hb}")
            b.dma("gpsimd",
                  raw_apr(h1_d, h1_d.h[hb], (0, 128), (hb * 1024, (hb + 1) * 1024)),
                  h1b[hb % 2].full, tag=f"n_h1s{hb}")
        b.barrier()

    with ExitStack() as ph:
        h3 = alloc_sb(env, ph, "n_h3", [128, 64 * 512], BF16)
        h2b = [alloc_sb(env, ph, f"n_h2{i}", [128, 4 * 512], BF16)
               for i in range(2)]
        w2q = [alloc_sb(env, ph, f"n_w2{i}", [128, 8 * 512], BF16)
               for i in range(2)]
        obuf = [alloc_sb(env, ph, f"n_ob{i}", [128, 2048], F32)
                for i in range(4)]
        ybuf = [alloc_sb(env, ph, f"n_yb{i}", [128, 2048], F32)
                for i in range(2)]
        psF2 = alloc_ps(env, ph, "n_psF2", [128, 2048], F32)
        psW = [alloc_ps(env, ph, f"n_psW{i}", [128, 512], F32)
               for i in range(4)]
        for th in range(2):
            for cq in range(16):
                r = cq % 2
                for p0 in range(2):
                    ov = h2b[r].h[p0 * 64:(p0 + 1) * 64, :].rearrange(
                        "p (cc t) -> p cc t", cc=4)
                    iv = h1_d.h[:, 8 * cq + p0:8 * cq + 8:2,
                                th * 512:(th + 1) * 512]
                    b.dma("sync",
                          raw_apr(h2b[r], ov, (p0 * 64, (p0 + 1) * 64),
                                  (0, 2048)),
                          raw_apr(h1_d, iv, (0, 128),
                                  (8 * cq * 1024, (8 * cq + 8) * 1024)),
                          tag=f"n_h2l{th}_{cq}_{p0}")
                for cc in range(4):
                    b.mm(psF2[:, cc * 512:(cc + 1) * 512], G64s.full,
                         h2b[r][:, cc * 512:(cc + 1) * 512], start=True,
                         stop=True, group=("F2", th, cq),
                         tag=f"n_f2{th}_{cq}_{cc}")
                b.act(h3[:, cq * 2048:(cq + 1) * 2048], psF2.full, AF.Copy,
                      scale=1.0 / 8.0, tag=f"n_f2e{th}_{cq}")
            for ds in range(4):
                for cb in range(8):
                    r = cb % 2
                    iv = W2d.h[ds, cb * 8:(cb + 1) * 8].rearrange(
                        "c p w -> p c w")
                    b.dma("gpsimd", w2q[r].full,
                          raw_apr(W2d, iv, (0, 128),
                                  (ds * 100000 + cb * 8, ds * 100000 + cb * 8 + 8)),
                          tag=f"n_w2l{th}_{ds}_{cb}")
                    for cc in range(8):
                        cg = cb * 8 + cc
                        for t in range(4):
                            b.mm(psW[t][:, 0:512],
                                 h3[:, cg * 512 + t * 128:cg * 512 + (t + 1) * 128],
                                 w2q[r][:, cc * 512:(cc + 1) * 512],
                                 start=(cg == 0), stop=(cg == 63),
                                 group=("W2f", th, ds, t),
                                 tag=f"n_w2m{th}_{ds}_{cg}_{t}")
                for t in range(4):
                    b.act(obuf[t][:, ds * 512:(ds + 1) * 512],
                          psW[t][:, 0:512], AF.Copy, tag=f"n_w2e{th}_{ds}_{t}")
            for t in range(4):
                tg = th * 4 + t
                r = t % 2
                b.tt(ybuf[r].full, obuf[t].full,
                     xstream[:, tg * 2048:(tg + 1) * 2048], ALU.add,
                     tag=f"n_res{tg}")
                emit_ln_stats(b, stt[r], ybuf[r].full, sq.full, EPS_LN, 1.0 / D)
                c_ = lambda k: stt[r][:, k:k + 1]
                b.ts(ybuf[r].full, ybuf[r].full, c_(MEAN), c_(RSTD),
                     op0=ALU.subtract, op1=ALU.mult, tag=f"n_ln{tg}")
                b.dma("gpsimd",
                      raw_apr(yout, yout.h[tg * 128:(tg + 1) * 128, :],
                              (0, 128), (0, D)),
                      ybuf[r].full, tag=f"n_out{tg}")
        b.barrier()



from concourse.bass_utils import run_bass_kernel_spmd


def kernel(**inputs):
    nc = build_program(upto="l1ffn", stream_out=False)
    in_maps = stage_inputs(inputs)
    res = run_bass_kernel_spmd(nc, in_maps, core_ids=list(range(8)))
    return unstage_output(res.results)


# version 10 (final)
# speedup vs baseline: 1.0242x; 1.0242x over previous
import sys as _sys
for _p in ("/opt/trn_rl_repo",):
    if _p not in _sys.path:
        try:
            import concourse.bass
            break
        except Exception:
            _sys.path.insert(0, _p)

import numpy as np
import ml_dtypes

import concourse.bass as bass
import concourse.mybir as mybir

F32 = mybir.dt.float32
BF16 = mybir.dt.bfloat16
F16 = mybir.dt.float16
I32 = mybir.dt.int32
AF = mybir.ActivationFunctionType
ALU = mybir.AluOpType
AX = mybir.AxisListType

BF16NP = ml_dtypes.bfloat16

B = 4
S = 2048
D = 2048
NHEAD = 16
HD = 128
DFF = 8192
NT = 16
NOWN = 8
TOK = NOWN * 128

OWN_TILES = [
    [0, 1, 2, 3, 12, 13, 14, 15],
    [4, 5, 6, 7, 8, 9, 10, 11],
]
EXT = [16, 15, 14, 13, 8, 7, 6, 5]
MASKW = 5

EPS_SUBLN = 1e-5
EPS_RMS = 1.1920929e-07
EPS_LN = 1e-5
NEG = -1e30

RSQRT_HD = 1.0 / float(np.sqrt(128.0))
MAGIC = 12582912.0
FFN_PARTS = 4
FFN_MM1 = 4


COMPUTE_ENGINES = ("tensor", "vector", "scalar", "gpsimd")
ALL_ENGINES = COMPUTE_ENGINES + ("sync",)
QSTREAM = {"sync": "qsync", "scalar": "qscal", "gpsimd": "qgp"}
DRAIN_ENGINES = ("vector", "scalar", "gpsimd")
SEM_CAP = 24000


class Tsr:
    __slots__ = ("h", "name", "space")

    def __init__(self, h, name, space):
        self.h = h
        self.name = name
        self.space = space

    def __getitem__(self, key):
        return APR(self, key)

    @property
    def full(self):
        return APR(self, None)


class APR:
    __slots__ = ("t", "ap", "p", "c")

    def __init__(self, t, key, ap_override=None):
        self.t = t
        shp = list(t.h.shape)
        if key is None:
            key = (slice(None),) * len(shp)
        if not isinstance(key, tuple):
            key = (key,)

        def rg(sl, n):
            if isinstance(sl, slice):
                return ((sl.start or 0), n if sl.stop is None else sl.stop)
            return (sl, sl + 1)
        self.p = rg(key[0], shp[0])
        self.c = rg(key[1], shp[1]) if len(key) > 1 and len(shp) > 1 else (0, shp[1] if len(shp) > 1 else 1)
        self.ap = ap_override if ap_override is not None else t.h[key]


def raw_apr(t, ap, prange, crange):
    a = APR.__new__(APR)
    a.t = t
    a.ap = ap
    a.p = prange
    a.c = crange
    return a


def _overlap(a, b):
    if a.t is not b.t:
        return False
    if a.t.space == "psum":
        return True
    return a.p[0] < b.p[1] and b.p[0] < a.p[1] \
        and a.c[0] < b.c[1] and b.c[0] < a.c[1]


class Op:
    __slots__ = ("stream", "issue_eng", "idx", "fn", "wait_ops", "sig", "sig_k",
                 "drain_before", "kind", "accum_group", "tag", "vc", "emitted",
                 "async_")

    def __init__(self, stream, issue_eng, idx, fn, kind, tag=""):
        self.stream = stream
        self.issue_eng = issue_eng
        self.idx = idx
        self.fn = fn
        self.wait_ops = []
        self.sig = False
        self.sig_k = None
        self.drain_before = False
        self.kind = kind
        self.accum_group = None
        self.tag = tag
        self.vc = {}
        self.emitted = False
        self.async_ = False


class Access:
    __slots__ = ("op", "apr", "is_write", "accum_group")

    def __init__(self, op, apr, is_write, accum_group=None):
        self.op = op
        self.apr = apr
        self.is_write = is_write
        self.accum_group = accum_group


class Builder:
    def __init__(self, nc):
        self.nc = nc
        self.stream_len = {}
        self.sig_count = {}
        self.pending = {e: [] for e in ALL_ENGINES}
        self.records = {}
        self.vc = {e: {} for e in ALL_ENGINES}
        self.drained = {e: -1 for e in DRAIN_ENGINES}
        self.op_count = 0
        self.n_drains = 0
        self.n_autowaits = 0
        self.n_dma = 0
        self.sems = None
        self._seen_waits = {e: {} for e in ALL_ENGINES}


    def op(self, eng, fn, reads=(), writes=(), accum_group=None, tag="",
           dma=False, after=()):
        if dma:
            q = QSTREAM[eng]
            qidx = self.stream_len.get(q, 0)
            self.stream_len[q] = qidx + 1
            stream = (q, qidx % self.NSEMQ)
            idx = qidx // self.NSEMQ
            self.stream_len[stream] = idx + 1
        else:
            stream = eng
            idx = self.stream_len.get(stream, 0)
            self.stream_len[stream] = idx + 1
        o = Op(stream, eng, idx, fn, "dma" if dma else "compute", tag)
        o.accum_group = accum_group
        self.pending[eng].append(o)
        self.op_count += 1
        if dma:
            self.n_dma += 1

        myvc = self.vc[eng]
        need = []
        need_drain = False

        def check(apr, is_w):
            nonlocal need_drain
            recs = self.records.get(id(apr.t))
            if not recs:
                return
            psum = apr.t.space == "psum"
            for acc in recs:
                if not (acc.is_write or is_w or psum):
                    continue
                if accum_group is not None and acc.accum_group == accum_group:
                    continue
                if not _overlap(apr, acc.apr):
                    continue
                p = acc.op
                if p is o:
                    continue
                if p.stream == stream and not p.async_:
                    if p.kind == "dma":
                        continue
                    if eng == "tensor":
                        continue
                    if p.idx > self.drained[eng]:
                        need_drain = True
                    continue
                if p.kind == "compute" and p.issue_eng == eng and not p.async_:
                    if eng == "tensor":
                        continue
                    if eng in DRAIN_ENGINES and p.idx > self.drained[eng]:
                        need_drain = True
                    continue
                if myvc.get(p.stream, -1) >= p.idx:
                    continue
                if p not in need:
                    need.append(p)

        for a in reads:
            check(a, False)
        for a in writes:
            check(a, True)
        for p in after:
            if p.stream == stream:
                continue
            if myvc.get(p.stream, -1) < p.idx and p not in need:
                need.append(p)

        if need_drain and eng in DRAIN_ENGINES:
            o.drain_before = True
            self.drained[eng] = (idx - 1) if not dma \
                else (self.stream_len.get(eng, 0) - 1)
            self.n_drains += 1

        for p in need:
            if p.emitted and not p.sig:
                raise RuntimeError(
                    f"dependency on already-emitted unsignaled op {p.tag!r} "
                    f"({p.stream}#{p.idx}) from {tag!r}")
            p.sig = True
            o.wait_ops.append(p)
            self.n_autowaits += 1
            cvc = dict(p.vc)
            cvc[p.stream] = p.idx
            for k_, v_ in cvc.items():
                if myvc.get(k_, -1) < v_:
                    myvc[k_] = v_
        o.vc = dict(myvc)

        for a in reads:
            self.records.setdefault(id(a.t), []).append(Access(o, a, False))
        for a in writes:
            self.records.setdefault(id(a.t), []).append(
                Access(o, a, True, accum_group))
        return o

    def barrier(self):
        evs = []
        last_by_stream = {}
        for eng in ALL_ENGINES:
            for o in self.pending[eng]:
                if o.kind in ("compute", "dma") and o.fn is not None:
                    last_by_stream[o.stream if not isinstance(o.stream, tuple)
                                   else o.stream] = o
        for o in last_by_stream.values():
            o.sig = True
            evs.append(o)
        for eng in ALL_ENGINES:
            myvc = self.vc[eng]
            nops = [p for p in evs
                    if p.stream != eng and myvc.get(p.stream, -1) < p.idx]
            if not nops and not self.pending[eng]:
                continue
            if nops:
                idx = self.stream_len.get(eng, 0)
                self.stream_len[eng] = idx + 1
                o = Op(eng, eng, idx, None, "wait", "barrier")
                o.wait_ops = nops
                self.pending[eng].append(o)
                for p in nops:
                    cvc = dict(p.vc)
                    cvc[p.stream] = p.idx
                    for k_, v_ in cvc.items():
                        if myvc.get(k_, -1) < v_:
                            myvc[k_] = v_
                o.vc = dict(myvc)
        self.records.clear()
        return evs


    NSEMQ = 12

    def make_sems(self, stack):
        self.sems = {}
        for s in ALL_ENGINES:
            for rot in range(3):
                self.sems[(s, rot)] = stack.enter_context(
                    self.nc.semaphore(f"as_{s}_{rot}"))
        for s in QSTREAM.values():
            for k in range(self.NSEMQ):
                self.sems[((s, k),)] = None
                self.sems[(s, k)] = stack.enter_context(
                    self.nc.semaphore(f"aq_{s}_{k}"))

    def _sig_value(self, op):
        if isinstance(op.stream, tuple):
            return op.stream, (op.idx + 1) * 16, 16
        cap_ops = SEM_CAP
        rot = op.sig_k // cap_ops
        val = op.sig_k - rot * cap_ops + 1
        if rot > 2:
            raise RuntimeError(f"sem rotation overflow on {op.stream}")
        return (op.stream, rot), val, 1

    def flush(self):
        nc = self.nc
        per_stream = {}
        for eng in ALL_ENGINES:
            for o in self.pending[eng]:
                if isinstance(o.stream, tuple):
                    o.sig_k = o.idx
                    continue
                per_stream.setdefault(o.stream, []).append(o)
        for sname, lst in per_stream.items():
            lst.sort(key=lambda o: o.idx)
            k = self.sig_count.get(sname, 0)
            for o in lst:
                if o.sig:
                    o.sig_k = k
                    k += 1
            self.sig_count[sname] = k

        with nc.Block() as block:
            eng_map = {"tensor": block.tensor, "vector": block.vector,
                       "scalar": block.scalar, "gpsimd": block.gpsimd,
                       "sync": block.sync}
            for ename, deco in eng_map.items():
                ops = self.pending[ename]
                if not ops:
                    continue

                def make_body(ops=ops, ename=ename):
                    def body(eng):
                        seen = self._seen_waits[ename]
                        for o in ops:
                            for p in o.wait_ops:
                                if p.sig_k is None:
                                    raise RuntimeError(
                                        f"wait on unsignaled {p.tag}")
                                key, val, _ = self._sig_value(p)
                                if seen.get(key, 0) >= val:
                                    continue
                                seen[key] = val
                                eng.wait_ge(self.sems[key], val)
                            if o.drain_before:
                                eng.drain()
                            if o.kind == "dma":
                                keyd, vald, _ = self._sig_value(o)
                                pred = vald - 16
                                if pred > 0 and seen.get(keyd, 0) < pred:
                                    seen[keyd] = pred
                                    eng.wait_ge(self.sems[keyd], pred)
                            if o.fn is None:
                                if o.sig:
                                    key, val, amt = self._sig_value(o)
                                    eng.nop().then_inc(self.sems[key], amt)
                                    o.emitted = True
                                continue
                            inst = o.fn(eng)
                            if o.sig:
                                key, val, amt = self._sig_value(o)
                                inst.then_inc(self.sems[key], amt)
                            o.emitted = True
                    return body
                deco(make_body())
        for eng in ALL_ENGINES:
            for o in self.pending[eng]:
                o.emitted = True
            self.pending[eng] = []


    def dma(self, eng, out, in_, tag=""):
        oap, owr = (out.ap, [out]) if isinstance(out, APR) else out
        iap, ird = (in_.ap, [in_]) if isinstance(in_, APR) else in_

        def fn(e):
            return e.dma_start(out=oap, in_=iap)
        o = self.op(eng, fn, reads=ird, writes=owr, dma=True, tag=tag)
        o.sig = True
        return o

    def mm(self, out, lhsT, rhs, start, stop, group, tag=""):
        def fn(e):
            return e.matmul(out.ap, lhsT.ap, rhs.ap, start=start, stop=stop)
        return self.op("tensor", fn, reads=[lhsT, rhs], writes=[out],
                       accum_group=group, tag=tag)

    def transpose(self, out, in_, ident, tag=""):
        def fn(e):
            return e.transpose(out.ap, in_.ap, ident.ap)
        return self.op("tensor", fn, reads=[in_, ident], writes=[out], tag=tag)

    def act(self, out, in_, func, bias=0.0, scale=1.0, accum_out=None, tag=""):
        reads, writes = [in_], [out]
        b_, s_ = bias, scale
        if isinstance(bias, APR):
            reads.append(bias)
            b_ = bias.ap
        if isinstance(scale, APR):
            reads.append(scale)
            s_ = scale.ap
        ao = None
        if accum_out is not None:
            writes.append(accum_out)
            ao = accum_out.ap

        def fn(e):
            return e.activation(out=out.ap, in_=in_.ap, func=func, bias=b_,
                                scale=s_, accum_out=ao)
        return self.op("scalar", fn, reads=reads, writes=writes, tag=tag)

    def copy(self, out, in_, eng="vector", tag=""):
        if eng == "scalar":
            def fn(e):
                return e.copy(out=out.ap, in_=in_.ap)
        else:
            def fn(e):
                return e.tensor_copy(out=out.ap, in_=in_.ap)
        return self.op(eng, fn, reads=[in_], writes=[out], tag=tag)

    def memset(self, apr, val, eng="vector", tag=""):
        def fn(e):
            return e.memset(apr.ap, val)
        return self.op(eng, fn, reads=[], writes=[apr], tag=tag)

    def ts(self, out, in0, s1, s2=None, op0=ALU.mult, op1=ALU.bypass,
           eng="vector", tag="", accum_out=None):
        reads, writes = [in0], [out]
        a1, a2 = s1, s2
        if isinstance(s1, APR):
            reads.append(s1)
            a1 = s1.ap
        if isinstance(s2, APR):
            reads.append(s2)
            a2 = s2.ap
        ao = None
        if accum_out is not None:
            writes.append(accum_out)
            ao = accum_out.ap
        single = s2 is None and ao is None

        def fn(e):
            if single:
                return e.tensor_single_scalar(out.ap, in0.ap, a1, op0)
            return e.tensor_scalar(out.ap, in0.ap, a1, a2, op0, op1,
                                   accum_out=ao)
        return self.op(eng, fn, reads=reads, writes=writes, tag=tag)

    def tt(self, out, in0, in1, alu, eng="vector", tag=""):
        def fn(e):
            return e.tensor_tensor(out.ap, in0.ap, in1.ap, alu)
        return self.op(eng, fn, reads=[in0, in1], writes=[out], tag=tag)

    def red(self, out, in_, alu=ALU.max, absval=None, negate=None, tag=""):
        def fn(e):
            return e.tensor_reduce(out.ap, in_.ap, AX.X, alu,
                                   apply_absolute_value=absval, negate=negate)
        return self.op("vector", fn, reads=[in_], writes=[out], tag=tag)

    def scan(self, out, d0, d1, initial, op0, op1, tag=""):
        reads = [d0, d1]
        ini = initial
        if isinstance(initial, APR):
            reads.append(initial)
            ini = initial.ap

        def fn(e):
            return e.tensor_tensor_scan(out.ap, d0.ap, d1.ap, ini, op0, op1)
        return self.op("vector", fn, reads=reads, writes=[out], tag=tag)

    def recip(self, out, in_, tag=""):
        def fn(e):
            return e.reciprocal(out.ap, in_.ap)
        return self.op("vector", fn, reads=[in_], writes=[out], tag=tag)

    def stt(self, out, in0, scalar, in1, op0, op1, eng="vector", tag=""):
        reads = [in0, in1]
        sc = scalar
        if isinstance(scalar, APR):
            reads.append(scalar)
            sc = scalar.ap

        def fn(e):
            return e.scalar_tensor_tensor(out.ap, in0.ap, sc, in1.ap, op0, op1)
        return self.op(eng, fn, reads=reads, writes=[out], tag=tag)

    def collective(self, kind, alu, replica_groups, ins, outs, tag="cc"):
        iaps = [x.ap for x in ins]
        oaps = [x.ap for x in outs]

        def fn(e):
            return e.collective_compute(kind, alu, replica_groups=replica_groups,
                                        ins=iaps, outs=oaps)
        o = self.op("gpsimd", fn, reads=list(ins), writes=list(outs), tag=tag)
        o.async_ = True
        return o



def _hadamard(n):
    h = np.array([[1.0]], dtype=np.float32)
    while h.shape[0] < n:
        h = np.block([[h, h], [h, -h]])
    return h


def stage_consts():
    ident = np.eye(128, dtype=np.float32).astype(BF16NP)
    h128 = _hadamard(128).astype(BF16NP)
    g64 = np.kron(np.eye(2, dtype=np.float32), _hadamard(64)).astype(BF16NP)
    return ident, h128, g64


def stage_mask(role):
    m = np.zeros((NOWN, 128, MASKW * 128), dtype=np.float32)
    qrow = np.arange(128)[:, None]
    col = np.arange(MASKW * 128)[None, :]
    for j in range(NOWN):
        st = 7 - j
        gt = OWN_TILES[role][st]
        q = gt * 128 + qrow
        k = (EXT[j] - MASKW) * 128 + col
        m[j] = np.where(k <= q, 0.0, NEG)
    return m


def stage_weights(inp):
    f32 = np.float32
    w = {}

    def tile_cols(a, width):
        return np.ascontiguousarray(
            a.reshape(a.shape[0], a.shape[1] // width, width).transpose(1, 0, 2))
    w["wqT"] = tile_cols(inp["L0_attn_wq"].T, 256)
    w["wkT"] = tile_cols(inp["L0_attn_wk"].T, 256)
    w["wvT"] = tile_cols(inp["L0_attn_wv"].T, 256)
    w["wpT"] = tile_cols(inp["L0_attn_wp"].T, 512)
    w["cfcT"] = np.ascontiguousarray(
        inp["L0_ffn_c_fc"].reshape(D, 32, 256).transpose(1, 0, 2))
    w["cprojT"] = np.ascontiguousarray(inp["L0_ffn_c_proj"].T)
    w["wqkvT"] = np.ascontiguousarray(
        inp["L1_attn_w_qkv"].T.reshape(D, 12, 512).transpose(1, 0, 2))
    w["wswT"] = np.ascontiguousarray(
        inp["L1_attn_w_swiglu"].T.reshape(D, 32, 512).transpose(1, 0, 2))
    w["woutT"] = np.ascontiguousarray(inp["L1_attn_w_out"].T)
    w["W1T"] = np.ascontiguousarray(
        inp["L1_ffn_W1"].T.reshape(D, 16, 512).transpose(1, 0, 2))
    w["W3T"] = np.ascontiguousarray(
        inp["L1_ffn_W3"].T.reshape(D, 16, 512).transpose(1, 0, 2))
    j = np.arange(128)
    c = np.arange(64)
    Hcj = ((j[None, :] % 64) * 128 + 2 * c[:, None] + (j[None, :] // 64))
    w2t = inp["L1_ffn_W2"].T
    w2r = w2t[Hcj.reshape(-1)]
    w["W2relay"] = np.ascontiguousarray(
        w2r.reshape(64, 128, 4, 512).transpose(2, 0, 1, 3))
    return {k: np.ascontiguousarray(v.astype(f32)) for k, v in w.items()}


def stage_inputs(inputs):
    ident, h128, g64 = stage_consts()
    wts = stage_weights(inputs)
    x = np.asarray(inputs["L0_attn_x"], dtype=np.float32)
    in_maps = []
    for core in range(8):
        b, role = core // 2, core % 2
        x_all = np.ascontiguousarray(x[b])
        own = OWN_TILES[role]
        x_own = np.ascontiguousarray(
            x[b].reshape(NT, 128, D)[own].reshape(TOK, D))
        cr = {0: [0.0, 0.0, 0.0, 1.0], 1: [1.0, 0.0, 1.0, 0.0]}[role]
        CR = np.tile(np.array(cr, dtype=np.float32), (128, 1))
        sel0 = np.full((128, 1), 1.0 if role == 1 else 0.0, np.float32)
        sel1 = np.full((128, 1), 1.0 if role == 0 else 0.0, np.float32)
        m = {"x_own": x_own, "x_all": x_all,
             "mask": stage_mask(role), "CR": CR, "sel0": sel0, "sel1": sel1,
             "ident": ident, "H128": h128, "G64": g64}
        m.update(wts)
        in_maps.append(m)
    return in_maps


def unstage_output(results):
    out = np.zeros((B, S, D), dtype=np.float32)
    for core in range(8):
        b, role = core // 2, core % 2
        y = results[core]["y_own"].reshape(NOWN, 128, D)
        for st, gt in enumerate(OWN_TILES[role]):
            out[b, gt * 128:(gt + 1) * 128] = y[st]
    return out



class Env:

    def __init__(self, nc):
        self.nc = nc
        self.dram = {}
        self.sb = {}

    def din(self, name, shape, dtype):
        t = Tsr(self.nc.dram_tensor(name, list(shape), dtype,
                                    kind="ExternalInput").ap(), name, "dram")
        self.dram[name] = t
        return t

    def dout(self, name, shape, dtype):
        t = Tsr(self.nc.dram_tensor(name, list(shape), dtype,
                                    kind="ExternalOutput").ap(), name, "dram")
        self.dram[name] = t
        return t

    def dint(self, name, shape, dtype):
        t = Tsr(self.nc.dram_tensor(name, list(shape), dtype,
                                    kind="Internal").ap(), name, "dram")
        self.dram[name] = t
        return t


_ALLOC_N = [0]


def alloc_sb(env, stack, name, shape, dtype):
    _ALLOC_N[0] += 1
    name = f"{name}_{_ALLOC_N[0]}"
    h = stack.enter_context(env.nc.sbuf_tensor(name, list(shape), dtype))
    return Tsr(h, name, "sbuf")


def alloc_ps(env, stack, name, shape, dtype):
    _ALLOC_N[0] += 1
    name = f"{name}_{_ALLOC_N[0]}"
    h = stack.enter_context(env.nc.psum_tensor(name, list(shape), dtype))
    return Tsr(h, name, "psum")


SUM, SS, MEAN, T1, M2, VE, RSTD, MX, MXC, RECM, INVS, SC1, SC2, SC3 = range(14)


def emit_ln_stats(b, stt, x_apr, sq_scratch, eps, n_inv):
    c = lambda i: stt[:, i:i + 1]
    b.red(c(SUM), x_apr, alu=ALU.add, tag="ln_sum")
    b.act(sq_scratch, x_apr, AF.Square, accum_out=c(SS), tag="ln_sq")
    b.ts(c(MEAN), c(SUM), n_inv, tag="ln_mean")
    b.ts(c(T1), c(SS), n_inv, tag="ln_ss")
    b.tt(c(M2), c(MEAN), c(MEAN), ALU.mult, tag="ln_m2")
    b.stt(c(VE), c(T1), eps, c(M2), ALU.add, ALU.subtract, tag="ln_var")
    b.act(c(RSTD), c(VE), AF.Ln, tag="ln_ln")
    b.act(c(RSTD), c(RSTD), AF.Exp, scale=-0.5, tag="ln_rstd")


def emit_rsqrt(b, out_apr, in_apr):
    b.act(out_apr, in_apr, AF.Ln, tag="rsq_ln")
    b.act(out_apr, out_apr, AF.Exp, scale=-0.5, tag="rsq_ex")


def emit_quant_x(b, stt, xn_apr, nq_i32, xq_out):
    c = lambda i: stt[:, i:i + 1]
    b.red(c(MX), xn_apr, alu=ALU.max, absval=True, tag="qx_mx")
    b.ts(c(MXC), c(MX), 1e-5, op0=ALU.max, tag="qx_clip")
    b.recip(c(RECM), c(MXC), tag="qx_rec")
    b.ts(c(INVS), c(MXC), 1.0 / 127.0, tag="qx_invs")
    b.ts(nq_i32, xn_apr, c(RECM), 127.0, op0=ALU.mult, op1=ALU.mult,
         tag="qx_scale1")
    b.ts(nq_i32, nq_i32, MAGIC, MAGIC, op0=ALU.add, op1=ALU.subtract,
         tag="qx_round")
    b.ts(xq_out, nq_i32, c(INVS), tag="qx_scale")



def phase_l0_stats(b, env, stack, weights, wcols=4096):
    nc = env.nc
    AMS, RAMS = env.sb["AMS"], env.sb["RAMS"]
    ones_c, ones_r = env.sb["ones_c"], env.sb["ones_r"]
    wbuf = alloc_sb(env, stack, "st_wbuf", [128, 2 * wcols], F32)
    acc = alloc_sb(env, stack, "st_acc", [128, 192], F32)
    red1 = alloc_sb(env, stack, "st_red", [128, 8], F32)
    am1 = alloc_sb(env, stack, "st_am", [1, 16], F32)
    psum = alloc_ps(env, stack, "st_ps", [128, 512], F32)

    for key, dname, col, numel, kind, eps in weights:
        dr = env.dram[dname]
        shp = dr.h.shape
        views = []
        if len(shp) == 3:
            nsl, rows, cols = shp
            per = rows * cols // 128
            for i in range(nsl):
                full = dr.h[i].rearrange("(p a) c -> p (a c)", p=128)
                for k in range(0, per, wcols):
                    w_ = min(wcols, per - k)
                    views.append(raw_apr(dr, full[:, k:k + w_], (0, 128),
                                         (i * per + k, i * per + k + w_)))
        else:
            rows, cols = shp
            for i in range(rows // 128):
                for k in range(0, cols, wcols):
                    w_ = min(wcols, cols - k)
                    views.append(raw_apr(
                        dr, dr.h[i * 128:(i + 1) * 128, k:k + w_], (0, 128),
                        (k, k + w_)))
        nacc = 0
        half = 0
        for i, v in enumerate(views):
            ncols = int(np.prod(v.ap.shape[1:]))
            assert ncols <= wcols
            dst = raw_apr(wbuf, wbuf.h[:, half * wcols: half * wcols + ncols],
                          (0, 128), (half * wcols, half * wcols + ncols))
            b.dma("sync", dst, v, tag=f"st_ld_{key}_{i}")
            b.red(acc[:, nacc:nacc + 1], dst, alu=ALU.add, absval=True,
                  tag=f"st_red_{key}_{i}")
            nacc += 1
            half ^= 1
        b.red(red1[:, 0:1], acc[:, 0:nacc], alu=ALU.add, tag=f"st_r2_{key}")
        b.mm(psum[0:1, 0:1], red1[:, 0:1], ones_c[:, 0:1],
             start=True, stop=True, group=None, tag=f"st_mm_{key}")
        b.copy(am1[0:1, col:col + 1], psum[0:1, 0:1], tag=f"st_cp_{key}")
        b.ts(am1[0:1, col:col + 1], am1[0:1, col:col + 1], 1.0 / numel,
             tag=f"st_am_{key}")
        if kind == "wq":
            b.ts(am1[0:1, col + 8:col + 9], am1[0:1, col:col + 1], eps,
                 op0=ALU.add, tag=f"st_ame_{key}")
        else:
            b.ts(am1[0:1, col:col + 1], am1[0:1, col:col + 1], eps,
                 op0=ALU.max, tag=f"st_mx_{key}")
            b.copy(am1[0:1, col + 8:col + 9], am1[0:1, col:col + 1],
                   tag=f"st_cp2_{key}")
        b.recip(am1[0:1, col + 8:col + 9], am1[0:1, col + 8:col + 9],
                tag=f"st_rec_{key}")
        b.mm(psum[0:128, col:col + 1], ones_r[0:1, :],
             am1[0:1, col:col + 1], start=True, stop=True, group=None,
             tag=f"st_bc_{key}")
        b.copy(AMS[:, col:col + 1], psum[0:128, col:col + 1],
               eng="scalar", tag=f"st_bc2_{key}")
        b.mm(psum[0:128, col + 8:col + 9], ones_r[0:1, :],
             am1[0:1, col + 8:col + 9], start=True, stop=True, group=None,
             tag=f"st_bc3_{key}")
        b.copy(RAMS[:, col:col + 1], psum[0:128, col + 8:col + 9],
               eng="scalar", tag=f"st_bc4_{key}")


W_WQ, W_WK, W_WV, W_WP, W_CFC, W_CPROJ, W_QSC = range(7)

L0_STATS = [
    ("wq", "wqT", W_WQ, D * D, "wq", EPS_SUBLN),
    ("wk", "wkT", W_WK, D * D, "wq", EPS_SUBLN),
    ("wv", "wvT", W_WV, D * D, "wq", EPS_SUBLN),
    ("wp", "wpT", W_WP, D * D, "wq", EPS_SUBLN),
]
L0FFN_STATS = [
    ("cfc", "cfcT", W_CFC, D * DFF, "ternary", EPS_SUBLN),
    ("cproj", "cprojT", W_CPROJ, D * DFF, "ternary", EPS_SUBLN),
]


def _xqT_rhs(xqT, tg, c, ntile=16):
    v = xqT.h.rearrange("p (t c k) -> p t c k", t=ntile, c=16, k=128)
    ap = v[:, tg * 4:(tg + 1) * 4, c, :]
    return raw_apr(xqT, ap, (0, 128),
                   (tg * 4 * 2048, (tg * 4 + 4) * 2048))


def phase_l0_pre(b, env, stack):
    nc = env.nc
    ident = env.sb["ident"]
    xq_allT = env.sb["xq_allT"]
    x_all, x_own = env.dram["x_all"], env.dram["x_own"]
    xq_ownT = env.dram["xq_ownT"]

    xin = [alloc_sb(env, stack, f"pre_x{i}", [128, D], F32) for i in range(2)]
    xn = [alloc_sb(env, stack, f"pre_xn{i}", [128, D], F32) for i in range(2)]
    nq = [alloc_sb(env, stack, f"pre_nq{i}", [128, D], F32) for i in range(2)]
    xq = [alloc_sb(env, stack, f"pre_xq{i}", [128, D], BF16) for i in range(2)]
    stg = [alloc_sb(env, stack, f"pre_st{i}", [128, D], BF16) for i in range(2)]
    sq = alloc_sb(env, stack, "pre_sq", [128, D], F32)
    stt = [alloc_sb(env, stack, f"pre_tt{i}", [128, 16], F32) for i in range(2)]
    pst = [alloc_ps(env, stack, f"pre_ps{i}", [128, 1024], BF16)
           for i in range(2)]

    work = [("all", g) for g in range(NT)] + [("own", t) for t in range(NOWN)]
    for i, (kind, g) in enumerate(work):
        r = i % 2
        src = x_all if kind == "all" else x_own
        b.dma("sync", xin[r].full,
              raw_apr(src, src.h[g * 128:(g + 1) * 128, :], (0, 128), (0, D)),
              tag=f"pre_ld{i}")
        emit_ln_stats(b, stt[r], xin[r].full, sq.full, EPS_SUBLN, 1.0 / D)
        c = lambda k: stt[r][:, k:k + 1]
        b.ts(xn[r].full, xin[r].full, c(MEAN), c(RSTD),
             op0=ALU.subtract, op1=ALU.mult, tag=f"pre_xn{i}")
        emit_quant_x(b, stt[r], xn[r].full, nq[r].full, xq[r].full)
        for half in range(2):
            for cc in range(8):
                ch = half * 8 + cc
                b.transpose(pst[half][:, cc * 128:(cc + 1) * 128],
                            xq[r][:, ch * 128:(ch + 1) * 128], ident.full,
                            tag=f"pre_tp{i}_{ch}")
            if kind == "all":
                dst = xq_allT[:, g * 2048 + half * 1024:
                              g * 2048 + (half + 1) * 1024]
                b.copy(dst, pst[half].full, eng="scalar", tag=f"pre_ev{i}_{half}")
            else:
                b.copy(stg[r][:, half * 1024:(half + 1) * 1024],
                       pst[half].full, eng="scalar", tag=f"pre_ev{i}_{half}")
        if kind == "own":
            b.dma("gpsimd",
                  raw_apr(xq_ownT, xq_ownT.h[g], (0, 128), (g * 2048, (g + 1) * 2048)),
                  stg[r].full, tag=f"pre_st{i}")


def phase_l0_attn_hg(b, env, stack, hg):
    nc = env.nc
    ident = env.sb["ident"]
    AMS, RAMS = env.sb["AMS"], env.sb["RAMS"]
    xq_allT = env.sb["xq_allT"]
    xq_ownT = env.dram["xq_ownT"]

    K_hg = alloc_sb(env, stack, "a_K", [128, 4 * 2048], BF16)
    V_hg = alloc_sb(env, stack, "a_V", [128, 16 * 512], BF16)
    Q_hg = alloc_sb(env, stack, "a_Q", [128, 4 * 1024], BF16)
    proj_stack = ExitStack()
    stack.enter_context(proj_stack)
    wsl_f = alloc_sb(env, proj_stack, "a_wf", [128, 16 * 256], F32)
    wsl_i = alloc_sb(env, proj_stack, "a_wi", [128, 8 * 256], F32)
    wsl_b = [alloc_sb(env, proj_stack, f"a_wb{i}", [128, 16 * 256], BF16)
             for i in range(2)]
    qin = [alloc_sb(env, proj_stack, f"a_qin{i}", [128, 4 * 2048], BF16)
           for i in range(2)]
    psA = [alloc_ps(env, proj_stack, f"a_psA{i}", [128, 512], F32)
           for i in range(2)]
    psV = [alloc_ps(env, proj_stack, f"a_psV{i}", [128, 512], F32)
           for i in range(2)]

    def load_quant_slab(wname, s, r, wcol, tag):
        dr = env.dram[wname]
        for h in range(2):
            v = raw_apr(dr,
                        dr.h[s].rearrange("(g p) w -> p g w", p=128)[
                            :, h * 8:(h + 1) * 8, :],
                        (0, 128), (h * 2048, (h + 1) * 2048))
            b.dma("sync",
                  wsl_f[:, h * 2048:(h + 1) * 2048], v, tag=f"{tag}_ld{h}")
            b.ts(wsl_i.full, wsl_f[:, h * 2048:(h + 1) * 2048],
                 RAMS[:, wcol:wcol + 1], op0=ALU.mult, tag=f"{tag}_scl{h}")
            b.ts(wsl_i.full, wsl_i.full, MAGIC, MAGIC, op0=ALU.add,
                 op1=ALU.subtract, tag=f"{tag}_rnd{h}")
            b.ts(wsl_b[r][:, h * 2048:(h + 1) * 2048], wsl_i.full, -1.0, 1.0,
                 op0=ALU.max, op1=ALU.min, tag=f"{tag}_clip{h}")

    gid = [0]

    def proj_form2(wname, wcol, scol, dst, dst_head_stride, rhs_tiles, rhs_src,
                   ntile, tagp):
        for si, s in enumerate((hg * 2, hg * 2 + 1)):
            r = si % 2
            load_quant_slab(wname, s, r, wcol, f"{tagp}{s}")
            for ob in range(2):
                hl = si * 2 + ob
                for tg in range(ntile // 4):
                    gid[0] += 1
                    g = (tagp, gid[0])
                    for c in range(16):
                        b.mm(psA[tg % 2][:, 0:512],
                             wsl_b[r][:, c * 256 + ob * 128:c * 256 + (ob + 1) * 128],
                             rhs_src(tg, c),
                             start=(c == 0), stop=(c == 15), group=g,
                             tag=f"{tagp}_mm{hl}_{tg}_{c}")
                    b.act(dst[:, hl * dst_head_stride + tg * 512:
                              hl * dst_head_stride + (tg + 1) * 512],
                          psA[tg % 2][:, 0:512], AF.Copy,
                          scale=AMS[:, scol:scol + 1], tag=f"{tagp}_ev{hl}_{tg}")

    proj_form2("wkT", W_WK, W_WK, K_hg, 2048,
               None, lambda tg, c: _xqT_rhs(xq_allT, tg, c), NT, "K")

    for si, s in enumerate((hg * 2, hg * 2 + 1)):
        r = si % 2
        load_quant_slab("wvT", s, r, W_WV, f"V{s}")
        for kt in range(NT):
            g = ("V", s, kt)
            for c in range(16):
                b.mm(psV[kt % 2][:, 0:256],
                     xq_allT[:, kt * 2048 + c * 128:kt * 2048 + (c + 1) * 128],
                     wsl_b[r][:, c * 256:(c + 1) * 256],
                     start=(c == 0), stop=(c == 15), group=g,
                     tag=f"V_mm{s}_{kt}_{c}")
            b.act(V_hg[:, kt * 512 + si * 256:kt * 512 + (si + 1) * 256],
                  psV[kt % 2][:, 0:256], AF.Copy,
                  scale=AMS[:, W_WV:W_WV + 1], tag=f"V_ev{s}_{kt}")

    for tg in range(2):
        v = xq_ownT.h.rearrange("t p c -> p t c", t=8)[:, tg * 4:(tg + 1) * 4, :]
        b.dma("gpsimd", qin[tg].full,
              raw_apr(xq_ownT, v, (0, 128), (tg * 4 * 2048, (tg + 1) * 4 * 2048)),
              tag=f"Q_ld{tg}")

    def q_rhs(tg, c):
        v = qin[tg].h.rearrange("p (t c k) -> p t c k", t=4, c=16, k=128)
        return raw_apr(qin[tg], v[:, :, c, :], (0, 128), (0, 4 * 2048))

    proj_form2("wqT", W_WQ, W_QSC, Q_hg, 1024,
               None, q_rhs, NOWN, "Q")

    dump = env.sb.get("_dumpKVQ")
    if dump:
        dbuf = alloc_sb(env, proj_stack, "dkv", [128, 2048], F32)
        for nm, src_t, n in (("d_K", K_hg, 4 * 2048), ("d_V", V_hg, 16 * 512),
                             ("d_Q", Q_hg, 4 * 1024)):
            for o in range(0, n, 2048):
                w_ = min(2048, n - o)
                b.copy(raw_apr(dbuf, dbuf.h[:, 0:w_], (0, 128), (0, w_)),
                       raw_apr(src_t, src_t.h[:, o:o + w_], (0, 128), (o, o + w_)))
                b.dma("gpsimd",
                      raw_apr(dump[nm], dump[nm].h[:, o:o + w_], (0, 128),
                              (o, o + w_)),
                      raw_apr(dbuf, dbuf.h[:, 0:w_], (0, 128), (0, w_)),
                      tag=f"dkv_{nm}_{o}")
        env.sb["_dumpKVQ"] = None

    b.barrier()
    proj_stack.close()
    maskrot = [alloc_sb(env, stack, f"b_msk{i}", [128, 640], F32)
               for i in range(2)]
    psS = alloc_ps(env, stack, "b_psS", [128, 2048], F32)
    psPT = [alloc_ps(env, stack, f"b_psPT{i}", [128, 1024], BF16)
            for i in range(2)]
    psAV = [alloc_ps(env, stack, f"b_psAV{i}", [128, 512], F32)
            for i in range(2)]
    P = [alloc_sb(env, stack, f"b_P{i}", [128, 2048], BF16) for i in range(2)]
    PT = [alloc_sb(env, stack, f"b_PT{i}", [128, 2048], BF16) for i in range(2)]
    st2 = [alloc_sb(env, stack, f"b_st{i}", [128, 4], F32) for i in range(2)]

    mask_d = env.dram["mask"]
    aoc = alloc_sb(env, stack, "b_aoc", [128, 512], BF16)
    attn_out_d = env.dram["attn_out_d"]
    for j in range(NOWN):
        stt_ = 7 - j
        E = EXT[j]
        b.dma("sync", maskrot[j % 2].full,
              raw_apr(mask_d, mask_d.h[j], (0, 128), (0, 640)),
              tag=f"msk_ld{j}")
        for hl in range(4):
            r = (j * 4 + hl) % 2
            gS = ("S", j, hl)
            for c0 in range(0, E * 128, 512):
                c1 = min(E * 128, c0 + 512)
                b.mm(psS[:, c0:c1],
                     Q_hg[:, hl * 1024 + stt_ * 128:hl * 1024 + (stt_ + 1) * 128],
                     K_hg[:, hl * 2048 + c0:hl * 2048 + c1],
                     start=True, stop=True, group=gS, tag=f"S_mm{j}_{hl}_{c0}")
            b.tt(psS[:, (E - MASKW) * 128:E * 128],
                 psS[:, (E - MASKW) * 128:E * 128],
                 maskrot[j % 2].full, ALU.add, tag=f"S_mask{j}_{hl}")
            b.act(P[r][:, 0:E * 128], psS[:, 0:E * 128], AF.Exp,
                  bias=env.sb["neg8"][:, 0:1], accum_out=st2[r][:, 1:2],
                  tag=f"S_exp{j}_{hl}")
            b.recip(st2[r][:, 2:3], st2[r][:, 1:2], tag=f"S_rec{j}_{hl}")
            for kb in range(E):
                half = (kb // 8) % 2
                b.transpose(psPT[half][:, (kb % 8) * 128:(kb % 8 + 1) * 128],
                            P[r][:, kb * 128:(kb + 1) * 128], ident.full,
                            tag=f"PT{j}_{hl}_{kb}")
                if kb % 8 == 7 or kb == E - 1:
                    lo = (kb // 8) * 8
                    b.copy(PT[r][:, lo * 128:(kb + 1) * 128],
                           psPT[half][:, 0:(kb + 1 - lo) * 128],
                           eng="vector", tag=f"PT_ev{j}_{hl}_{half}")
            gAV = ("AV", j, hl)
            for kb in range(E):
                b.mm(psAV[r][:, 0:128],
                     PT[r][:, kb * 128:(kb + 1) * 128],
                     V_hg[:, kb * 512 + hl * 128:kb * 512 + (hl + 1) * 128],
                     start=(kb == 0), stop=(kb == E - 1), group=gAV,
                     tag=f"AV{j}_{hl}_{kb}")
            b.act(aoc[:, hl * 128:(hl + 1) * 128],
                  psAV[r][:, 0:128], AF.Copy, scale=st2[r][:, 2:3],
                  tag=f"AV_ev{j}_{hl}")
        b.dma("gpsimd",
              raw_apr(attn_out_d,
                      attn_out_d.h[stt_][:, hg * 512:(hg + 1) * 512],
                      (0, 128), (stt_ * 2048 + hg * 512,
                                 stt_ * 2048 + (hg + 1) * 512)),
              aoc.full, tag=f"AV_st{j}")


def phase_l0_wp(b, env, stack):
    nc = env.nc
    ident = env.sb["ident"]
    AMS, RAMS = env.sb["AMS"], env.sb["RAMS"]
    attn_out_d = env.dram["attn_out_d"]
    xstream = env.sb["xstream"]
    x_own = env.dram["x_own"]

    wsl_f = alloc_sb(env, stack, "c_wf", [128, 2 * 2048], F32)
    wsl_i = alloc_sb(env, stack, "c_wi", [128, 2048], F32)
    wpq = alloc_sb(env, stack, "c_wpq", [128, 16 * 2048], BF16)
    dr = env.dram["wpT"]
    for s in range(4):
        for h in range(4):
            r = h % 2
            v = raw_apr(dr, dr.h[s].rearrange("(c p) w -> p c w", p=128)[
                :, h * 4:(h + 1) * 4, :],
                (0, 128), (s * 8192 + h * 2048, s * 8192 + (h + 1) * 2048))
            b.dma("sync", wsl_f[:, r * 2048:(r + 1) * 2048], v,
                  tag=f"wp_ld{s}_{h}")
            b.ts(wsl_i.full, wsl_f[:, r * 2048:(r + 1) * 2048],
                 RAMS[:, W_WP:W_WP + 1], op0=ALU.mult, tag=f"wp_scl{s}_{h}")
            b.ts(wsl_i.full, wsl_i.full, MAGIC, MAGIC, op0=ALU.add,
                 op1=ALU.subtract, tag=f"wp_rnd{s}_{h}")
            wv = wpq.h.rearrange("p (c o) -> p c o", c=16)[
                :, h * 4:(h + 1) * 4, s * 512:(s + 1) * 512]
            b.ts(raw_apr(wpq, wv, (0, 128), (0, 16 * 2048)),
                 wsl_i.full, -1.0, 1.0, op0=ALU.max, op1=ALU.min,
                 tag=f"wp_clip{s}_{h}")

    dump = env.sb.get("_dumpWP")
    if dump:
        dbuf = alloc_sb(env, stack, "dwp", [128, 2048], F32)
        b.copy(dbuf[:, 0:16], AMS.full)
        b.copy(dbuf[:, 16:32], RAMS.full)
        b.dma("gpsimd", dump["d_ams"].full, dbuf[:, 0:32], tag="dams")
        for o in range(0, 16 * 2048, 2048):
            b.copy(dbuf.full,
                   raw_apr(wpq, wpq.h[:, o:o + 2048], (0, 128), (o, o + 2048)))
            b.dma("gpsimd",
                  raw_apr(dump["d_wpq"], dump["d_wpq"].h[:, o:o + 2048],
                          (0, 128), (o, o + 2048)), dbuf.full, tag=f"dwpq{o}")
        env.sb["_dumpWP"] = None

    xin = [alloc_sb(env, stack, f"c_x{i}", [128, D], F32) for i in range(2)]
    xn = alloc_sb(env, stack, "c_xn", [128, D], F32)
    nq = alloc_sb(env, stack, "c_nq", [128, D], F32)
    aoq = alloc_sb(env, stack, "c_aoq", [128, D], BF16)
    aoqT = [alloc_sb(env, stack, f"c_aoqT{i}", [128, D], BF16) for i in range(2)]
    ow = [alloc_sb(env, stack, f"c_ow{i}", [128, D], F32) for i in range(2)]
    sq = alloc_sb(env, stack, "c_sq", [128, D], F32)
    stt = [alloc_sb(env, stack, f"c_tt{i}", [128, 16], F32) for i in range(2)]
    pst = alloc_ps(env, stack, "c_pst", [128, 1024], BF16)
    psW = [alloc_ps(env, stack, f"c_psW{i}", [128, 512], F32) for i in range(4)]

    aob = [alloc_sb(env, stack, f"c_ao{i}", [128, D], BF16) for i in range(2)]
    for t in range(NOWN):
        r = t % 2
        b.dma("gpsimd", aob[r].full,
              raw_apr(attn_out_d, attn_out_d.h[t], (0, 128), (0, D)),
              tag=f"wp_aold{t}")
        ao = aob[r].full
        emit_ln_stats(b, stt[r], ao, sq.full, EPS_SUBLN, 1.0 / D)
        c = lambda k: stt[r][:, k:k + 1]
        b.ts(xn.full, ao, c(MEAN), c(RSTD), op0=ALU.subtract, op1=ALU.mult,
             tag=f"wp_xn{t}")
        emit_quant_x(b, stt[r], xn.full, nq.full, aoq.full)
        for half in range(2):
            for cc in range(8):
                ch = half * 8 + cc
                b.transpose(pst[:, cc * 128:(cc + 1) * 128],
                            aoq[:, ch * 128:(ch + 1) * 128], ident.full,
                            tag=f"wp_tp{t}_{ch}")
            b.copy(aoqT[r][:, half * 1024:(half + 1) * 1024], pst.full,
                   eng="vector", tag=f"wp_tpe{t}_{half}")
        b.dma("sync", xin[r].full,
              raw_apr(x_own, x_own.h[t * 128:(t + 1) * 128, :], (0, 128),
                      (0, D)), tag=f"wp_x0{t}")
        for ds in range(4):
            g = ("WP", t, ds)
            for ch in range(16):
                b.mm(psW[ds][:, 0:512],
                     aoqT[r][:, ch * 128:(ch + 1) * 128],
                     wpq[:, ch * 2048 + ds * 512:ch * 2048 + (ds + 1) * 512],
                     start=(ch == 0), stop=(ch == 15), group=g,
                     tag=f"wp_mm{t}_{ds}_{ch}")
            b.act(ow[r][:, ds * 512:(ds + 1) * 512], psW[ds][:, 0:512],
                  AF.Copy, scale=AMS[:, W_WP:W_WP + 1], tag=f"wp_ev{t}_{ds}")
        b.tt(ow[r].full, ow[r].full, xin[r].full, ALU.add, tag=f"wp_res{t}")
        if env.sb.get("_dumpOW") is not None:
            dmp = env.sb["_dumpOW"]
            b.dma("gpsimd",
                  raw_apr(dmp, dmp.h[:, t * 2048:(t + 1) * 2048], (0, 128),
                          (t * 2048, (t + 1) * 2048)), ow[r].full,
                  tag=f"dow{t}")
        emit_ln_stats(b, stt[r], ow[r].full, sq.full, EPS_LN, 1.0 / D)
        b.ts(xstream[:, t * 2048:(t + 1) * 2048], ow[r].full, c(MEAN), c(RSTD),
             op0=ALU.subtract, op1=ALU.mult, tag=f"wp_str{t}")



from contextlib import ExitStack

DRAM_INPUTS = [
    ("x_own", [TOK, D], F32),
    ("x_all", [S, D], F32),
    ("mask", [NOWN, 128, MASKW * 128], F32),
    ("CR", [128, 4], F32),
    ("sel0", [128, 1], F32),
    ("sel1", [128, 1], F32),
    ("ident", [128, 128], BF16),
    ("H128", [128, 128], BF16),
    ("G64", [128, 128], BF16),
    ("wqT", [8, D, 256], F32),
    ("wkT", [8, D, 256], F32),
    ("wvT", [8, D, 256], F32),
    ("wpT", [4, D, 512], F32),
    ("cfcT", [32, D, 256], F32),
    ("cprojT", [DFF, D], F32),
    ("wqkvT", [12, D, 512], F32),
    ("wswT", [32, D, 512], F32),
    ("woutT", [DFF, D], F32),
    ("W1T", [16, D, 512], F32),
    ("W3T", [16, D, 512], F32),
    ("W2relay", [4, 64, 128, 512], F32),
]


def build_program(upto="l1ffn", stream_out=False):
    nc = bass.Bass("TRN2", target_bir_lowering=False, debug=False)
    env = Env(nc)
    for name, shape, dt in DRAM_INPUTS:
        env.din(name, shape, dt)
    env.dint("xq_ownT", [NOWN, 128, 2048], BF16)
    env.dint("attn_out_d", [NOWN, 128, 2048], BF16)
    env.dint("x2nT_d", [NOWN, 128, 2048], BF16)
    env.dint("h1_d", [64, 128, 1024], BF16)
    env.dint("kvc_d", [16, 128, 1024], BF16)
    env.dint("wc_d", [16, 128, 1024], BF16)
    env.dint("y_d", [16, 128, 1024], BF16)
    env.dint("cfcq_d", [32, 128, 4096], BF16)
    env.dint("h_d", [NOWN, 128, DFF], BF16)
    env.dint("cpjq_d", [64, 128, 2048], BF16)
    env.dint("cc_src", [128, 64], F32)
    env.dint("cc_dst", [256, 64], F32)
    if stream_out:
        env.dout("xs_out", [128, NOWN * 2048], F32)
    else:
        env.dout("y_own", [TOK, D], F32)

    b = Builder(nc)
    with ExitStack() as top:
        env.sb["ident"] = alloc_sb(env, top, "k_ident", [128, 128], BF16)
        env.sb["ones_c"] = alloc_sb(env, top, "k_onec", [128, 1], F32)
        env.sb["ones_r"] = alloc_sb(env, top, "k_oner", [1, 128], F32)
        env.sb["AMS"] = alloc_sb(env, top, "k_AMS", [128, 16], F32)
        env.sb["RAMS"] = alloc_sb(env, top, "k_RAMS", [128, 16], F32)
        env.sb["neg8"] = alloc_sb(env, top, "k_neg8", [128, 1], F32)
        b.make_sems(top)

        b.dma("sync", env.sb["ident"].full, env.dram["ident"].full,
              tag="ld_ident")
        b.memset(env.sb["ones_c"].full, 1.0)
        b.memset(env.sb["ones_r"].full, 1.0)
        b.memset(env.sb["neg8"].full, -8.0)

        with ExitStack() as attn_scope:
            env.sb["xq_allT"] = alloc_sb(env, attn_scope, "k_xqT",
                                         [128, NT * 2048], BF16)
            with ExitStack() as ph:
                phase_l0_stats(b, env, ph, L0_STATS)
                b.ts(env.sb["AMS"][:, W_QSC:W_QSC + 1],
                     env.sb["AMS"][:, W_WQ:W_WQ + 1], RSQRT_HD, tag="qsc")
                b.barrier()
                b.flush()
            with ExitStack() as ph:
                phase_l0_pre(b, env, ph)
                b.barrier()
                b.flush()
            for hg in range(4):
                with ExitStack() as ph:
                    phase_l0_attn_hg(b, env, ph, hg)
                    b.barrier()
                    b.flush()
        env.sb["xstream"] = alloc_sb(env, top, "k_xs", [128, NOWN * 2048], F16)
        with ExitStack() as ph:
            phase_l0_wp(b, env, ph)
            b.barrier()
            b.flush()

        if upto == "l0attn":
            with ExitStack() as ph:
                _emit_stream_dump(b, env, ph, stream_out)
                b.barrier()
                b.flush()
            return nc

        with ExitStack() as ph:
            phase_l0_stats(b, env, ph, L0FFN_STATS)
            b.barrier()
            b.flush()
        with ExitStack() as ph:
            phase_l0_ffn(b, env, ph)
            b.barrier()
            b.flush()
        if upto == "l0ffn":
            with ExitStack() as ph:
                _emit_stream_dump(b, env, ph, stream_out)
                b.barrier()
                b.flush()
            return nc

        with ExitStack() as ph:
            phase_l1_attn(b, env, ph)
            b.barrier()
            b.flush()
        if upto == "l1attn":
            with ExitStack() as ph:
                _emit_stream_dump(b, env, ph, stream_out)
                b.barrier()
                b.flush()
            return nc

        with ExitStack() as ph:
            phase_l1_ffn(b, env, ph)
            b.barrier()
            b.flush()
    return nc


def _emit_stream_dump(b, env, stack, stream_out):
    xs = env.sb["xstream"]
    buf = alloc_sb(env, stack, "dump", [128, 2048], F32)
    out = env.dram["xs_out"]
    for t in range(NOWN):
        b.copy(buf.full, xs[:, t * 2048:(t + 1) * 2048], tag=f"dump_c{t}")
        b.dma("gpsimd", raw_apr(out, out.h[:, t * 2048:(t + 1) * 2048],
                                (0, 128), (t * 2048, (t + 1) * 2048)),
              buf.full, tag=f"dump_d{t}")


def build_debug_l0(dumps):
    nc = bass.Bass("TRN2", target_bir_lowering=False, debug=False)
    env = Env(nc)
    for name, shape, dt in DRAM_INPUTS:
        env.din(name, shape, dt)
    env.dint("xq_ownT", [NOWN, 128, 2048], BF16)
    env.dint("attn_out_d", [NOWN, 128, 2048], BF16)
    outs = {}
    if "xq" in dumps:
        outs["d_xq"] = env.dout("d_xq", [128, NT * 2048], F32)
        outs["d_xqo"] = env.dout("d_xqo", [128, NOWN * 2048], F32)
    if "K" in dumps:
        outs["d_K"] = env.dout("d_K", [128, 4 * 2048], F32)
        outs["d_V"] = env.dout("d_V", [128, 16 * 512], F32)
        outs["d_Q"] = env.dout("d_Q", [128, 4 * 1024], F32)
    if "ao" in dumps:
        outs["d_ao"] = env.dout("d_ao", [128, NOWN * 2048], F32)
    if "wp" in dumps:
        outs["d_ams"] = env.dout("d_ams", [128, 32], F32)
        outs["d_wpq"] = env.dout("d_wpq", [128, 16 * 2048], F32)
        outs["d_xs"] = env.dout("d_xs", [128, NOWN * 2048], F32)
        outs["d_ow"] = env.dout("d_ow", [128, NOWN * 2048], F32)

    b = Builder(nc)
    with ExitStack() as top:
        env.sb["ident"] = alloc_sb(env, top, "k_ident", [128, 128], BF16)
        env.sb["ones_c"] = alloc_sb(env, top, "k_onec", [128, 1], F32)
        env.sb["ones_r"] = alloc_sb(env, top, "k_oner", [1, 128], F32)
        env.sb["AMS"] = alloc_sb(env, top, "k_AMS", [128, 16], F32)
        env.sb["RAMS"] = alloc_sb(env, top, "k_RAMS", [128, 16], F32)
        env.sb["neg8"] = alloc_sb(env, top, "k_neg8", [128, 1], F32)
        b.make_sems(top)
        b.dma("sync", env.sb["ident"].full, env.dram["ident"].full,
              tag="ld_ident")
        b.memset(env.sb["ones_c"].full, 1.0)
        b.memset(env.sb["ones_r"].full, 1.0)
        b.memset(env.sb["neg8"].full, -8.0)
        attn_scope = ExitStack()
        top.enter_context(attn_scope)
        env.sb["xq_allT"] = alloc_sb(env, attn_scope, "k_xqT",
                                     [128, NT * 2048], BF16)

        with ExitStack() as ph:
            phase_l0_stats(b, env, ph, L0_STATS)
            b.ts(env.sb["AMS"][:, W_QSC:W_QSC + 1],
                 env.sb["AMS"][:, W_WQ:W_WQ + 1], RSQRT_HD, tag="qsc")
            b.barrier()
            b.flush()
        with ExitStack() as ph:
            phase_l0_pre(b, env, ph)
            b.barrier()
            b.flush()
        if "xq" in dumps:
            with ExitStack() as ph:
                buf = alloc_sb(env, ph, "dbuf", [128, 2048], F32)
                for t in range(NT):
                    b.copy(buf.full, env.sb["xq_allT"][:, t * 2048:(t + 1) * 2048])
                    b.dma("gpsimd",
                          raw_apr(outs["d_xq"], outs["d_xq"].h[:, t * 2048:(t + 1) * 2048],
                                  (0, 128), (t * 2048, (t + 1) * 2048)),
                          buf.full, tag=f"dxq{t}")
                bo = alloc_sb(env, ph, "dbuf2", [128, 2048], BF16)
                for t in range(NOWN):
                    b.dma("gpsimd", bo.full,
                          raw_apr(env.dram["xq_ownT"], env.dram["xq_ownT"].h[t],
                                  (0, 128), (0, 2048)), tag=f"dxqo_l{t}")
                    b.copy(buf.full, bo.full)
                    b.dma("gpsimd",
                          raw_apr(outs["d_xqo"], outs["d_xqo"].h[:, t * 2048:(t + 1) * 2048],
                                  (0, 128), (t * 2048, (t + 1) * 2048)),
                          buf.full, tag=f"dxqo{t}")
                b.barrier()
                b.flush()
        if "K" in dumps or "ao" in dumps:
            hgs = range(4) if "ao" in dumps else [0]
            for hg in hgs:
                with ExitStack() as ph:
                    env.sb["_dumpKVQ"] = outs if ("K" in dumps and hg == 0) else None
                    phase_l0_attn_hg(b, env, ph, hg)
                    b.barrier()
                    b.flush()
        if "wp" in dumps:
            b.barrier()
            attn_scope.close()
            env.sb["xstream"] = alloc_sb(env, top, "k_xs", [128, NOWN * 2048], F16)
            with ExitStack() as ph:
                env.sb["_dumpWP"] = outs
                env.sb["_dumpOW"] = outs["d_ow"]
                phase_l0_wp(b, env, ph)
                env.sb["_dumpOW"] = None
                b.barrier()
                b.flush()
            with ExitStack() as ph:
                buf = alloc_sb(env, ph, "dbuf9", [128, 2048], F32)
                for t in range(NOWN):
                    b.copy(buf.full, env.sb["xstream"][:, t * 2048:(t + 1) * 2048])
                    b.dma("gpsimd",
                          raw_apr(outs["d_xs"], outs["d_xs"].h[:, t * 2048:(t + 1) * 2048],
                                  (0, 128), (t * 2048, (t + 1) * 2048)),
                          buf.full, tag=f"dxs{t}")
                b.barrier()
                b.flush()
        if "ao" in dumps:
            with ExitStack() as ph:
                buf = alloc_sb(env, ph, "dbuf3", [128, 2048], F32)
                bo = alloc_sb(env, ph, "dbuf4", [128, 2048], BF16)
                for t in range(NOWN):
                    b.dma("gpsimd", bo.full,
                          raw_apr(env.dram["attn_out_d"], env.dram["attn_out_d"].h[t],
                                  (0, 128), (0, 2048)), tag=f"dao_l{t}")
                    b.copy(buf.full, bo.full)
                    b.dma("gpsimd",
                          raw_apr(outs["d_ao"], outs["d_ao"].h[:, t * 2048:(t + 1) * 2048],
                                  (0, 128), (t * 2048, (t + 1) * 2048)),
                          buf.full, tag=f"dao{t}")
                b.barrier()
                b.flush()
    return nc


def emit_rms_stats(b, stt, x_apr, sq_scratch, eps, n_inv):
    c = lambda i: stt[:, i:i + 1]
    b.act(sq_scratch, x_apr, AF.Square, accum_out=c(SS), tag="rms_sq")
    b.ts(c(T1), c(SS), n_inv, s2=eps, op0=ALU.mult, op1=ALU.add, tag="rms_ms")
    b.act(c(RSTD), c(T1), AF.Ln, tag="rms_ln")
    b.act(c(RSTD), c(RSTD), AF.Exp, scale=-0.5, tag="rms_ex")



def phase_l0_ffn(b, env, stack):
    ident = env.sb["ident"]
    AMS, RAMS = env.sb["AMS"], env.sb["RAMS"]
    xstream = env.sb["xstream"]
    cfc = env.dram["cfcT"]
    cpj = env.dram["cprojT"]
    h_d = env.dram["h_d"]
    cpjq = env.dram["cpjq_d"]

    sq = alloc_sb(env, stack, "f_sq", [128, 2048], F32)
    stt = [alloc_sb(env, stack, f"f_tt{i}", [128, 16], F32) for i in range(2)]
    rmx = alloc_sb(env, stack, "f_rmx", [128, 8], F32)
    esc = alloc_sb(env, stack, "f_esc", [128, 8], F32)
    with ExitStack() as p1:
        xn = alloc_sb(env, p1, "f_xn", [128, D], F32)
        nq = alloc_sb(env, p1, "f_nq", [128, D], F32)
        xq = alloc_sb(env, p1, "f_xq", [128, D], BF16)
        x1qT = alloc_sb(env, p1, "f_x1qT", [128, NOWN * 2048], BF16)
        wsl_f = alloc_sb(env, p1, "f_wf", [128, 2 * 2048], F32)
        wsl_i = alloc_sb(env, p1, "f_wi", [128, 2048], F32)
        wsl_b = [alloc_sb(env, p1, f"f_wb{i}", [128, 16 * 256], BF16)
                 for i in range(3)]
        hst = [alloc_sb(env, p1, f"f_hs{i}", [128, 8 * 256], BF16)
               for i in range(2)]
        pst = alloc_ps(env, p1, "f_pst", [128, 1024], BF16)
        psH = [alloc_ps(env, p1, f"f_psH{i}", [128, 512], F32)
               for i in range(2)]
        for t in range(NOWN):
            r = t % 2
            xs_t = xstream[:, t * 2048:(t + 1) * 2048]
            emit_ln_stats(b, stt[r], xs_t, sq.full, EPS_SUBLN, 1.0 / D)
            c = lambda k: stt[r][:, k:k + 1]
            b.ts(xn.full, xs_t, c(MEAN), c(RSTD), op0=ALU.subtract,
                 op1=ALU.mult, tag=f"ff_xn{t}")
            emit_quant_x(b, stt[r], xn.full, nq.full, xq.full)
            for hf in range(2):
                for cc in range(8):
                    ch = hf * 8 + cc
                    b.transpose(pst[:, cc * 128:(cc + 1) * 128],
                                xq[:, ch * 128:(ch + 1) * 128], ident.full,
                                tag=f"ff_tp{t}_{ch}")
                b.copy(x1qT[:, t * 2048 + hf * 1024:t * 2048 + (hf + 1) * 1024],
                       pst.full, eng="scalar", tag=f"ff_tpe{t}_{hf}")
        b.memset(rmx[:, 0:8], -1e30, tag="ff_rmx")
        for s in range(32):
            r = s % 3
            for hf in range(2):
                v = raw_apr(cfc, cfc.h[s].rearrange("(g p) w -> p g w", p=128)[
                    :, hf * 8:(hf + 1) * 8, :],
                    (0, 128), (s * 4096 + hf * 2048, s * 4096 + (hf + 1) * 2048))
                b.dma("sync", wsl_f[:, hf * 2048:(hf + 1) * 2048], v,
                      tag=f"ff_wld{s}_{hf}")
                b.ts(wsl_i.full, wsl_f[:, hf * 2048:(hf + 1) * 2048],
                     RAMS[:, W_CFC:W_CFC + 1], op0=ALU.mult, tag=f"ff_ws{s}_{hf}")
                b.ts(wsl_i.full, wsl_i.full, MAGIC, MAGIC, op0=ALU.add,
                     op1=ALU.subtract, tag=f"ff_wr{s}_{hf}")
                b.ts(wsl_b[r][:, hf * 2048:(hf + 1) * 2048], wsl_i.full,
                     -1.0, 1.0, op0=ALU.max, op1=ALU.min, tag=f"ff_wc{s}_{hf}")
            for t in range(NOWN):
                g = ("H", s, t)
                for ch in range(16):
                    b.mm(psH[t % 2][:, 0:256],
                         x1qT[:, t * 2048 + ch * 128:t * 2048 + (ch + 1) * 128],
                         wsl_b[r][:, ch * 256:(ch + 1) * 256],
                         start=(ch == 0), stop=(ch == 15), group=g,
                         tag=f"ff_mm{s}_{t}_{ch}")
                b.act(hst[s % 2][:, t * 256:(t + 1) * 256], psH[t % 2][:, 0:256],
                      AF.Copy, scale=AMS[:, W_CFC:W_CFC + 1], tag=f"ff_ev{s}_{t}")
                b.red(stt[t % 2][:, SC3:SC3 + 1], psH[t % 2][:, 0:256],
                      alu=ALU.max, tag=f"ff_rm{s}_{t}")
                b.tt(rmx[:, t:t + 1], rmx[:, t:t + 1],
                     stt[t % 2][:, SC3:SC3 + 1], ALU.max, tag=f"ff_rmu{s}_{t}")
            hv = h_d.h[:, :, s * 256:(s + 1) * 256].rearrange("t p w -> p t w")
            b.dma("gpsimd",
                  raw_apr(h_d, hv, (0, 128), (0, NOWN * DFF)),
                  hst[s % 2].full, tag=f"ff_hsp{s}")
        b.barrier()

    hbuf = alloc_sb(env, stack, "f_h", [128, 2 * 8192], BF16)
    nT = alloc_sb(env, stack, "f_nT", [128, 64 * 256], BF16)
    cpj_b = [alloc_sb(env, stack, f"f_cp{i}", [128, 2048], BF16)
             for i in range(3)]
    wsl_f2 = alloc_sb(env, stack, "f_wf2", [128, 2 * 2048], F32)
    wsl_i2 = alloc_sb(env, stack, "f_wi2", [128, 2048], F32)
    rbuf = alloc_sb(env, stack, "f_rb", [128, 1024], F32)
    tbuf = alloc_sb(env, stack, "f_tb", [128, 1024], F32)
    ibuf = alloc_sb(env, stack, "f_ib", [128, 1024], F32)
    nstg = alloc_sb(env, stack, "f_ns", [128, 1024], BF16)
    ow_ = alloc_sb(env, stack, "f_ow", [128, D], F32)
    ow = [ow_, ow_]

    for quarter in range(4):
        tiles = [quarter * 2, quarter * 2 + 1]
        hs1 = ExitStack()
        pst = alloc_ps(env, hs1, "f_pst2", [128, 1024], BF16)
        for i, t in enumerate(tiles):
            b.dma("sync", hbuf[:, i * 8192:(i + 1) * 8192],
                  raw_apr(h_d, h_d.h[t], (0, 128), (0, NOWN * DFF)),
                  tag=f"ff_hld{quarter}_{i}")
        for i, t in enumerate(tiles):
            r = i % 2
            c = lambda k: stt[r][:, k:k + 1]
            b.ts(c(MX), rmx[:, t:t + 1], AMS[:, W_CFC:W_CFC + 1],
                 op0=ALU.mult, s2=0.0, op1=ALU.max, tag=f"ff_xm{t}")
            b.tt(c(MX), c(MX), c(MX), ALU.mult, tag=f"ff_xm2{t}")
            b.ts(c(MXC), c(MX), 1e-5, op0=ALU.max, tag=f"ff_xmc{t}")
            b.recip(c(RECM), c(MXC), tag=f"ff_rc{t}")
            b.ts(c(SC1), c(RECM), 255.0, tag=f"ff_c2{t}")
            b.act(c(SC2), c(SC1), AF.Ln, tag=f"ff_sql{t}")
            b.act(c(SC2), c(SC2), AF.Exp, scale=0.5, tag=f"ff_sqe{t}")
            b.ts(esc[:, i:i + 1], c(MXC), 1.0 / 255.0, tag=f"ff_is{t}")
            b.tt(esc[:, i:i + 1], esc[:, i:i + 1],
                 AMS[:, W_CPROJ:W_CPROJ + 1], ALU.mult, tag=f"ff_is2{t}")
            for piece in range(8):
                hsl = hbuf[:, i * 8192 + piece * 1024:i * 8192 + (piece + 1) * 1024]
                b.ts(rbuf.full, hsl, 0.0, op0=ALU.max, tag=f"ff_rl{t}_{piece}")
                b.act(tbuf.full, rbuf.full, AF.Square, scale=c(SC2),
                      tag=f"ff_s2{t}_{piece}")
                b.ts(ibuf.full, tbuf.full, MAGIC, MAGIC, op0=ALU.add,
                     op1=ALU.subtract, tag=f"ff_rn{t}_{piece}")
                b.ts(nstg.full, ibuf.full, 1.0, op0=ALU.mult,
                     tag=f"ff_nb{t}_{piece}")
                for cc in range(8):
                    b.transpose(pst[:, cc * 128:(cc + 1) * 128],
                                nstg[:, cc * 128:(cc + 1) * 128], ident.full,
                                tag=f"ff_ntp{t}_{piece}_{cc}")
                cg0 = piece * 8
                dst = nT.h.rearrange("p (cg w) -> p cg w", cg=64)[
                    :, cg0:cg0 + 8, i * 128:(i + 1) * 128]
                b.copy(raw_apr(nT, dst, (0, 128),
                               (cg0 * 256, (cg0 + 8) * 256)),
                       pst.full, eng="scalar", tag=f"ff_ne{t}_{piece}")
        b.barrier()
        hs1.close()
        hs2 = ExitStack()
        psW = [alloc_ps(env, hs2, f"f_psW{i}", [128, 2048], F32)
               for i in range(2)]
        for ch in range(64):
            r = ch % 3
            if quarter == 0:
                v = raw_apr(cpj, cpj.h[ch * 128:(ch + 1) * 128, :],
                            (0, 128), (0, 2048))
                b.dma("sync", wsl_f2[:, (ch % 2) * 2048:(ch % 2 + 1) * 2048], v,
                      tag=f"ff2_wld{quarter}_{ch}")
                b.ts(wsl_i2.full, wsl_f2[:, (ch % 2) * 2048:(ch % 2 + 1) * 2048],
                     RAMS[:, W_CPROJ:W_CPROJ + 1], op0=ALU.mult,
                     tag=f"ff2_ws{quarter}_{ch}")
                b.ts(wsl_i2.full, wsl_i2.full, MAGIC, MAGIC, op0=ALU.add,
                     op1=ALU.subtract, tag=f"ff2_wr{quarter}_{ch}")
                b.ts(cpj_b[r].full, wsl_i2.full, -1.0, 1.0, op0=ALU.max,
                     op1=ALU.min, tag=f"ff2_wc{quarter}_{ch}")
                b.dma("gpsimd",
                      raw_apr(cpjq, cpjq.h[ch], (0, 128),
                              (ch * 2048, (ch + 1) * 2048)),
                      cpj_b[r].full, tag=f"ff2_wst{quarter}_{ch}")
            else:
                b.dma("sync", cpj_b[r].full,
                      raw_apr(cpjq, cpjq.h[ch], (0, 128),
                              (ch * 2048, (ch + 1) * 2048)),
                      tag=f"ff2_wlq{quarter}_{ch}")
            for ti in range(2):
                g = ("W2", quarter, ti)
                for ds in range(4):
                    b.mm(psW[ti][:, ds * 512:(ds + 1) * 512],
                         nT[:, ch * 256 + ti * 128:ch * 256 + (ti + 1) * 128],
                         cpj_b[r][:, ds * 512:(ds + 1) * 512],
                         start=(ch == 0), stop=(ch == 63), group=g,
                         tag=f"ff2_mm{quarter}_{ch}_{ti}_{ds}")
        for ti in range(2):
            t = tiles[ti]
            r = ti % 2
            c = lambda k: stt[r][:, k:k + 1]
            b.act(ow[ti].full, psW[ti].full, AF.Copy,
                  scale=esc[:, ti:ti + 1], tag=f"ff2_ev{quarter}_{ti}")
            b.tt(ow[ti].full, ow[ti].full,
                 xstream[:, t * 2048:(t + 1) * 2048], ALU.add,
                 tag=f"ff2_res{quarter}_{ti}")
            emit_ln_stats(b, stt[r], ow[ti].full, sq.full, EPS_LN, 1.0 / D)
            b.ts(xstream[:, t * 2048:(t + 1) * 2048], ow[ti].full,
                 c(MEAN), c(RSTD), op0=ALU.subtract, op1=ALU.mult,
                 tag=f"ff2_str{quarter}_{ti}")
        b.barrier()
        hs2.close()



def phase_l1_attn(b, env, stack):
    ident = env.sb["ident"]
    xstream = env.sb["xstream"]
    x2nT_d = env.dram["x2nT_d"]
    wqkv = env.dram["wqkvT"]
    CRd, sel0d, sel1d = env.dram["CR"], env.dram["sel0"], env.dram["sel1"]

    sq = alloc_sb(env, stack, "l_sq", [128, 2048], F32)
    stt = [alloc_sb(env, stack, f"l_tt{i}", [128, 16], F32) for i in range(2)]
    rstdk = alloc_sb(env, stack, "l_rk", [128, 8], F32)
    tmp16 = alloc_sb(env, stack, "l_t16", [128, 16], F32)
    send = alloc_sb(env, stack, "l_send", [128, 64], F32)
    addv = alloc_sb(env, stack, "l_addv", [128, 64], F32)
    kvc_d, wc_d = env.dram["kvc_d"], env.dram["wc_d"]
    y_d = env.dram["y_d"]

    qscope = ExitStack()
    stack.enter_context(qscope)
    qkvbufs = {
        "wslb": [alloc_sb(env, qscope, f"l_ws{i}", [128, 16 * 512], BF16)
                 for i in range(2)],
        "xtn": [alloc_sb(env, qscope, f"l_xt{i}", [128, 2048], BF16)
                for i in range(3)],
        "psK": [alloc_ps(env, qscope, f"l_psK{i}", [128, 512], F32)
                for i in range(2)],
    }

    def qkv_proj(slab0, dst_write, tag):
        wslb, xtn, psK = qkvbufs["wslb"], qkvbufs["xtn"], qkvbufs["psK"]
        for si in range(4):
            s = slab0 + si
            r = si % 2
            b.dma("gpsimd", wslb[r].full,
                  raw_apr(wqkv, wqkv.h[s].rearrange("(g p) w -> p g w", p=128),
                          (0, 128), (s * 8192, (s + 1) * 8192)),
                  tag=f"{tag}_w{s}")
            for t in range(NOWN):
                b.dma("sync", xtn[t % 3].full,
                      raw_apr(x2nT_d, x2nT_d.h[t], (0, 128),
                              (t * 2048, (t + 1) * 2048)),
                      tag=f"{tag}_x{s}_{t}")
                g = (tag, s, t)
                for ch in range(16):
                    b.mm(psK[t % 2][:, 0:512],
                         xtn[t % 3][:, ch * 128:(ch + 1) * 128],
                         wslb[r][:, ch * 512:(ch + 1) * 512],
                         start=(ch == 0), stop=(ch == 15), group=g,
                         tag=f"{tag}_mm{s}_{t}_{ch}")
                dst_write(t, si, psK[t % 2], t % 2)

    with ExitStack() as wscope:
        w_tm = alloc_sb(env, wscope, "l_w", [128, NOWN * 2048], BF16)
        wv_tm = alloc_sb(env, wscope, "l_wv", [128, NOWN * 2048], BF16)

        with ExitStack() as ph:
            xqb = alloc_sb(env, ph, "l_xq", [128, 2048], BF16)
            stg = [alloc_sb(env, ph, f"l_st{i}", [128, 2048], BF16)
                   for i in range(2)]
            pst = alloc_ps(env, ph, "l_pst", [128, 1024], BF16)
            for t in range(NOWN):
                r = t % 2
                xs_t = xstream[:, t * 2048:(t + 1) * 2048]
                emit_rms_stats(b, stt[r], xs_t, sq.full, EPS_RMS, 1.0 / D)
                c = lambda k: stt[r][:, k:k + 1]
                b.ts(xqb.full, xs_t, c(RSTD), op0=ALU.mult, tag=f"la_xn{t}")
                for hf in range(2):
                    for cc in range(8):
                        ch = hf * 8 + cc
                        b.transpose(pst[:, cc * 128:(cc + 1) * 128],
                                    xqb[:, ch * 128:(ch + 1) * 128], ident.full,
                                    tag=f"la_tp{t}_{ch}")
                    b.copy(stg[r][:, hf * 1024:(hf + 1) * 1024], pst.full,
                           eng="scalar", tag=f"la_te{t}_{hf}")
                b.dma("gpsimd",
                      raw_apr(x2nT_d, x2nT_d.h[t], (0, 128),
                              (t * 2048, (t + 1) * 2048)),
                      stg[r].full, tag=f"la_sp{t}")
            b.barrier()

        with ExitStack() as kscope:
            k_tm = alloc_sb(env, kscope, "l_k", [128, NOWN * 2048], BF16)

            def k_ev(t, si, ps, r):
                b.copy(k_tm[:, t * 2048 + si * 512:t * 2048 + (si + 1) * 512],
                       ps[:, 0:512], eng="scalar", tag=f"k_ev{t}_{si}")
            qkv_proj(4, k_ev, "lk")
            for t in range(NOWN):
                r = t % 2
                kt = k_tm[:, t * 2048:(t + 1) * 2048]
                emit_rms_stats(b, stt[r], kt, sq.full, EPS_RMS, 1.0 / D)
                b.copy(rstdk[:, t:t + 1], stt[r][:, RSTD:RSTD + 1],
                       tag=f"k_rs{t}")
                b.act(w_tm[:, t * 2048:(t + 1) * 2048], kt, AF.Exp,
                      scale=rstdk[:, t:t + 1], tag=f"k_exp{t}")
            b.barrier()

        def v_ev(t, si, ps, r):
            sl = slice(t * 2048 + si * 512, t * 2048 + (si + 1) * 512)
            b.tt(raw_apr(wv_tm, wv_tm.h[:, sl], (0, 128), (sl.start, sl.stop)),
                 ps[:, 0:512],
                 raw_apr(w_tm, w_tm.h[:, sl], (0, 128), (sl.start, sl.stop)),
                 ALU.mult, tag=f"v_ev{t}_{si}")
        qkv_proj(8, v_ev, "lv")

        with ExitStack() as ph:
            NR = 3
            wvT = [alloc_sb(env, ph, f"l_wvT{i}", [128, 1024], BF16)
                   for i in range(NR)]
            wT = [alloc_sb(env, ph, f"l_wT{i}", [128, 1024], BF16)
                  for i in range(NR)]
            kvcC = [alloc_sb(env, ph, f"l_kvcC{i}", [128, 1024], BF16)
                    for i in range(NR)]
            wcC = [alloc_sb(env, ph, f"l_wcC{i}", [128, 1024], BF16)
                   for i in range(NR)]
            pstA = alloc_ps(env, ph, "l_psA", [128, 1024], BF16)
            pstB = alloc_ps(env, ph, "l_psB", [128, 1024], BF16)
            for c in range(16):
                r = c % NR
                for t in range(NOWN):
                    b.transpose(pstA[:, t * 128:(t + 1) * 128],
                                wv_tm[:, t * 2048 + c * 128:t * 2048 + (c + 1) * 128],
                                ident.full, tag=f"sc_tA{c}_{t}")
                    b.transpose(pstB[:, t * 128:(t + 1) * 128],
                                w_tm[:, t * 2048 + c * 128:t * 2048 + (c + 1) * 128],
                                ident.full, tag=f"sc_tB{c}_{t}")
                b.copy(wvT[r].full, pstA.full, eng="scalar", tag=f"sc_eA{c}")
                b.copy(wT[r].full, pstB.full, eng="scalar", tag=f"sc_eB{c}")
                b.scan(kvcC[r].full, wvT[r].full, wvT[r].full, 0.0,
                       ALU.add, ALU.bypass, tag=f"sc_kv{c}")
                b.scan(wcC[r].full, wT[r].full, wT[r].full, 0.0,
                       ALU.add, ALU.bypass, tag=f"sc_w{c}")
                b.copy(send[:, 0 * 16 + c:0 * 16 + c + 1],
                       kvcC[r][:, 511:512], tag=f"sc_g0{c}")
                b.copy(send[:, 1 * 16 + c:1 * 16 + c + 1],
                       kvcC[r][:, 1023:1024], tag=f"sc_g1{c}")
                b.copy(send[:, 2 * 16 + c:2 * 16 + c + 1],
                       wcC[r][:, 511:512], tag=f"sc_g2{c}")
                b.copy(send[:, 3 * 16 + c:3 * 16 + c + 1],
                       wcC[r][:, 1023:1024], tag=f"sc_g3{c}")
                b.dma("gpsimd",
                      raw_apr(kvc_d, kvc_d.h[c], (0, 128),
                              (c * 1024, (c + 1) * 1024)),
                      kvcC[r].full, tag=f"sc_sp0{c}")
                b.dma("gpsimd",
                      raw_apr(wc_d, wc_d.h[c], (0, 128),
                              (c * 1024, (c + 1) * 1024)),
                      wcC[r].full, tag=f"sc_sp1{c}")
            b.barrier()

    cc_src, cc_dst = env.dram["cc_src"], env.dram["cc_dst"]
    with ExitStack() as ph:
        ag = [alloc_sb(env, ph, f"l_ag{i}", [128, 64], F32) for i in range(2)]
        sib = alloc_sb(env, ph, "l_sib", [128, 64], F32)
        CRs = alloc_sb(env, ph, "l_CR", [128, 4], F32)
        sels = alloc_sb(env, ph, "l_sel", [128, 2], F32)
        b.dma("sync", CRs.full, CRd.full, tag="cc_cr")
        b.dma("sync", sels[:, 0:1], sel0d.full, tag="cc_s0")
        b.dma("sync", sels[:, 1:2], sel1d.full, tag="cc_s1")
        b.dma("gpsimd", cc_src.full, send.full, tag="cc_up")
        b.collective("AllGather", ALU.bypass,
                     [[0, 1], [2, 3], [4, 5], [6, 7]],
                     [cc_src.full], [cc_dst.full], tag="cc_ag")
        b.dma("gpsimd", ag[0].full,
              raw_apr(cc_dst, cc_dst.h[0:128, :], (0, 128), (0, 64)),
              tag="cc_d0")
        b.dma("gpsimd", ag[1].full,
              raw_apr(cc_dst, cc_dst.h[128:256, :], (0, 128), (0, 64)),
              tag="cc_d1")
        b.ts(ag[0].full, ag[0].full, sels[:, 0:1], op0=ALU.mult, tag="cc_m0")
        b.ts(ag[1].full, ag[1].full, sels[:, 1:2], op0=ALU.mult, tag="cc_m1")
        b.tt(sib.full, ag[0].full, ag[1].full, ALU.add, tag="cc_sib")
        for kind in range(2):
            r1c = slice(kind * 32, kind * 32 + 16)
            r2c = slice(kind * 32 + 16, kind * 32 + 32)
            for run in range(2):
                dst = addv[:, kind * 32 + run * 16:kind * 32 + (run + 1) * 16]
                b.ts(dst, sib[:, r1c], CRs[:, run * 2:run * 2 + 1],
                     op0=ALU.mult, tag=f"cc_a{kind}{run}")
                b.ts(tmp16.full, sib[:, r2c], CRs[:, run * 2 + 1:run * 2 + 2],
                     op0=ALU.mult, tag=f"cc_b{kind}{run}")
                b.tt(dst, dst, tmp16.full, ALU.add, tag=f"cc_c{kind}{run}")
        b.barrier()

    with ExitStack() as ph:
        sigq = alloc_sb(env, ph, "l_sigq", [128, NOWN * 2048], BF16)
        with ExitStack() as qs:
            q_tm = alloc_sb(env, qs, "l_q", [128, NOWN * 2048], BF16)

            def q_ev(t, si, ps, r):
                b.copy(q_tm[:, t * 2048 + si * 512:t * 2048 + (si + 1) * 512],
                       ps[:, 0:512], eng="scalar", tag=f"q_ev{t}_{si}")
            qkv_proj(0, q_ev, "lq")
            for t in range(NOWN):
                r = t % 2
                qt = q_tm[:, t * 2048:(t + 1) * 2048]
                emit_rms_stats(b, stt[r], qt, sq.full, EPS_RMS, 1.0 / D)
                b.act(sigq[:, t * 2048:(t + 1) * 2048], qt, AF.Sigmoid,
                      scale=stt[r][:, RSTD:RSTD + 1], tag=f"q_sig{t}")
            b.barrier()
        rc = [alloc_sb(env, ph, f"l_rc{i}", [128, 1024], F32) for i in range(2)]
        kvb = [alloc_sb(env, ph, f"l_kvb{i}", [128, 1024], BF16)
               for i in range(2)]
        wcb = [alloc_sb(env, ph, f"l_wcb{i}", [128, 1024], BF16)
               for i in range(2)]
        sqT = [alloc_sb(env, ph, f"l_sqT{i}", [128, 1024], BF16)
               for i in range(2)]
        pstC = alloc_ps(env, ph, "l_psC", [128, 1024], BF16)
        for c in range(16):
            r = c % 2
            b.dma("gpsimd", kvb[r].full,
                  raw_apr(kvc_d, kvc_d.h[c], (0, 128), (c * 1024, (c + 1) * 1024)),
                  tag=f"y_lk{c}")
            b.dma("gpsimd", wcb[r].full,
                  raw_apr(wc_d, wc_d.h[c], (0, 128), (c * 1024, (c + 1) * 1024)),
                  tag=f"y_lw{c}")
            for run in range(2):
                b.ts(kvb[r][:, run * 512:(run + 1) * 512],
                     kvb[r][:, run * 512:(run + 1) * 512],
                     addv[:, 0 * 32 + run * 16 + c:0 * 32 + run * 16 + c + 1],
                     op0=ALU.add, tag=f"y_ck{c}_{run}")
                b.ts(wcb[r][:, run * 512:(run + 1) * 512],
                     wcb[r][:, run * 512:(run + 1) * 512],
                     addv[:, 1 * 32 + run * 16 + c:1 * 32 + run * 16 + c + 1],
                     op0=ALU.add, tag=f"y_cw{c}_{run}")
            b.ts(rc[r].full, wcb[r].full, 1e-6, op0=ALU.add, tag=f"y_e{c}")
            b.recip(rc[r].full, rc[r].full, tag=f"y_r{c}")
            b.tt(rc[r].full, rc[r].full, kvb[r].full, ALU.mult, tag=f"y_kv{c}")
            for t in range(NOWN):
                b.transpose(pstC[:, t * 128:(t + 1) * 128],
                            sigq[:, t * 2048 + c * 128:t * 2048 + (c + 1) * 128],
                            ident.full, tag=f"y_tp{c}_{t}")
            b.copy(sqT[r].full, pstC.full, eng="scalar", tag=f"y_te{c}")
            b.tt(raw_apr(sqT[r], sqT[r].h[:, :], (0, 128), (0, 1024)),
                 rc[r].full, sqT[r].full, ALU.mult, tag=f"y_y{c}")
            b.dma("gpsimd",
                  raw_apr(y_d, y_d.h[c], (0, 128), (c * 1024, (c + 1) * 1024)),
                  sqT[r].full, tag=f"y_sp{c}")
        b.barrier()

    b.barrier()
    qscope.close()
    wsw = env.dram["wswT"]
    wout = env.dram["woutT"]
    with ExitStack() as ph:
        y = alloc_sb(env, ph, "l_y", [128, 16 * 1024], BF16)
        for c in range(16):
            b.dma("gpsimd", y[:, c * 1024:(c + 1) * 1024],
                  raw_apr(y_d, y_d.h[c], (0, 128), (c * 1024, (c + 1) * 1024)),
                  tag=f"y_ld{c}")
        acc = alloc_sb(env, ph, "l_acc", [128, NOWN * 2048], F16)
        hh = alloc_sb(env, ph, "l_hh", [128, 4 * 1024], BF16)
        woq = alloc_sb(env, ph, "l_woq", [128, 4 * 2048], BF16)
        uslab_ = [alloc_sb(env, ph, f"l_us{i}", [128, 16 * 512], BF16)
                  for i in range(2)]
        gslab_ = [alloc_sb(env, ph, f"l_gs{i}", [128, 16 * 512], BF16)
                  for i in range(2)]
        sgb = alloc_sb(env, ph, "l_sg", [128, 1024], BF16)
        psU = alloc_ps(env, ph, "l_psU", [128, 1024], F32)
        psG = alloc_ps(env, ph, "l_psG", [128, 1024], F32)
        psO = [alloc_ps(env, ph, f"l_psO{i}", [128, 512], F32) for i in range(2)]
        first = [True] * NOWN * 4
        for e8 in range(8):
            for h16 in range(2):
                e16 = e8 * 2 + h16
                uslab = uslab_[e16 % 2]
                gslab = gslab_[e16 % 2]
                b.dma("gpsimd", uslab.full,
                      raw_apr(wsw, wsw.h[e16].rearrange("(g p) w -> p g w", p=128),
                              (0, 128), (e16 * 8192, (e16 + 1) * 8192)),
                      tag=f"sw_u{e16}")
                b.dma("gpsimd", gslab.full,
                      raw_apr(wsw, wsw.h[16 + e16].rearrange("(g p) w -> p g w", p=128),
                              (0, 128), ((16 + e16) * 8192, (17 + e16) * 8192)),
                      tag=f"sw_g{e16}")
                for bi in range(4):
                    du = e16 * 4 + bi
                    gU = ("U", du)
                    gG = ("G", du)
                    for th in range(2):
                        for ch in range(16):
                            lu = uslab.h.rearrange("p (c w) -> p c w", c=16)[
                                :, ch, bi * 128:(bi + 1) * 128]
                            lg = gslab.h.rearrange("p (c w) -> p c w", c=16)[
                                :, ch, bi * 128:(bi + 1) * 128]
                            rhsv = y.h.rearrange("p (c t) -> p c t", c=16)[
                                :, ch, th * 512:(th + 1) * 512]
                            rapr = raw_apr(y, rhsv, (0, 128),
                                           (ch * 1024 + th * 512,
                                            ch * 1024 + (th + 1) * 512))
                            b.mm(psU[:, th * 512:(th + 1) * 512],
                                 raw_apr(uslab, lu, (0, 128),
                                         (ch * 512 + bi * 128,
                                          ch * 512 + (bi + 1) * 128)),
                                 rapr, start=(ch == 0), stop=(ch == 15),
                                 group=gU, tag=f"sw_mu{du}_{th}_{ch}")
                            b.mm(psG[:, th * 512:(th + 1) * 512],
                                 raw_apr(gslab, lg, (0, 128),
                                         (ch * 512 + bi * 128,
                                          ch * 512 + (bi + 1) * 128)),
                                 rapr, start=(ch == 0), stop=(ch == 15),
                                 group=gG, tag=f"sw_mg{du}_{th}_{ch}")
                    b.act(sgb.full, psG.full, AF.Silu, tag=f"sw_si{du}")
                    b.tt(hh[:, (du % 4) * 1024:(du % 4 + 1) * 1024],
                         psU.full, sgb.full, ALU.mult, tag=f"sw_hh{du}")
                    b.dma("gpsimd", woq[:, (du % 4) * 2048:(du % 4 + 1) * 2048],
                          raw_apr(wout, wout.h[du * 128:(du + 1) * 128, :],
                                  (0, 128), (0, 2048)), tag=f"sw_wo{du}")
                for t in range(NOWN):
                    for ds in range(4):
                        rot = (t * 4 + ds) % 2
                        g = ("O", e16, t, ds)
                        for dloc in range(4):
                            b.mm(psO[rot][:, 0:512],
                                 hh[:, dloc * 1024 + t * 128:dloc * 1024 + (t + 1) * 128],
                                 woq[:, dloc * 2048 + ds * 512:dloc * 2048 + (ds + 1) * 512],
                                 start=(dloc == 0), stop=(dloc == 3), group=g,
                                 tag=f"sw_o{e16}_{t}_{ds}_{dloc}")
                        dst = acc[:, t * 2048 + ds * 512:t * 2048 + (ds + 1) * 512]
                        if e16 == 0:
                            b.copy(dst, psO[rot][:, 0:512], eng="scalar",
                                   tag=f"sw_oc{e16}_{t}_{ds}")
                        else:
                            b.tt(dst, psO[rot][:, 0:512], dst, ALU.add,
                                 tag=f"sw_oa{e16}_{t}_{ds}")
        ow = alloc_sb(env, ph, "l_ow", [128, 2048], F32)
        for t in range(NOWN):
            r = t % 2
            b.stt(ow.full, xstream[:, t * 2048:(t + 1) * 2048], 2.0,
                  acc[:, t * 2048:(t + 1) * 2048], ALU.mult, ALU.add,
                  tag=f"l1_res{t}")
            emit_ln_stats(b, stt[r], ow.full, sq.full, EPS_LN, 1.0 / D)
            c = lambda k: stt[r][:, k:k + 1]
            b.ts(xstream[:, t * 2048:(t + 1) * 2048], ow.full, c(MEAN),
                 c(RSTD), op0=ALU.subtract, op1=ALU.mult, tag=f"l1_str{t}")
        b.barrier()



def phase_l1_ffn(b, env, stack):
    ident = env.sb["ident"]
    xstream = env.sb["xstream"]
    h1_d = env.dram["h1_d"]
    W1d, W3d = env.dram["W1T"], env.dram["W3T"]
    W2d = env.dram["W2relay"]
    yout = env.dram["y_own"]

    sq = alloc_sb(env, stack, "n_sq", [128, 2048], F32)
    stt = [alloc_sb(env, stack, f"n_tt{i}", [128, 16], F32) for i in range(2)]
    H128s = alloc_sb(env, stack, "n_H", [128, 128], BF16)
    G64s = alloc_sb(env, stack, "n_G", [128, 128], BF16)
    b.dma("sync", H128s.full, env.dram["H128"].full, tag="n_ldH")
    b.dma("sync", G64s.full, env.dram["G64"].full, tag="n_ldG")

    with ExitStack() as ph:
        x3T = alloc_sb(env, ph, "n_x3T", [128, NOWN * 2048], BF16)
        xb = alloc_sb(env, ph, "n_xb", [128, 2048], BF16)
        pst = alloc_ps(env, ph, "n_pst", [128, 1024], BF16)
        for t in range(NOWN):
            b.copy(xb.full, xstream[:, t * 2048:(t + 1) * 2048], tag=f"n_xc{t}")
            for hf in range(2):
                for cc in range(8):
                    ch = hf * 8 + cc
                    b.transpose(pst[:, cc * 128:(cc + 1) * 128],
                                xb[:, ch * 128:(ch + 1) * 128], ident.full,
                                tag=f"n_tp{t}_{ch}")
                b.copy(x3T[:, t * 2048 + hf * 1024:t * 2048 + (hf + 1) * 1024],
                       pst.full, eng="scalar", tag=f"n_te{t}_{hf}")
        w1s = [alloc_sb(env, ph, f"n_w1{i}", [128, 16 * 512], BF16)
               for i in range(2)]
        w3s = [alloc_sb(env, ph, f"n_w3{i}", [128, 16 * 512], BF16)
               for i in range(2)]
        sgb = alloc_sb(env, ph, "n_sg", [128, 1024], BF16)
        h0 = alloc_sb(env, ph, "n_h0", [128, 1024], BF16)
        h1b = [alloc_sb(env, ph, f"n_h1{i}", [128, 1024], BF16)
               for i in range(2)]
        psU = alloc_ps(env, ph, "n_psU", [128, 1024], F32)
        psL = alloc_ps(env, ph, "n_psL", [128, 1024], F32)
        psF = alloc_ps(env, ph, "n_psF", [128, 1024], F32)
        for hb in range(64):
            sl, off = hb // 4, hb % 4
            r = sl % 2
            if off == 0:
                b.dma("gpsimd", w1s[r].full,
                      raw_apr(W1d, W1d.h[sl].rearrange("(g p) w -> p g w", p=128),
                              (0, 128), (sl * 8192, (sl + 1) * 8192)),
                      tag=f"n_lw1{sl}")
                b.dma("gpsimd", w3s[r].full,
                      raw_apr(W3d, W3d.h[sl].rearrange("(g p) w -> p g w", p=128),
                              (0, 128), (sl * 8192, (sl + 1) * 8192)),
                      tag=f"n_lw3{sl}")
            gU, gL = ("nU", hb), ("nL", hb)
            for th in range(2):
                for ch in range(16):
                    lu = w1s[r].h.rearrange("p (c w) -> p c w", c=16)[
                        :, ch, off * 128:(off + 1) * 128]
                    ll = w3s[r].h.rearrange("p (c w) -> p c w", c=16)[
                        :, ch, off * 128:(off + 1) * 128]
                    rhs = _xqT_rhs(x3T, th, ch, ntile=8)
                    b.mm(psU[:, th * 512:(th + 1) * 512],
                         raw_apr(w1s[r], lu, (0, 128),
                                 (ch * 512 + off * 128, ch * 512 + (off + 1) * 128)),
                         rhs, start=(ch == 0), stop=(ch == 15), group=gU,
                         tag=f"n_mu{hb}_{th}_{ch}")
                    b.mm(psL[:, th * 512:(th + 1) * 512],
                         raw_apr(w3s[r], ll, (0, 128),
                                 (ch * 512 + off * 128, ch * 512 + (off + 1) * 128)),
                         rhs, start=(ch == 0), stop=(ch == 15), group=gL,
                         tag=f"n_ml{hb}_{th}_{ch}")
            b.act(sgb.full, psU.full, AF.Silu, tag=f"n_si{hb}")
            b.tt(h0.full, psL.full, sgb.full, ALU.mult, tag=f"n_h0{hb}")
            gF = ("nF", hb)
            b.mm(psF[:, 0:512], H128s.full, h0[:, 0:512], start=True,
                 stop=True, group=gF, tag=f"n_f1a{hb}")
            b.mm(psF[:, 512:1024], H128s.full, h0[:, 512:1024], start=True,
                 stop=True, group=gF, tag=f"n_f1b{hb}")
            b.act(h1b[hb % 2].full, psF.full, AF.Copy,
                  scale=1.0 / float(np.sqrt(128.0)), tag=f"n_f1e{hb}")
            b.dma("gpsimd",
                  raw_apr(h1_d, h1_d.h[hb], (0, 128), (hb * 1024, (hb + 1) * 1024)),
                  h1b[hb % 2].full, tag=f"n_h1s{hb}")
        b.barrier()

    with ExitStack() as ph:
        h3 = alloc_sb(env, ph, "n_h3", [128, 64 * 512], BF16)
        h2b = [alloc_sb(env, ph, f"n_h2{i}", [128, 4 * 512], BF16)
               for i in range(2)]
        w2q = [alloc_sb(env, ph, f"n_w2{i}", [128, 8 * 512], BF16)
               for i in range(2)]
        obuf = [alloc_sb(env, ph, f"n_ob{i}", [128, 2048], F32)
                for i in range(4)]
        ybuf = [alloc_sb(env, ph, f"n_yb{i}", [128, 2048], F32)
                for i in range(2)]
        psF2 = alloc_ps(env, ph, "n_psF2", [128, 2048], F32)
        psW = [alloc_ps(env, ph, f"n_psW{i}", [128, 512], F32)
               for i in range(4)]
        for th in range(2):
            for cq in range(16):
                r = cq % 2
                for p0 in range(2):
                    ov = h2b[r].h[p0 * 64:(p0 + 1) * 64, :].rearrange(
                        "p (cc t) -> p cc t", cc=4)
                    iv = h1_d.h[:, 8 * cq + p0:8 * cq + 8:2,
                                th * 512:(th + 1) * 512]
                    b.dma("sync",
                          raw_apr(h2b[r], ov, (p0 * 64, (p0 + 1) * 64),
                                  (0, 2048)),
                          raw_apr(h1_d, iv, (0, 128),
                                  (8 * cq * 1024, (8 * cq + 8) * 1024)),
                          tag=f"n_h2l{th}_{cq}_{p0}")
                for cc in range(4):
                    b.mm(psF2[:, cc * 512:(cc + 1) * 512], G64s.full,
                         h2b[r][:, cc * 512:(cc + 1) * 512], start=True,
                         stop=True, group=("F2", th, cq),
                         tag=f"n_f2{th}_{cq}_{cc}")
                b.act(h3[:, cq * 2048:(cq + 1) * 2048], psF2.full, AF.Copy,
                      scale=1.0 / 8.0, tag=f"n_f2e{th}_{cq}")
            for ds in range(4):
                for cb in range(8):
                    r = cb % 2
                    iv = W2d.h[ds, cb * 8:(cb + 1) * 8].rearrange(
                        "c p w -> p c w")
                    b.dma("gpsimd", w2q[r].full,
                          raw_apr(W2d, iv, (0, 128),
                                  (ds * 100000 + cb * 8, ds * 100000 + cb * 8 + 8)),
                          tag=f"n_w2l{th}_{ds}_{cb}")
                    for cc in range(8):
                        cg = cb * 8 + cc
                        for t in range(4):
                            b.mm(psW[t][:, 0:512],
                                 h3[:, cg * 512 + t * 128:cg * 512 + (t + 1) * 128],
                                 w2q[r][:, cc * 512:(cc + 1) * 512],
                                 start=(cg == 0), stop=(cg == 63),
                                 group=("W2f", th, ds, t),
                                 tag=f"n_w2m{th}_{ds}_{cg}_{t}")
                for t in range(4):
                    b.act(obuf[t][:, ds * 512:(ds + 1) * 512],
                          psW[t][:, 0:512], AF.Copy, tag=f"n_w2e{th}_{ds}_{t}")
            for t in range(4):
                tg = th * 4 + t
                r = t % 2
                b.tt(ybuf[r].full, obuf[t].full,
                     xstream[:, tg * 2048:(tg + 1) * 2048], ALU.add,
                     tag=f"n_res{tg}")
                emit_ln_stats(b, stt[r], ybuf[r].full, sq.full, EPS_LN, 1.0 / D)
                c_ = lambda k: stt[r][:, k:k + 1]
                b.ts(ybuf[r].full, ybuf[r].full, c_(MEAN), c_(RSTD),
                     op0=ALU.subtract, op1=ALU.mult, tag=f"n_ln{tg}")
                b.dma("gpsimd",
                      raw_apr(yout, yout.h[tg * 128:(tg + 1) * 128, :],
                              (0, 128), (0, D)),
                      ybuf[r].full, tag=f"n_out{tg}")
        b.barrier()



from concourse.bass_utils import run_bass_kernel_spmd


def kernel(**inputs):
    nc = build_program(upto="l1ffn", stream_out=False)
    in_maps = stage_inputs(inputs)
    res = run_bass_kernel_spmd(nc, in_maps, core_ids=list(range(8)))
    return unstage_output(res.results)
